# version 12
# speedup vs baseline: 2.0451x; 1.0223x over previous
import os
import numpy as np
from contextlib import ExitStack

import concourse.bass as bass
import concourse.mybir as mybir
from concourse.bass_utils import run_bass_kernel_spmd

F32 = mybir.dt.float32
BF16 = mybir.dt.bfloat16
I8 = mybir.dt.int8
AF = mybir.ActivationFunctionType
ALU = mybir.AluOpType
AX = mybir.AxisListType.X

NT = 1024
TT = 8
D = 2048
KC = 16
NH = 16
HD = 8192
HC = 64
EPS_LN = 1e-5
EPS_RMS = 1e-8
EPS_AT = 1e-8
ATANH_HALF = 0.5493061443340548


class Sem:
    def __init__(self, handle):
        self.h = handle
        self.n = 0


class Eng:
    def __init__(self, section):
        self.section = section

    def do(self, fn, inc=None, k=1):
        if inc is None:
            self.section(fn)
            return None
        self.section(lambda h, fn=fn, s=inc.h, k=k: fn(h).then_inc(s, k))
        inc.n += k
        return inc.n

    def wait(self, sem, val=None):
        v = sem.n if val is None else val
        if v > 0:
            self.section(lambda h, s=sem.h, v=v: h.wait_ge(s, v))


def build_program():
    PHASE = int(os.environ.get("KPHASE", "99"))
    nc = bass.Bass("TRN2", target_bir_lowering=False,
                   detect_race_conditions=False)

    x_d = nc.dram_tensor("x", [NT, D], F32, kind="ExternalInput").ap()
    w_attc = {m: nc.dram_tensor(m + "c", [D, 256], F32, kind="ExternalInput").ap()
              for m in ("wq", "wk", "wv", "wo")}
    bqr_d = nc.dram_tensor("bq_r", [16, 128], F32, kind="ExternalInput").ap()
    bor_d = nc.dram_tensor("bo_r", [16, 128], F32, kind="ExternalInput").ap()
    bo_row = nc.dram_tensor("bo_row", [1, D], F32, kind="ExternalInput").ap()
    bk_d = nc.dram_tensor("bk", [1, D], F32, kind="ExternalInput").ap()
    bv_d = nc.dram_tensor("bv", [1, D], F32, kind="ExternalInput").ap()
    g1_d = nc.dram_tensor("g1", [1, D], F32, kind="ExternalInput").ap()
    g2_d = nc.dram_tensor("g2", [1, D], F32, kind="ExternalInput").ap()
    g3_d = nc.dram_tensor("g3", [1, HD], F32, kind="ExternalInput").ap()
    w1c_d = nc.dram_tensor("w1c", [HD, 256], F32, kind="ExternalInput").ap()
    w2c_d = nc.dram_tensor("w2c", [HD, 256], F32, kind="ExternalInput").ap()
    w3c_d = nc.dram_tensor("w3c", [D, 1024], F32, kind="ExternalInput").ap()
    w1s_d = nc.dram_tensor("w1s", [HD // 8, D], F32, kind="ExternalInput").ap()
    w2s_d = nc.dram_tensor("w2s", [HD // 8, D], F32, kind="ExternalInput").ap()
    w3s_d = nc.dram_tensor("w3s", [D // 8, HD], F32, kind="ExternalInput").ap()
    out_d = nc.dram_tensor("out", [NT, D], F32, kind="ExternalOutput").ap()

    cc_in = nc.dram_tensor("cc_in", [128, 2068], F32)
    cc_out = nc.dram_tensor("cc_out", [128, 2068], F32)
    xp_sp = nc.dram_tensor("xp_sp", [NT, D], F32).ap()
    y_sp = nc.dram_tensor("y_sp", [NT, D], F32).ap()
    h_sp = nc.dram_tensor("h_sp", [NT, HD], F32).ap()
    agA_in = nc.dram_tensor("agA_in", [256, 8192], BF16)
    agA_out = nc.dram_tensor("agA_out", [D, 8192], BF16)
    agB_in = nc.dram_tensor("agB_in", [256, 8192], BF16)
    agB_out = nc.dram_tensor("agB_out", [D, 8192], BF16)
    ag1_in = nc.dram_tensor("ag1_in", [D // 8, HD], I8)
    ag2_in = nc.dram_tensor("ag2_in", [D // 8, HD], I8)
    ag3_in = nc.dram_tensor("ag3_in", [HD // 8, D], I8)
    w1t_d = nc.dram_tensor("w1t", [D, HD], I8).ap()
    w2t_d = nc.dram_tensor("w2t", [D, HD], I8).ap()
    w3t_d = nc.dram_tensor("w3t", [HD, D], I8).ap()

    with ExitStack() as outer:
        eo = outer.enter_context
        sSY = Sem(eo(nc.semaphore("sSY")))
        sL2 = Sem(eo(nc.semaphore("sL2")))
        sGD = Sem(eo(nc.semaphore("sGD")))
        sTE = Sem(eo(nc.semaphore("sTE")))
        sVE = Sem(eo(nc.semaphore("sVE")))
        sAC = Sem(eo(nc.semaphore("sAC")))
        sGC = Sem(eo(nc.semaphore("sGC")))
        sCC = Sem(eo(nc.semaphore("sCC")))
        sST = Sem(eo(nc.semaphore("sST")))

        idf = eo(nc.sbuf_tensor("idf", [128, 128], F32))
        idb = eo(nc.sbuf_tensor("idb", [128, 128], BF16))
        onesf = eo(nc.sbuf_tensor("onesf", [128, 128], F32))
        onec = eo(nc.sbuf_tensor("onec", [128, 1], F32))
        colix = eo(nc.sbuf_tensor("colix", [128, 128], F32))
        rowix = eo(nc.sbuf_tensor("rowix", [128, 1], F32))
        bqc = eo(nc.sbuf_tensor("bqc", [128, 16], F32))
        boc = eo(nc.sbuf_tensor("boc", [128, 16], F32))
        wsumc = eo(nc.sbuf_tensor("wsumc", [128, 4], F32))
        tern = eo(nc.sbuf_tensor("tern", [128, 12], F32))
        statA = eo(nc.sbuf_tensor("statA", [128, 32], F32))
        statF = eo(nc.sbuf_tensor("statF", [128, TT * 8], F32))
        hstat = eo(nc.sbuf_tensor("hstat", [128, TT * 2], F32))
        lnsc = eo(nc.sbuf_tensor("lnsc", [128, D], F32))

        blk = eo(nc.Block())
        SY = Eng(blk.sync)
        SC = Eng(blk.scalar)
        VE = Eng(blk.vector)
        TE = Eng(blk.tensor)
        GP = Eng(blk.gpsimd)

        GP.do(lambda g: g.iota(colix[:], pattern=[[1, 128]], base=0,
                               channel_multiplier=0,
                               allow_small_or_imprecise_dtypes=True))
        GP.do(lambda g: g.iota(rowix[:], pattern=[[0, 1]], base=0,
                               channel_multiplier=1,
                               allow_small_or_imprecise_dtypes=True), inc=sGC)
        VE.wait(sGC)
        VE.do(lambda v: v.tensor_scalar(out=idf[:], in0=colix[:],
                                        scalar1=rowix[:, :], scalar2=None,
                                        op0=ALU.is_equal))
        VE.do(lambda v: v.tensor_copy(out=idb[:], in_=idf[:]))
        VE.do(lambda v: v.memset(onesf[:], 1.0))
        VE.do(lambda v: v.memset(onec[:], 1.0))
        VE.do(lambda v: v.memset(wsumc[:], 0.0), inc=sVE)
        consts_ready = sVE.n

        def emit_ln(ytile, xout):
            VE.wait(sAC)
            VE.do(lambda v: v.tensor_reduce(out=statA[:, 24:25], in_=ytile,
                                            axis=AX, op=ALU.add), inc=sVE)
            VE.wait(sVE)
            VE.do(lambda v: v.tensor_tensor(out=lnsc[:, 0:D], in0=ytile,
                                            in1=ytile, op=ALU.mult))
            VE.do(lambda v: v.tensor_reduce(out=statA[:, 25:26],
                                            in_=lnsc[:, 0:D], axis=AX,
                                            op=ALU.add), inc=sVE)
            VE.wait(sVE)
            VE.do(lambda v: v.tensor_scalar(out=statA[:, 26:27],
                                            in0=statA[:, 24:25],
                                            scalar1=1.0 / D, scalar2=None,
                                            op0=ALU.mult))
            VE.do(lambda v: v.tensor_tensor(out=statA[:, 27:28],
                                            in0=statA[:, 26:27],
                                            in1=statA[:, 26:27], op=ALU.mult))
            VE.do(lambda v: v.tensor_scalar(out=statA[:, 28:29],
                                            in0=statA[:, 25:26],
                                            scalar1=1.0 / D, scalar2=None,
                                            op0=ALU.mult))
            VE.do(lambda v: v.tensor_tensor(out=statA[:, 29:30],
                                            in0=statA[:, 28:29],
                                            in1=statA[:, 27:28],
                                            op=ALU.subtract))
            VE.do(lambda v: v.tensor_scalar(out=statA[:, 29:30],
                                            in0=statA[:, 29:30],
                                            scalar1=EPS_LN, scalar2=None,
                                            op0=ALU.add), inc=sVE)
            SC.wait(sVE)
            SC.do(lambda s: s.activation(out=statA[:, 30:31],
                                         in_=statA[:, 29:30], func=AF.Sqrt),
                  inc=sAC)
            VE.wait(sAC)
            VE.do(lambda v: v.reciprocal(out=statA[:, 31:32],
                                         in_=statA[:, 30:31]), inc=sVE)
            VE.wait(sVE)
            VE.do(lambda v: v.tensor_scalar(out=xout, in0=ytile,
                                            scalar1=statA[:, 26:27],
                                            scalar2=statA[:, 31:32],
                                            op0=ALU.subtract, op1=ALU.mult),
                  inc=sVE)

        def full_barrier():
            sems = (sSY, sL2, sGD, sTE, sVE, sAC, sGC, sST)
            for eng in (SY, SC, VE, TE, GP):
                for sm in sems:
                    eng.wait(sm)

        with ExitStack() as SA:
            ea = SA.enter_context
            xThi = ea(nc.sbuf_tensor("xThi", [128, KC * NT], BF16))
            xTlo = ea(nc.sbuf_tensor("xTlo", [128, KC * NT], BF16))
            kvacc = ea(nc.sbuf_tensor("kvacc", [128, 2068], F32))
            XSld = ExitStack()
            xld = XSld.enter_context(nc.sbuf_tensor("xld", [128, D], F32))
            VE.do(lambda v: v.memset(kvacc[:], 0.0), inc=sVE)
            pQ = ea(nc.psum_tensor("pQ", [128, 1024], F32))
            pK = ea(nc.psum_tensor("pK", [128, 512], F32))
            pV = ea(nc.psum_tensor("pV", [128, 512], F32))
            pKV = ea(nc.psum_tensor("pKV", [128, 512], F32))
            pT = ea(nc.psum_tensor("pT", [128, 512], F32))
            pBt = ea(nc.psum_tensor("pBt", [128, 1024], BF16))
            pT2 = ea(nc.psum_tensor("pT2", [128, 512], F32))

            xThi3 = xThi.ap().rearrange("p (c t) -> p c t", c=KC)
            xTlo3 = xTlo.ap().rearrange("p (c t) -> p c t", c=KC)
            pT4 = pT.ap().rearrange("p (j x) -> p j x", j=4)
            pBt8 = pBt.ap().rearrange("p (j x) -> p j x", j=8)

            def transpose16_f32(src_tile, dst3, dst_sel):
                for b in range(4):
                    for j in range(4):
                        c = b * 4 + j
                        TE.do(lambda h, c=c, j=j, s=src_tile: h.transpose(
                            pT[:, j * 128:(j + 1) * 128],
                            s[:, c * 128:(c + 1) * 128], idf[:]),
                            inc=sTE if j == 3 else None)
                    VE.wait(sTE)
                    d1, d2 = dst_sel(b)
                    VE.do(lambda v, d1=d1, d2=d2, dst3=dst3: v.tensor_copy(
                        out=dst3[:, d1, d2], in_=pT4[:, :, :]), inc=sVE)
                    TE.wait(sVE)

            with ExitStack() as PW:
                ew = PW.enter_context
                wcl = ew(nc.sbuf_tensor("wcl", [128, 4096], F32))
                whi = ew(nc.sbuf_tensor("whi", [128, 4096], BF16))
                wlo = ew(nc.sbuf_tensor("wlo", [128, 4096], BF16))
                wstg = ew(nc.sbuf_tensor("wstg", [128, 2048], BF16))
                wcl3 = wcl.ap().rearrange("p (cc j) -> p cc j", cc=16)
                pBt8w = pBt.ap().rearrange("p (j x) -> p j x", j=8)
                prep_cfg = [("wk", agA_in, 0), ("wv", agA_in, 4096),
                            ("wq", agB_in, 0), ("wo", agB_in, 4096)]
                st_mark = {"agA": 0, "agB": 0}
                for mi, (mn, agd, colb) in enumerate(prep_cfg):
                    src3 = w_attc[mn].rearrange("(cc p) j -> p cc j", p=128)
                    SY.wait(sTE)
                    SY.wait(sVE)
                    SY.do(lambda s, src3=src3: s.dma_start(
                        out=wcl3[:, :, :], in_=src3[:, :, :]), inc=sSY, k=16)
                    VE.wait(sSY)
                    VE.do(lambda v: v.tensor_copy(out=whi[:], in_=wcl[:]))
                    VE.wait(sVE)
                    VE.do(lambda v: v.tensor_tensor(
                        out=wlo[:], in0=wcl[:], in1=whi[:],
                        op=ALU.subtract), inc=sVE)
                    for half, hbuf in ((0, whi), (1, wlo)):
                        for kb in range(2):
                            TE.wait(sVE)
                            for bb in range(2):
                                for j in range(8):
                                    cc = bb * 8 + j
                                    TE.do(lambda h, cc=cc, j=j, kb=kb,
                                          hbuf=hbuf: h.transpose(
                                              pBt[:, j * 128:(j + 1) * 128],
                                              hbuf[:, cc * 256 + kb * 128:
                                                   cc * 256 + kb * 128 + 128],
                                              idb[:]),
                                          inc=sTE if j == 7 else None)
                                VE.wait(sTE)
                                VE.do(lambda v, bb=bb: v.tensor_copy(
                                    out=wstg.ap().rearrange(
                                        "p (cc j) -> p cc j", cc=16)
                                    [:, bb * 8:(bb + 1) * 8, :],
                                    in_=pBt8w[:, :, :]), inc=sVE)
                                TE.wait(sVE)
                            SY.wait(sVE)
                            SY.do(lambda s, agd=agd, colb=colb, half=half,
                                  kb=kb: s.dma_start(
                                      out=agd.ap()[kb * 128:(kb + 1) * 128,
                                                   colb + half * 2048:
                                                   colb + half * 2048 + 2048],
                                      in_=wstg[:]), inc=sSY, k=16)
                            VE.wait(sSY)
                agw_stores = sSY.n
                GP.wait(sSY, agw_stores)
                GP.do(lambda g: g.collective_compute(
                    "AllGather", ALU.bypass, replica_groups=[list(range(8))],
                    ins=[agA_in.ap().opt()], outs=[agA_out.ap().opt()]),
                    inc=sCC)
                agA_done = 1
                agB_done = 2

            for t in range(TT):
                SY.wait(sTE)
                SY.do(lambda s, t=t: s.dma_start(
                    out=xld[:], in_=x_d[t * 128:(t + 1) * 128, :]),
                    inc=sSY, k=16)
                TE.wait(sSY)
                if t == 0:
                    TE.wait(sVE, consts_ready)
                for b in range(4):
                    for j in range(4):
                        c = b * 4 + j
                        TE.do(lambda h, c=c, j=j: h.transpose(
                            pT[:, j * 128:(j + 1) * 128],
                            xld[:, c * 128:(c + 1) * 128], idf[:]),
                            inc=sTE if j == 3 else None)
                    VE.wait(sTE)
                    VE.do(lambda v, b=b, t=t: v.tensor_copy(
                        out=xThi3[:, b * 4:(b + 1) * 4,
                                  t * 128:(t + 1) * 128],
                        in_=pT4[:, :, :]), inc=sVE)
                    VE.wait(sVE)
                    VE.do(lambda v, b=b, t=t: v.tensor_tensor(
                        out=xTlo3[:, b * 4:(b + 1) * 4,
                                  t * 128:(t + 1) * 128],
                        in0=pT4[:, :, :],
                        in1=xThi3[:, b * 4:(b + 1) * 4,
                                  t * 128:(t + 1) * 128],
                        op=ALU.subtract), inc=sVE)
                    TE.wait(sVE)
            xT_ready = sVE.n

            SY.wait(sTE)
            SY.do(lambda s: s.dma_start(out=xld[0:16, 0:128], in_=bqr_d[:, :]),
                  inc=sSY, k=16)
            TE.wait(sSY)
            TE.do(lambda h: h.transpose(pT[:, 0:16], xld[0:16, 0:128],
                                        idf[0:16, 0:16]), inc=sTE)
            VE.wait(sTE)
            VE.do(lambda v: v.tensor_copy(out=bqc[:], in_=pT[:, 0:16]),
                  inc=sVE)
            SY.wait(sTE)
            SY.do(lambda s: s.dma_start(out=xld[0:16, 0:128], in_=bor_d[:, :]),
                  inc=sSY, k=16)
            TE.wait(sSY)
            TE.wait(sVE)
            TE.do(lambda h: h.transpose(pT[:, 128:144], xld[0:16, 0:128],
                                        idf[0:16, 0:16]), inc=sTE)
            VE.wait(sTE)
            VE.do(lambda v: v.tensor_copy(out=boc[:], in_=pT[:, 128:144]),
                  inc=sVE)
            SY.wait(sVE)

            if PHASE <= 0:
                return nc
            XSld.close()
            FW = 512
            with ExitStack() as S1:
                e1 = S1.enter_context
                wkhi = e1(nc.sbuf_tensor("wkhi", [128, KC * FW], BF16))
                wklo = e1(nc.sbuf_tensor("wklo", [128, KC * FW], BF16))
                wvhi = e1(nc.sbuf_tensor("wvhi", [128, KC * FW], BF16))
                wvlo = e1(nc.sbuf_tensor("wvlo", [128, KC * FW], BF16))
                bkrep = e1(nc.sbuf_tensor("bkrep", [128, D], F32))
                bvrep = e1(nc.sbuf_tensor("bvrep", [128, D], F32))
                ktile = e1(nc.sbuf_tensor("ktile", [128, 2 * FW], F32))
                vtile = e1(nc.sbuf_tensor("vtile", [128, 2 * FW], F32))
                etile = e1(nc.sbuf_tensor("etile", [128, 2 * FW], F32))
                etile2 = e1(nc.sbuf_tensor("etile2", [128, 2 * FW], F32))
                wsld = e1(nc.sbuf_tensor("wsld", [128, 2048], F32))

                SY.do(lambda s: s.dma_start(
                    out=bkrep[:], in_=bk_d[0:1, :].broadcast_to([128, D])),
                    inc=sSY, k=16)
                SY.do(lambda s: s.dma_start(
                    out=bvrep[:], in_=bv_d[0:1, :].broadcast_to([128, D])),
                    inc=sSY, k=16)
                bias_ready = sSY.n

                wsum_jobs = []
                for i in range(8):
                    wsum_jobs.append((0, w1s_d, i, 0))
                for i in range(8):
                    wsum_jobs.append((1, w2s_d, i, 0))
                for i in range(2):
                    for hf in range(4):
                        wsum_jobs.append((2, w3s_d, i, hf))

                def emit_wsum_job(job):
                    m, dram, i, hf = job
                    SC.wait(sVE)
                    SC.do(lambda s, dram=dram, i=i, hf=hf: s.dma_start(
                        out=wsld[:],
                        in_=dram[i * 128:(i + 1) * 128,
                                 hf * 2048:(hf + 1) * 2048]), inc=sL2, k=16)
                    VE.wait(sL2)
                    VE.do(lambda v: v.tensor_reduce(
                        out=statA[:, 0:1], in_=wsld[:], axis=AX,
                        op=ALU.add, apply_absolute_value=True), inc=sVE)
                    VE.wait(sVE)
                    VE.do(lambda v, m=m: v.tensor_tensor(
                        out=wsumc[:, m:m + 1], in0=wsumc[:, m:m + 1],
                        in1=statA[:, 0:1], op=ALU.add), inc=sVE)

                evacK = {0: None, 1: None}
                evacV = {0: None, 1: None}
                kvmm_half = {0: None, 1: None}
                headprev = {}
                panel_mark = {"khi": None, "klo": None, "vhi": None,
                              "vlo": None}
                ld_mark = {}
                agA3 = agA_out.ap().rearrange("(c p) f -> p c f", p=128)
                for q in range(4):
                    if q == 0:
                        SY.wait(sCC, agA_done)
                    for nm, buf, colb in (("khi", wkhi, 0), ("vhi", wvhi, 4096),
                                          ("klo", wklo, 2048),
                                          ("vlo", wvlo, 6144)):
                        if panel_mark[nm] is not None:
                            SY.wait(sTE, panel_mark[nm])
                        SY.do(lambda s, buf=buf, colb=colb, q=q: s.dma_start(
                            out=buf.ap().rearrange("p (c f) -> p c f", c=KC)
                            [:, :, :],
                            in_=agA3[:, :, colb + q * FW:
                                     colb + (q + 1) * FW]), inc=sSY, k=16)
                        ld_mark[nm] = sSY.n
                    if q == 0:
                        GP.wait(sSY)
                        GP.do(lambda g: g.collective_compute(
                            "AllGather", ALU.bypass,
                            replica_groups=[list(range(8))],
                            ins=[agB_in.ap().opt()],
                            outs=[agB_out.ap().opt()]), inc=sCC)
                        TE.wait(sSY, bias_ready)
                        TE.wait(sVE, xT_ready)

                    def emit_proj(t):
                        par = t % 2
                        bK = pK[:] if par == 0 else pQ[:, 0:FW]
                        bV = pV[:] if par == 0 else pQ[:, 512:512 + FW]
                        ksl = slice(par * FW, (par + 1) * FW)
                        if evacK[par] is not None:
                            TE.wait(sVE, evacK[par])
                        passes = [(xThi, wkhi, "khi"), (xTlo, wkhi, "khi"),
                                  (xThi, wklo, "klo")]
                        for pi, (xa, wb, nm) in enumerate(passes):
                            TE.wait(sSY, ld_mark["vlo"])
                            for k in range(KC):
                                TE.do(lambda h, k=k, t=t, bK=bK, xa=xa, wb=wb:
                                      h.matmul(
                                          bK,
                                          xa[:, k * NT + t * 128:
                                             k * NT + (t + 1) * 128],
                                          wb[:, k * FW:(k + 1) * FW],
                                          start=(pi == 0 and k == 0),
                                          stop=(pi == 2 and k == KC - 1)),
                                      inc=sTE if (pi == 2 and k == KC - 1)
                                      else None)
                        kmm = sTE.n
                        if evacV[par] is not None:
                            TE.wait(sVE, evacV[par])
                        vpasses = [(xThi, wvhi, "vhi"), (xTlo, wvhi, "vhi"),
                                   (xThi, wvlo, "vlo")]
                        for pi, (xa, wb, nm) in enumerate(vpasses):
                            TE.wait(sSY, ld_mark["vlo"])
                            for k in range(KC):
                                TE.do(lambda h, k=k, t=t, bV=bV, xa=xa, wb=wb:
                                      h.matmul(
                                          bV,
                                          xa[:, k * NT + t * 128:
                                             k * NT + (t + 1) * 128],
                                          wb[:, k * FW:(k + 1) * FW],
                                          start=(pi == 0 and k == 0),
                                          stop=(pi == 2 and k == KC - 1)),
                                      inc=sTE if (pi == 2 and k == KC - 1)
                                      else None)
                        vmm = sTE.n
                        VE.wait(sTE, kmm)
                        if kvmm_half[par] is not None:
                            VE.wait(sTE, kvmm_half[par])
                        VE.do(lambda v, q=q, bK=bK, ksl=ksl: v.tensor_tensor(
                            out=ktile[:, ksl], in0=bK,
                            in1=bkrep[:, q * FW:(q + 1) * FW], op=ALU.add))
                        VE.do(lambda v, ksl=ksl: v.tensor_scalar(
                            out=etile[:, ksl], in0=ktile[:, ksl], scalar1=0.0,
                            scalar2=None, op0=ALU.min), inc=sVE)
                        evacK[par] = sVE.n
                        SC.wait(sVE, evacK[par])
                        SC.do(lambda s, ksl=ksl: s.activation(
                            out=etile2[:, ksl], in_=etile[:, ksl],
                            func=AF.Exp), inc=sAC)
                        VE.wait(sAC)
                        VE.do(lambda v, ksl=ksl: v.scalar_tensor_tensor(
                            out=ktile[:, ksl], in0=ktile[:, ksl], scalar=0.0,
                            in1=etile2[:, ksl], op0=ALU.max, op1=ALU.add))
                        VE.wait(sTE, vmm)
                        VE.do(lambda v, q=q, bV=bV, ksl=ksl: v.tensor_tensor(
                            out=vtile[:, ksl], in0=bV,
                            in1=bvrep[:, q * FW:(q + 1) * FW], op=ALU.add),
                            inc=sVE)
                        evacV[par] = sVE.n

                    def emit_kv(t):
                        par = t % 2
                        for h2 in range(4):
                            gh = q * 4 + h2
                            bank = pKV if h2 % 2 == 0 else pT
                            off = par * FW + h2 * 128
                            if (h2 % 2) in headprev:
                                TE.wait(sVE, headprev[h2 % 2])
                            TE.wait(sVE, evacV[par])
                            TE.do(lambda h, off=off, bank=bank: h.matmul(
                                bank[:, 0:128],
                                ktile[:, off:off + 128],
                                vtile[:, off:off + 128],
                                start=True, stop=False))
                            TE.do(lambda h, off=off, bank=bank: h.matmul(
                                bank[:, 128:129],
                                ktile[:, off:off + 128],
                                onec[:], start=False, stop=True), inc=sTE)
                            VE.wait(sTE)
                            VE.do(lambda v, gh=gh, bank=bank: v.tensor_tensor(
                                out=kvacc[:, gh * 129:(gh + 1) * 129],
                                in0=kvacc[:, gh * 129:(gh + 1) * 129],
                                in1=bank[:, 0:129], op=ALU.add), inc=sVE)
                            headprev[h2 % 2] = sVE.n
                        kvmm_half[par] = sTE.n

                    for t in range(TT + 1):
                        if t < TT:
                            emit_proj(t)
                        if t >= 1:
                            emit_kv(t - 1)
                        if wsum_jobs:
                            emit_wsum_job(wsum_jobs.pop(0))
                    for nm in panel_mark:
                        panel_mark[nm] = sTE.n
                while wsum_jobs:
                    emit_wsum_job(wsum_jobs.pop(0))

                TE.wait(sVE)
                for m in range(3):
                    TE.do(lambda h, m=m: h.matmul(
                        pT[:, 256 + m:257 + m], onesf[:], wsumc[:, m:m + 1],
                        start=True, stop=True), inc=sTE if m == 2 else None)
                VE.wait(sTE)
                VE.do(lambda v: v.tensor_copy(out=kvacc[:, 2064:2067],
                                              in_=pT[:, 256:259]), inc=sVE)

                SY.wait(sVE)
                SY.do(lambda s: s.dma_start(out=cc_in.ap()[:, :],
                                            in_=kvacc[:]), inc=sSY, k=16)
                GP.wait(sSY)
                GP.do(lambda g: g.collective_compute(
                    "AllReduce", ALU.add, replica_groups=[list(range(8))],
                    ins=[cc_in.ap().opt()], outs=[cc_out.ap().opt()]),
                    inc=sCC)
                if PHASE <= 1:
                    return nc

            tern_jobs = []
            for m, (src, dst) in enumerate(((w1c_d, ag1_in.ap()),
                                            (w2c_d, ag2_in.ap()))):
                for kc in range(2):
                    for qq in range(4):
                        for hf in range(2):
                            tern_jobs.append((m, src, dst, kc, qq * 2 + hf))
            for hc in range(8):
                for hf in range(2):
                    tern_jobs.append((2, w3c_d, ag3_in.ap(), hc, hf))
            tstate = {"ld": None, "st": None, "first": True,
                      "stores": {0: 0, 1: 0, 2: 0}}
            ag_emitted = [False]

            def emit_tern_job(job):
                m, src, dst, a_, qh = job
                src3 = src.rearrange("(cc p) k -> p cc k", p=128)
                if tstate["ld"] is not None:
                    SC.wait(sTE, tstate["ld"])
                if m < 2:
                    kc = a_
                    SC.do(lambda s, src3=src3, kc=kc, qh=qh: s.dma_start(
                        out=tld3[:, :, :],
                        in_=src3[:, qh * 8:(qh + 1) * 8,
                                 kc * 128:(kc + 1) * 128]), inc=sL2, k=16)
                else:
                    hc = a_
                    SC.do(lambda s, src3=src3, hc=hc, qh=qh: s.dma_start(
                        out=tld3[:, :, :],
                        in_=src3[:, qh * 8:(qh + 1) * 8,
                                 hc * 128:(hc + 1) * 128]), inc=sL2, k=16)
                VE.wait(sL2)
                if tstate["st"] is not None:
                    VE.wait(sGD, tstate["st"])
                if tstate["first"]:
                    VE.wait(sVE, tern_ready)
                    tstate["first"] = False
                VE.do(lambda v, m=m: v.tensor_scalar(
                    out=tpos[:], in0=tld[:], scalar1=tern[:, 3 + m:4 + m],
                    scalar2=None, op0=ALU.is_lt))
                VE.do(lambda v, m=m: v.scalar_tensor_tensor(
                    out=tbf[:], in0=tld[:], scalar=tern[:, m:m + 1],
                    in1=tpos[:], op0=ALU.is_gt, op1=ALU.subtract), inc=sVE)
                TE.wait(sVE)
                for j in range(8):
                    TE.do(lambda h, j=j: h.transpose(
                        pBt[:, j * 128:(j + 1) * 128],
                        tbf[:, j * 128:(j + 1) * 128], idb[:]),
                        inc=sTE if j == 7 else None)
                VE.wait(sTE)
                VE.do(lambda v: v.tensor_copy(
                    out=tstg3[:, :, :], in_=pBt8[:, :, :]), inc=sVE)
                TE.wait(sVE)
                tstate["ld"] = sTE.n
                GP.wait(sVE)
                dst3 = dst.rearrange("(c p) o -> p c o", p=128)
                if m < 2:
                    kc = a_
                    GP.do(lambda g, dst3=dst3, kc=kc, qh=qh: g.dma_start(
                        out=dst3[:, kc:kc + 1, qh * 1024:(qh + 1) * 1024],
                        in_=tstg1[:, :, :]), inc=sGD, k=16)
                else:
                    hc = a_
                    GP.do(lambda g, dst3=dst3, hc=hc, qh=qh: g.dma_start(
                        out=dst3[:, hc:hc + 1, qh * 1024:(qh + 1) * 1024],
                        in_=tstg1[:, :, :]), inc=sGD, k=16)
                tstate["st"] = sGD.n

            full_barrier()
            SY.wait(sCC)
            SY.do(lambda s: s.dma_start(out=kvacc[:],
                                        in_=cc_out.ap()[:, :]),
                  inc=sSY, k=16)
            kv_reduced = sSY.n

            VE.wait(sSY, kv_reduced)
            NEL = float(HD * D)
            for m in range(3):
                c0 = 12 + m * 4
                VE.do(lambda v, m=m, c0=c0: v.tensor_scalar(
                    out=statA[:, c0:c0 + 1],
                    in0=kvacc[:, 2064 + m:2065 + m],
                    scalar1=1.0 / NEL, scalar2=None, op0=ALU.mult))
                VE.do(lambda v, m=m, c0=c0: v.tensor_scalar(
                    out=tern[:, m:m + 1], in0=statA[:, c0:c0 + 1],
                    scalar1=EPS_RMS, scalar2=ATANH_HALF,
                    op0=ALU.add, op1=ALU.mult))
                VE.do(lambda v, m=m: v.tensor_scalar(
                    out=tern[:, 3 + m:4 + m], in0=tern[:, m:m + 1],
                    scalar1=-1.0, scalar2=None, op0=ALU.mult))
                VE.do(lambda v, c0=c0: v.tensor_scalar(
                    out=statA[:, c0 + 1:c0 + 2], in0=statA[:, c0:c0 + 1],
                    scalar1=1.0, scalar2=None, op0=ALU.add))
                VE.do(lambda v, c0=c0: v.tensor_scalar(
                    out=statA[:, c0 + 2:c0 + 3], in0=statA[:, c0:c0 + 1],
                    scalar1=-1.0, scalar2=1.0, op0=ALU.mult, op1=ALU.add))
                VE.do(lambda v, c0=c0: v.reciprocal(
                    out=statA[:, c0 + 3:c0 + 4],
                    in_=statA[:, c0 + 2:c0 + 3]), inc=sVE)
                VE.wait(sVE)
                VE.do(lambda v, c0=c0: v.tensor_tensor(
                    out=statA[:, c0 + 1:c0 + 2],
                    in0=statA[:, c0 + 1:c0 + 2],
                    in1=statA[:, c0 + 3:c0 + 4], op=ALU.mult), inc=sVE)
                SC.wait(sVE)
                SC.do(lambda s, c0=c0: s.activation(
                    out=statA[:, c0 + 2:c0 + 3],
                    in_=statA[:, c0 + 1:c0 + 2], func=AF.Ln), inc=sAC)
                VE.wait(sAC)
                VE.do(lambda v, m=m, c0=c0: v.tensor_scalar(
                    out=tern[:, 6 + m:7 + m], in0=statA[:, c0 + 2:c0 + 3],
                    scalar1=0.5, scalar2=None, op0=ALU.mult), inc=sVE)
            tern_ready = sVE.n
            with ExitStack() as S2:
                e2 = S2.enter_context
                attnThi = e2(nc.sbuf_tensor("attnThi", [128, NH * NT], BF16))
                attnTlo = e2(nc.sbuf_tensor("attnTlo", [128, NH * NT], BF16))
                TERN = ExitStack()
                et = TERN.enter_context
                tld = et(nc.sbuf_tensor("tld", [128, 1024], F32))
                tpos = et(nc.sbuf_tensor("tpos", [128, 1024], F32))
                tbf = et(nc.sbuf_tensor("tbf", [128, 1024], BF16))
                tstg = et(nc.sbuf_tensor("tstg", [128, 1024], BF16))
                tld3 = tld.ap().rearrange("p (cc j) -> p cc j", cc=8)
                tstg3 = tstg.ap().rearrange("p (cc j) -> p cc j", cc=8)
                tstg1 = tstg.ap().rearrange("p (c o) -> p c o", c=1)
                with ExitStack() as S2a:
                    e2a = S2a.enter_context
                    qwh = e2a(nc.sbuf_tensor("qwh", [128, KC * 512], BF16))
                    qwl = e2a(nc.sbuf_tensor("qwl", [128, KC * 512], BF16))
                    qthbuf = [e2a(nc.sbuf_tensor(f"qth{i}", [128, NT], F32))
                              for i in range(2)]
                    qm = lnsc[:, 0:NT]
                    qe = lnsc[:, NT:2 * NT]
                    at8 = e2a(nc.sbuf_tensor("at8", [128, 1024], F32))
                    recc = e2a(nc.sbuf_tensor("recc", [128, 16], F32))

                    prevQ = {"v": None}
                    agB3 = agB_out.ap().rearrange("(c p) f -> p c f", p=128)
                    qw_ld = None
                    qw_mm = None
                    qb_evac = {"v": None}

                    def emit_qmm(h, qth):
                        nonlocal qw_ld, qw_mm
                        if h % 4 == 0:
                            if h == 0:
                                SY.wait(sCC, agB_done)
                            if qw_mm is not None:
                                SY.wait(sTE, qw_mm)
                            hg = h // 4
                            for buf, colb in ((qwh, 0), (qwl, 2048)):
                                SY.do(lambda s, buf=buf, colb=colb, hg=hg:
                                      s.dma_start(
                                          out=buf.ap().rearrange(
                                              "p (c j) -> p c j", c=KC)
                                          [:, :, :],
                                          in_=agB3[:, :,
                                                   colb + hg * 512:
                                                   colb + (hg + 1) * 512]),
                                      inc=sSY, k=16)
                            qw_ld = sSY.n
                        TE.wait(sSY, qw_ld)
                        if qb_evac["v"] is not None:
                            TE.wait(sVE, qb_evac["v"])
                        hof = (h % 4) * 128
                        passes = [(xThi, qwh), (xThi, qwl), (xTlo, qwh)]
                        for half in range(2):
                            for pi, (xa, wb) in enumerate(passes):
                                for k in range(KC):
                                    TE.do(lambda h_, k=k, half=half, xa=xa,
                                          wb=wb, pi=pi, hof=hof: h_.matmul(
                                              pQ[:, half * 512:
                                                 (half + 1) * 512],
                                              wb[:, k * 512 + hof:
                                                 k * 512 + hof + 128],
                                              xa[:, k * NT + half * 512:
                                                 k * NT + (half + 1) * 512],
                                              start=(pi == 0 and k == 0),
                                              stop=(pi == 2 and k == KC - 1)),
                                          inc=sTE if (pi == 2 and
                                                      k == KC - 1) else None)
                        qw_mm = sTE.n
                        VE.wait(sTE, qw_mm)
                        VE.do(lambda v, h=h: v.tensor_scalar(
                            out=qth[:], in0=pQ[:], scalar1=bqc[:, h:h + 1],
                            scalar2=None, op0=ALU.add), inc=sVE)
                        qb_evac["v"] = sVE.n
                        VE.do(lambda v: v.tensor_scalar(
                            out=qm, in0=qth[:], scalar1=0.0, scalar2=None,
                            op0=ALU.min), inc=sVE)
                        mv = sVE.n
                        SC.wait(sVE, mv)
                        SC.do(lambda s: s.activation(out=qe, in_=qm,
                                                     func=AF.Exp), inc=sAC)
                        VE.wait(sAC)
                        VE.do(lambda v: v.scalar_tensor_tensor(
                            out=qth[:], in0=qth[:], scalar=0.0, in1=qe,
                            op0=ALU.max, op1=ALU.add), inc=sVE)
                        return sVE.n

                    def emit_qcons(h, q_ready, qth2):
                        TE.wait(sVE, q_ready)
                        TE.wait(sVE)
                        if h == 0:
                            TE.wait(sSY, kv_reduced)
                        for t in range(TT):
                            TE.do(lambda h_, h=h, t=t: h_.matmul(
                                pKV[:, t:t + 1],
                                qth2[:, t * 128:(t + 1) * 128],
                                kvacc[:, h * 129 + 128:h * 129 + 129],
                                start=True, stop=True),
                                inc=sTE if t == TT - 1 else None)
                        VE.wait(sTE)
                        VE.do(lambda v: v.tensor_scalar(
                            out=recc[:, 0:8], in0=pKV[:, 0:8],
                            scalar1=EPS_AT, scalar2=None, op0=ALU.add),
                            inc=sVE)
                        VE.wait(sVE)
                        VE.do(lambda v: v.reciprocal(out=recc[:, 8:16],
                                                     in_=recc[:, 0:8]),
                              inc=sVE)
                        rec_ready = sVE.n
                        for t in range(TT):
                            bank = pK if t < 4 else pV
                            tof = (t % 4) * 128
                            TE.do(lambda h_, h=h, t=t, bank=bank, tof=tof:
                                  h_.matmul(
                                      bank[:, tof:tof + 128],
                                      qth2[:, t * 128:(t + 1) * 128],
                                      kvacc[:, h * 129:h * 129 + 128],
                                      start=True, stop=True),
                                  inc=sTE if t == TT - 1 else None)
                        VE.wait(sTE)
                        VE.wait(sVE, rec_ready)
                        for t in range(TT):
                            bank = pK if t < 4 else pV
                            tof = (t % 4) * 128
                            VE.do(lambda v, t=t, bank=bank, tof=tof:
                                  v.tensor_scalar(
                                      out=at8[:, t * 128:(t + 1) * 128],
                                      in0=bank[:, tof:tof + 128],
                                      scalar1=recc[:, 8 + t:9 + t],
                                      scalar2=None, op0=ALU.mult),
                                  inc=sVE if t == TT - 1 else None)
                        TE.wait(sVE)
                        for b in range(2):
                            for j in range(4):
                                t = b * 4 + j
                                TE.do(lambda h_, t=t, j=j: h_.transpose(
                                    pT2[:, j * 128:(j + 1) * 128],
                                    at8[:, t * 128:(t + 1) * 128], idf[:]),
                                    inc=sTE if j == 3 else None)
                            VE.wait(sTE)
                            VE.do(lambda v, h=h, b=b: v.tensor_copy(
                                out=attnThi[:, h * NT + b * 512:
                                            h * NT + (b + 1) * 512],
                                in_=pT2[:, 0:512]), inc=sVE)
                            VE.wait(sVE)
                            VE.do(lambda v, h=h, b=b: v.tensor_tensor(
                                out=attnTlo[:, h * NT + b * 512:
                                            h * NT + (b + 1) * 512],
                                in0=pT2[:, 0:512],
                                in1=attnThi[:, h * NT + b * 512:
                                            h * NT + (b + 1) * 512],
                                op=ALU.subtract), inc=sVE)
                            TE.wait(sVE)

                    qr = {}
                    for h in range(NH + 1):
                        if h < NH:
                            qr[h] = emit_qmm(h, qthbuf[h % 2])
                        if h >= 1:
                            emit_qcons(h - 1, qr[h - 1], qthbuf[(h - 1) % 2])
                        for _ in range(4):
                            if tern_jobs:
                                emit_tern_job(tern_jobs.pop(0))
                        if not tern_jobs and not ag_emitted[0]:
                            ag_emitted[0] = True
                            GP.wait(sGD)
                            for agi, ago, nm in ((ag1_in, w1t_d, "g1"),
                                                 (ag2_in, w2t_d, "g2"),
                                                 (ag3_in, w3t_d, "g3")):
                                GP.do(lambda g, agi=agi, ago=ago:
                                      g.collective_compute(
                                          "AllGather", ALU.bypass,
                                          replica_groups=[list(range(8))],
                                          ins=[agi.ap().opt()],
                                          outs=[ago.opt()]), inc=sCC)
                    attnT_ready = sVE.n
                assert not tern_jobs, "tern jobs must drain inside P3a"
                TERN.close()

                full_barrier()
                with ExitStack() as S3:
                    e3 = S3.enter_context
                    scols = e3(nc.sbuf_tensor("scols", [128, 64], F32))
                    borep = e3(nc.sbuf_tensor("borep", [128, D], F32))
                    SY.do(lambda s: s.dma_start(
                        out=borep[:],
                        in_=bo_row[0:1, :].broadcast_to([128, D])),
                        inc=sSY, k=16)
                    bo_ready = sSY.n
                    agB3b = agB_out.ap().rearrange("(c p) f -> p c f",
                                                   p=128)
                    x3r = x_d.rearrange("(tt p) d -> p tt d", p=128)
                    with ExitStack() as N1:
                        en = N1.enter_context
                        woh = en(nc.sbuf_tensor("woh", [128, KC * 512],
                                                BF16))
                        wol = en(nc.sbuf_tensor("wol", [128, KC * 512],
                                                BF16))
                        ybuf = en(nc.sbuf_tensor("ybuf", [128, 1024], F32))
                        ysq = en(nc.sbuf_tensor("ysq", [128, 512], F32))
                        xrel = en(nc.sbuf_tensor("xrel", [128, 1024], F32))
                        xrel3 = xrel.ap().rearrange("p (o j) -> p o j", o=2)
                        owg = {"mm": None, "ld": None}
                        bank_mark = {0: None, 1: None}
                        ybuf_st = {0: None, 1: None}
                        xrel_free = {0: None, 1: None}
                        xrel_ld = {}
                        ylast = {}

                        def emit_xrel(fc, t):
                            par = t % 2
                            if xrel_free[par] is not None:
                                SY.wait(sVE, xrel_free[par])
                            SY.do(lambda s, fc=fc, t=t, par=par:
                                  s.dma_start(
                                      out=xrel3[:, par:par + 1, :],
                                      in_=x3r[:, t:t + 1,
                                              fc * 512:(fc + 1) * 512]),
                                  inc=sSY, k=16)
                            xrel_ld[(fc, t)] = sSY.n

                        def emit_oload(fc):
                            if owg["mm"] is not None:
                                SY.wait(sTE, owg["mm"])
                            for buf, colb in ((woh, 4096), (wol, 6144)):
                                SY.do(lambda s, buf=buf, colb=colb, fc=fc:
                                      s.dma_start(
                                          out=buf.ap().rearrange(
                                              "p (c j) -> p c j", c=KC)
                                          [:, :, :],
                                          in_=agB3b[:, :,
                                                    colb + fc * 512:
                                                    colb + (fc + 1) * 512]),
                                      inc=sSY, k=16)
                            emit_xrel(fc, 0)
                            owg["ld"] = sSY.n

                        def emit_omm(fc, t):
                            par = t % 2
                            pb = pQ[:, par * 512:(par + 1) * 512]
                            TE.wait(sSY, owg["ld"])
                            if fc == 0 and t == 0:
                                TE.wait(sVE, attnT_ready)
                            if bank_mark[par] is not None:
                                TE.wait(sVE, bank_mark[par])
                            opasses = [(attnThi, woh), (attnThi, wol),
                                       (attnTlo, woh)]
                            for pi, (ab, wb) in enumerate(opasses):
                                for ec in range(NH):
                                    TE.do(lambda h_, ec=ec, t=t, ab=ab,
                                          wb=wb, pb=pb, pi=pi: h_.matmul(
                                              pb,
                                              ab[:, ec * NT + t * 128:
                                                 ec * NT + (t + 1) * 128],
                                              wb[:, ec * 512:
                                                 (ec + 1) * 512],
                                              start=(pi == 0 and ec == 0),
                                              stop=(pi == 2 and
                                                    ec == NH - 1)),
                                          inc=sTE if (pi == 2 and
                                                      ec == NH - 1)
                                          else None)
                            owg["mm"] = sTE.n
                            return sTE.n

                        def emit_oevac(fc, t, omm):
                            par = t % 2
                            pb = pQ[:, par * 512:(par + 1) * 512]
                            yb = ybuf[:, par * 512:(par + 1) * 512]
                            VE.wait(sTE, omm)
                            if fc == 0 and t == 0:
                                VE.wait(sSY, bo_ready)
                            VE.wait(sSY)
                            VE.do(lambda v, pb=pb, yb=yb, fc=fc:
                                  v.tensor_tensor(
                                      out=yb, in0=pb,
                                      in1=borep[:, fc * 512:
                                                (fc + 1) * 512],
                                      op=ALU.add))
                            VE.do(lambda v, yb=yb, par=par:
                                  v.tensor_tensor(
                                      out=yb, in0=yb,
                                      in1=xrel[:, par * 512:
                                               (par + 1) * 512],
                                      op=ALU.add), inc=sVE)
                            bank_mark[par] = sVE.n
                            xrel_free[par] = sVE.n
                            y_ready = sVE.n
                            VE.do(lambda v, yb=yb, fc=fc, t=t:
                                  v.tensor_reduce(
                                      out=scols[:, t * 4 + fc:
                                                t * 4 + fc + 1],
                                      in_=yb, axis=AX, op=ALU.add))
                            VE.do(lambda v, yb=yb: v.tensor_tensor(
                                out=ysq[:], in0=yb, in1=yb, op=ALU.mult))
                            VE.do(lambda v, fc=fc, t=t: v.tensor_reduce(
                                out=scols[:, 32 + t * 4 + fc:
                                          33 + t * 4 + fc],
                                in_=ysq[:], axis=AX, op=ALU.add),
                                inc=sVE)
                            SY.wait(sVE, y_ready)
                            SY.do(lambda s, fc=fc, t=t, yb=yb:
                                  s.dma_start(
                                      out=y_sp[t * 128:(t + 1) * 128,
                                               fc * 512:(fc + 1) * 512],
                                      in_=yb), inc=sSY, k=16)
                            ybuf_st[par] = sSY.n
                            if fc == 3:
                                ylast[t] = sSY.n

                        for fc in range(4):
                            emit_oload(fc)
                            for t in range(TT):
                                om = emit_omm(fc, t)
                                emit_oevac(fc, t, om)
                                if t + 1 < TT:
                                    emit_xrel(fc, t + 1)
                        stats_done = sVE.n
                    with ExitStack() as N2:
                        en2 = N2.enter_context
                        yld = en2(nc.sbuf_tensor("yld", [128, 2 * D], F32))
                        xpt2 = en2(nc.sbuf_tensor("xpt2", [128, 2 * D],
                                                  F32))
                        xpt_st = {0: None, 1: None}
                        yld_free = {0: None, 1: None}
                        SY.wait(sVE, stats_done)
                        for t in range(TT):
                            par = t % 2
                            SY.wait(sSY)
                            if yld_free[par] is not None:
                                SY.wait(sVE, yld_free[par])
                            SY.do(lambda s, t=t, par=par: s.dma_start(
                                out=yld[:, par * D:(par + 1) * D],
                                in_=y_sp[t * 128:(t + 1) * 128, :]),
                                inc=sSY, k=16)
                            yld_ld = sSY.n
                            VE.wait(sSY, yld_ld)
                            if t == 0:
                                VE.wait(sVE, stats_done)
                            VE.wait(sSY)
                            VE.wait(sAC)
                            VE.do(lambda v, t=t: v.tensor_reduce(
                                out=statA[:, 24:25],
                                in_=scols[:, t * 4:(t + 1) * 4], axis=AX,
                                op=ALU.add))
                            VE.do(lambda v, t=t: v.tensor_reduce(
                                out=statA[:, 25:26],
                                in_=scols[:, 32 + t * 4:
                                           32 + (t + 1) * 4],
                                axis=AX, op=ALU.add), inc=sVE)
                            VE.wait(sVE)
                            VE.do(lambda v: v.tensor_scalar(
                                out=statA[:, 26:27], in0=statA[:, 24:25],
                                scalar1=1.0 / D, scalar2=None,
                                op0=ALU.mult))
                            VE.do(lambda v: v.tensor_tensor(
                                out=statA[:, 27:28], in0=statA[:, 26:27],
                                in1=statA[:, 26:27], op=ALU.mult))
                            VE.do(lambda v: v.tensor_scalar(
                                out=statA[:, 28:29], in0=statA[:, 25:26],
                                scalar1=1.0 / D, scalar2=None,
                                op0=ALU.mult))
                            VE.do(lambda v: v.tensor_tensor(
                                out=statA[:, 29:30], in0=statA[:, 28:29],
                                in1=statA[:, 27:28], op=ALU.subtract))
                            VE.do(lambda v: v.tensor_scalar(
                                out=statA[:, 29:30], in0=statA[:, 29:30],
                                scalar1=EPS_LN, scalar2=None,
                                op0=ALU.add), inc=sVE)
                            SC.wait(sVE)
                            SC.do(lambda s: s.activation(
                                out=statA[:, 30:31], in_=statA[:, 29:30],
                                func=AF.Sqrt), inc=sAC)
                            VE.wait(sAC)
                            VE.do(lambda v: v.reciprocal(
                                out=statA[:, 31:32], in_=statA[:, 30:31]),
                                inc=sVE)
                            VE.wait(sVE)
                            VE.do(lambda v, par=par: v.tensor_scalar(
                                out=xpt2[:, par * D:(par + 1) * D],
                                in0=yld[:, par * D:(par + 1) * D],
                                scalar1=statA[:, 26:27],
                                scalar2=statA[:, 31:32],
                                op0=ALU.subtract, op1=ALU.mult), inc=sVE)
                            xv = sVE.n
                            yld_free[par] = xv
                            SY.wait(sVE, xv)
                            SY.do(lambda s, t=t, par=par: s.dma_start(
                                out=xp_sp[t * 128:(t + 1) * 128, :],
                                in_=xpt2[:, par * D:(par + 1) * D]),
                                inc=sSY, k=16)
                            xpt_st[par] = sSY.n
                        xp_spilled = sSY.n
            while tern_jobs:
                emit_tern_job(tern_jobs.pop(0))
            if PHASE <= 2:
                return nc
            w12_stores = max(tstate["stores"][0], tstate["stores"][1])
            w3_stores = tstate["stores"][2]

        full_barrier()
        with ExitStack() as SF:
            ef = SF.enter_context
            pU = [ef(nc.psum_tensor(f"pU{i}", [128, 512], F32))
                  for i in range(4)]
            pBq = ef(nc.psum_tensor("pBq", [128, 1024], BF16))
            pBq8 = pBq.ap().rearrange("p (j x) -> p j x", j=8)

            g1rep = ef(nc.sbuf_tensor("g1rep", [128, D], F32))
            g2rep = ef(nc.sbuf_tensor("g2rep", [128, D], F32))
            g3sl = ef(nc.sbuf_tensor("g3sl", [128, 1024], F32))
            xq1T = ef(nc.sbuf_tensor("xq1T", [128, KC * NT], BF16))
            xq2T = ef(nc.sbuf_tensor("xq2T", [128, KC * NT], BF16))
            xptF = ef(nc.sbuf_tensor("xptF", [128, D], F32))
            tq = ef(nc.sbuf_tensor("tq", [128, D], F32))
            pqb = ef(nc.sbuf_tensor("pqb", [128, D], BF16))
            nqb = ef(nc.sbuf_tensor("nqb", [128, D], BF16))
            xqc = ef(nc.sbuf_tensor("xqc", [128, D], BF16))
            w1pA = ef(nc.sbuf_tensor("w1pA", [128, KC * 512], BF16))
            w1pB = ef(nc.sbuf_tensor("w1pB", [128, KC * 512], BF16))
            w2pA = ef(nc.sbuf_tensor("w2pA", [128, KC * 512], BF16))
            w2pB = ef(nc.sbuf_tensor("w2pB", [128, KC * 512], BF16))
            silu_sb = ef(nc.sbuf_tensor("silu_sb", [128, 1024], F32))
            h_sb = ef(nc.sbuf_tensor("h_sb", [128, 2 * 512], F32))
            hh_scr = ef(nc.sbuf_tensor("hh_scr", [128, 512], F32))

            SY.do(lambda s: s.dma_start(
                out=g1rep[:], in_=g1_d[0:1, :].broadcast_to([128, D])),
                inc=sSY, k=16)
            SY.do(lambda s: s.dma_start(
                out=g2rep[:], in_=g2_d[0:1, :].broadcast_to([128, D])),
                inc=sSY, k=16)
            VE.do(lambda v: v.memset(hstat[:], 0.0), inc=sVE)
            for t in range(TT):
                SY.wait(sVE)
                SY.wait(sSY, xp_spilled)
                SY.do(lambda s, t=t: s.dma_start(
                    out=xptF[:], in_=xp_sp[t * 128:(t + 1) * 128, :]),
                    inc=sSY, k=16)
                VE.wait(sSY)
                VE.wait(sAC)
                c = t * 8
                VE.do(lambda v: v.tensor_tensor(out=lnsc[:], in0=xptF[:],
                                                in1=xptF[:], op=ALU.mult))
                VE.do(lambda v: v.tensor_reduce(out=statA[:, 24:25],
                                                in_=lnsc[:], axis=AX,
                                                op=ALU.add), inc=sVE)
                VE.wait(sVE)
                VE.do(lambda v: v.tensor_scalar(
                    out=statA[:, 25:26], in0=statA[:, 24:25],
                    scalar1=1.0 / D, scalar2=EPS_RMS, op0=ALU.mult,
                    op1=ALU.add), inc=sVE)
                SC.wait(sVE)
                SC.do(lambda s: s.activation(out=statA[:, 26:27],
                                             in_=statA[:, 25:26],
                                             func=AF.Sqrt), inc=sAC)
                VE.wait(sAC)
                VE.do(lambda v, c=c: v.reciprocal(out=statF[:, c:c + 1],
                                                  in_=statA[:, 26:27]), inc=sVE)
                VE.wait(sVE)
                for gi, (grep, xqT, amcol) in enumerate(
                        ((g1rep, xq1T, 1), (g2rep, xq2T, 2))):
                    VE.do(lambda v, grep=grep: v.tensor_tensor(
                        out=tq[:], in0=xptF[:], in1=grep[:], op=ALU.mult))
                    VE.do(lambda v: v.tensor_reduce(
                        out=statA[:, 27:28], in_=tq[:], axis=AX, op=ALU.max,
                        apply_absolute_value=True), inc=sVE)
                    VE.wait(sVE)
                    VE.do(lambda v, c=c: v.tensor_tensor(
                        out=statA[:, 28:29], in0=statA[:, 27:28],
                        in1=statF[:, c:c + 1], op=ALU.mult))
                    VE.do(lambda v: v.tensor_scalar(
                        out=statA[:, 28:29], in0=statA[:, 28:29],
                        scalar1=1e-4, scalar2=None, op0=ALU.max))
                    VE.do(lambda v: v.reciprocal(out=statA[:, 29:30],
                                                 in_=statA[:, 28:29]), inc=sVE)
                    VE.wait(sVE)
                    VE.do(lambda v: v.tensor_scalar(
                        out=statA[:, 30:31], in0=statA[:, 29:30],
                        scalar1=127.0, scalar2=None, op0=ALU.mult))
                    VE.do(lambda v, c=c: v.tensor_tensor(
                        out=statA[:, 31:32], in0=statF[:, c:c + 1],
                        in1=statA[:, 30:31], op=ALU.mult))
                    VE.do(lambda v: v.tensor_scalar(
                        out=statA[:, 27:28], in0=statA[:, 28:29],
                        scalar1=1.0 / 127.0, scalar2=None, op0=ALU.mult))
                    VE.do(lambda v, c=c, gi=gi, amcol=amcol: v.tensor_tensor(
                        out=statF[:, c + amcol:c + amcol + 1],
                        in0=statA[:, 27:28], in1=tern[:, 6 + gi:7 + gi],
                        op=ALU.mult))
                    VE.do(lambda v: v.tensor_scalar(
                        out=tq[:], in0=tq[:], scalar1=statA[:, 31:32],
                        scalar2=None, op0=ALU.mult))
                    VE.do(lambda v: v.tensor_scalar(
                        out=pqb[:], in0=tq[:], scalar1=0.0, scalar2=128.0,
                        op0=ALU.max, op1=ALU.add))
                    VE.do(lambda v: v.tensor_scalar(
                        out=nqb[:], in0=tq[:], scalar1=0.0, scalar2=128.0,
                        op0=ALU.min, op1=ALU.subtract))
                    VE.do(lambda v: v.tensor_tensor(
                        out=xqc[:], in0=pqb[:], in1=nqb[:], op=ALU.add),
                        inc=sVE)
                    TE.wait(sVE)
                    for b in range(2):
                        for j in range(8):
                            cc = b * 8 + j
                            TE.do(lambda h, cc=cc, j=j: h.transpose(
                                pBq[:, j * 128:(j + 1) * 128],
                                xqc[:, cc * 128:(cc + 1) * 128], idb[:]),
                                inc=sTE if j == 7 else None)
                        VE.wait(sTE)
                        xqT3 = xqT.ap().rearrange("p (cx tx) -> p cx tx",
                                                  cx=KC)
                        VE.do(lambda v, b=b, t=t, xqT3=xqT3: v.tensor_copy(
                            out=xqT3[:, b * 8:(b + 1) * 8,
                                     t * 128:(t + 1) * 128],
                            in_=pBq8[:, :, :]), inc=sVE)
                        TE.wait(sVE)
            xq_ready = sVE.n

            w1t3 = w1t_d.rearrange("(c p) o -> p c o", p=128)
            w2t3 = w2t_d.rearrange("(c p) o -> p c o", p=128)
            w1pv = [w1pA.ap().rearrange("p (c o) -> p c o", c=KC),
                    w1pB.ap().rearrange("p (c o) -> p c o", c=KC)]
            w2pv = [w2pA.ap().rearrange("p (c o) -> p c o", c=KC),
                    w2pB.ap().rearrange("p (c o) -> p c o", c=KC)]
            w1pf = [w1pA, w1pB]
            w2pf = [w2pA, w2pB]
            GP.wait(sCC)
            panel_mark = {0: None, 1: None}
            panel_ld = {0: None, 1: None}
            hprev0 = {0: None, 1: None}
            hprev1 = {0: None, 1: None}
            for oc in range(KC):
                opar = oc % 2
                if panel_mark[opar] is not None:
                    GP.wait(sTE, panel_mark[opar])
                GP.do(lambda g, oc=oc, opar=opar: g.dma_start(
                    out=w1pv[opar][:, :, :],
                    in_=w1t3[:, :, oc * 512:(oc + 1) * 512]), inc=sGD, k=16)
                GP.do(lambda g, oc=oc, opar=opar: g.dma_start(
                    out=w2pv[opar][:, :, :],
                    in_=w2t3[:, :, oc * 512:(oc + 1) * 512]), inc=sGD, k=16)
                panel_ld[opar] = sGD.n
                SY.wait(sVE)
                SY.do(lambda s, oc=oc, opar=opar: s.dma_start(
                    out=g3sl[:, opar * 512:(opar + 1) * 512],
                    in_=g3_d[0:1, oc * 512:(oc + 1) * 512]
                    .broadcast_to([128, 512])), inc=sSY, k=16)
                g3_ld = sSY.n
                TE.wait(sGD, panel_ld[opar])
                if oc == 0:
                    TE.wait(sVE, xq_ready)
                for t in range(TT):
                    tpar = t % 2
                    bU1 = pU[2 * tpar]
                    bU2 = pU[2 * tpar + 1]
                    if hprev0[tpar] is not None:
                        TE.wait(sAC, hprev0[tpar])
                    for k in range(KC):
                        TE.do(lambda h, k=k, t=t, opar=opar, bU1=bU1: h.matmul(
                            bU1[:],
                            xq1T[:, k * NT + t * 128:k * NT + (t + 1) * 128],
                            w1pf[opar][:, k * 512:(k + 1) * 512],
                            start=(k == 0), stop=(k == KC - 1)),
                            inc=sTE if k == KC - 1 else None)
                    u1mm = sTE.n
                    if hprev1[tpar] is not None:
                        TE.wait(sVE, hprev1[tpar])
                    for k in range(KC):
                        TE.do(lambda h, k=k, t=t, opar=opar, bU2=bU2: h.matmul(
                            bU2[:],
                            xq2T[:, k * NT + t * 128:k * NT + (t + 1) * 128],
                            w2pf[opar][:, k * 512:(k + 1) * 512],
                            start=(k == 0), stop=(k == KC - 1)),
                            inc=sTE if k == KC - 1 else None)
                    u2mm = sTE.n
                    SC.wait(sTE, u1mm)
                    if hprev1[tpar] is not None:
                        SC.wait(sVE, hprev1[tpar])
                    c = t * 8
                    SC.do(lambda s, c=c, tpar=tpar, bU1=bU1: s.activation(
                        out=silu_sb[:, tpar * 512:(tpar + 1) * 512],
                        in_=bU1[:], func=AF.Silu,
                        scale=statF[:, c + 1:c + 2]), inc=sAC)
                    hprev0[tpar] = sAC.n
                    par = (oc * TT + t) % 2
                    VE.wait(sAC, hprev0[tpar])
                    VE.wait(sTE, u2mm)
                    VE.wait(sSY)
                    VE.do(lambda v, c=c, par=par, tpar=tpar, bU2=bU2:
                          v.scalar_tensor_tensor(
                              out=h_sb[:, par * 512:(par + 1) * 512],
                              in0=bU2[:], scalar=statF[:, c + 2:c + 3],
                              in1=silu_sb[:, tpar * 512:(tpar + 1) * 512],
                              op0=ALU.mult, op1=ALU.mult), inc=sVE)
                    hprev1[tpar] = sVE.n
                    VE.do(lambda v, par=par: v.tensor_tensor(
                        out=hh_scr[:], in0=h_sb[:, par * 512:(par + 1) * 512],
                        in1=h_sb[:, par * 512:(par + 1) * 512], op=ALU.mult))
                    VE.do(lambda v: v.tensor_reduce(
                        out=statA[:, 24:25], in_=hh_scr[:], axis=AX,
                        op=ALU.add), inc=sVE)
                    VE.wait(sVE)
                    VE.do(lambda v, t=t: v.tensor_tensor(
                        out=hstat[:, 2 * t:2 * t + 1],
                        in0=hstat[:, 2 * t:2 * t + 1], in1=statA[:, 24:25],
                        op=ALU.add))
                    VE.do(lambda v, par=par, opar=opar: v.tensor_tensor(
                        out=hh_scr[:], in0=h_sb[:, par * 512:(par + 1) * 512],
                        in1=g3sl[:, opar * 512:(opar + 1) * 512],
                        op=ALU.mult))
                    VE.do(lambda v: v.tensor_reduce(
                        out=statA[:, 25:26], in_=hh_scr[:], axis=AX,
                        op=ALU.max, apply_absolute_value=True), inc=sVE)
                    VE.wait(sVE)
                    VE.do(lambda v, t=t: v.tensor_tensor(
                        out=hstat[:, 2 * t + 1:2 * t + 2],
                        in0=hstat[:, 2 * t + 1:2 * t + 2],
                        in1=statA[:, 25:26], op=ALU.max), inc=sVE)
                    SY.wait(sVE)
                    SY.do(lambda s, t=t, oc=oc, par=par: s.dma_start(
                        out=h_sp[t * 128:(t + 1) * 128,
                                 oc * 512:(oc + 1) * 512],
                        in_=h_sb[:, par * 512:(par + 1) * 512]),
                        inc=sSY, k=16)
                panel_mark[opar] = sTE.n
            h_all_stored = sSY.n
            if PHASE <= 3:
                return nc

        full_barrier()
        with ExitStack() as SG:
            eg = SG.enter_context
            pBig = eg(nc.psum_tensor("pBig", [128, 2048], F32))
            pBr = eg(nc.psum_tensor("pBr", [128, 1024], BF16))
            pBr8 = pBr.ap().rearrange("p (j x) -> p j x", j=8)
            g3repF = eg(nc.sbuf_tensor("g3repF", [128, HD], F32))
            hld = eg(nc.sbuf_tensor("hld", [128, 2 * 2048], F32))
            y3 = eg(nc.sbuf_tensor("y3", [128, 2048], F32))
            p3b = eg(nc.sbuf_tensor("p3b", [128, 2048], BF16))
            n3b = eg(nc.sbuf_tensor("n3b", [128, 2048], BF16))
            xq3c = eg(nc.sbuf_tensor("xq3c", [128, 2048], BF16))
            xq3T = eg(nc.sbuf_tensor("xq3T", [128, 4 * HC * 128], BF16))
            w3pA = eg(nc.sbuf_tensor("w3pA", [128, 4 * 1024], BF16))
            w3pB = eg(nc.sbuf_tensor("w3pB", [128, 4 * 1024], BF16))
            xptG = eg(nc.sbuf_tensor("xptG", [128, D], F32))
            yt2 = eg(nc.sbuf_tensor("yt2", [128, 2 * D], F32))
            outsb = eg(nc.sbuf_tensor("outsb", [128, D], F32))
            w3pv = [w3pA.ap().rearrange("p (c d) -> p c d", c=4),
                    w3pB.ap().rearrange("p (c d) -> p c d", c=4)]
            w3pf = [w3pA, w3pB]
            w3t3 = w3t_d.rearrange("(c p) d -> p c d", p=128)

            SY.do(lambda s: s.dma_start(
                out=g3repF[:], in_=g3_d[0:1, :].broadcast_to([128, HD])),
                inc=sSY, k=16)
            g3f_ready = sSY.n
            SY.wait(sSY, h_all_stored)

            def emit_scales(t):
                c = t * 8
                VE.wait(sAC)
                VE.do(lambda v, t=t: v.tensor_scalar(
                    out=statA[:, 24:25], in0=hstat[:, 2 * t:2 * t + 1],
                    scalar1=1.0 / HD, scalar2=EPS_RMS, op0=ALU.mult,
                    op1=ALU.add), inc=sVE)
                SC.wait(sVE)
                SC.do(lambda s: s.activation(out=statA[:, 25:26],
                                             in_=statA[:, 24:25],
                                             func=AF.Sqrt), inc=sAC)
                VE.wait(sAC)
                VE.do(lambda v: v.reciprocal(out=statA[:, 26:27],
                                             in_=statA[:, 25:26]), inc=sVE)
                VE.wait(sVE)
                VE.do(lambda v, t=t: v.tensor_tensor(
                    out=statA[:, 27:28], in0=hstat[:, 2 * t + 1:2 * t + 2],
                    in1=statA[:, 26:27], op=ALU.mult))
                VE.do(lambda v: v.tensor_scalar(
                    out=statA[:, 27:28], in0=statA[:, 27:28], scalar1=1e-4,
                    scalar2=None, op0=ALU.max))
                VE.do(lambda v: v.reciprocal(out=statA[:, 28:29],
                                             in_=statA[:, 27:28]), inc=sVE)
                VE.wait(sVE)
                VE.do(lambda v: v.tensor_scalar(
                    out=statA[:, 28:29], in0=statA[:, 28:29], scalar1=127.0,
                    scalar2=None, op0=ALU.mult))
                VE.do(lambda v, c=c: v.tensor_tensor(
                    out=statF[:, c + 5:c + 6], in0=statA[:, 26:27],
                    in1=statA[:, 28:29], op=ALU.mult))
                VE.do(lambda v: v.tensor_scalar(
                    out=statA[:, 27:28], in0=statA[:, 27:28],
                    scalar1=1.0 / 127.0, scalar2=None, op0=ALU.mult))
                VE.do(lambda v, c=c: v.tensor_tensor(
                    out=statF[:, c + 6:c + 7], in0=statA[:, 27:28],
                    in1=tern[:, 8:9], op=ALU.mult), inc=sVE)

            mm3_mark = {0: None, 1: None}
            xq3T3v = xq3T.ap().rearrange("p (s c j) -> p s c j", s=4, j=128)

            def emit_quant(t):
                c = t * 8
                slot = t % 4
                for ch in range(4):
                    hpar = ch % 2
                    SY.wait(sVE)
                    SY.do(lambda s, t=t, ch=ch, hpar=hpar: s.dma_start(
                        out=hld[:, hpar * 2048:(hpar + 1) * 2048],
                        in_=h_sp[t * 128:(t + 1) * 128,
                                 ch * 2048:(ch + 1) * 2048]), inc=sSY, k=16)
                    VE.wait(sSY)
                    if t == 0 and ch == 0:
                        VE.wait(sSY, g3f_ready)
                    VE.do(lambda v, ch=ch, hpar=hpar: v.tensor_tensor(
                        out=y3[:], in0=hld[:, hpar * 2048:(hpar + 1) * 2048],
                        in1=g3repF[:, ch * 2048:(ch + 1) * 2048],
                        op=ALU.mult))
                    VE.do(lambda v, c=c: v.tensor_scalar(
                        out=y3[:], in0=y3[:], scalar1=statF[:, c + 5:c + 6],
                        scalar2=None, op0=ALU.mult))
                    VE.do(lambda v: v.tensor_scalar(
                        out=p3b[:], in0=y3[:], scalar1=0.0, scalar2=128.0,
                        op0=ALU.max, op1=ALU.add))
                    VE.do(lambda v: v.tensor_scalar(
                        out=n3b[:], in0=y3[:], scalar1=0.0, scalar2=128.0,
                        op0=ALU.min, op1=ALU.subtract))
                    VE.do(lambda v: v.tensor_tensor(
                        out=xq3c[:], in0=p3b[:], in1=n3b[:], op=ALU.add),
                        inc=sVE)
                    TE.wait(sVE)
                    for b in range(2):
                        for j in range(8):
                            cc = b * 8 + j
                            TE.do(lambda h, cc=cc, j=j: h.transpose(
                                pBr[:, j * 128:(j + 1) * 128],
                                xq3c[:, cc * 128:(cc + 1) * 128], idb[:]),
                                inc=sTE if j == 7 else None)
                        VE.wait(sTE)
                        if mm3_mark[(t // 2) % 2] is not None:
                            VE.wait(sTE, mm3_mark[(t // 2) % 2])
                        VE.do(lambda v, ch=ch, b=b, slot=slot: v.tensor_copy(
                            out=xq3T3v[:, slot,
                                       ch * 16 + b * 8:ch * 16 + (b + 1) * 8,
                                       :], in_=pBr8[:, :, :]), inc=sVE)
                        TE.wait(sVE)

            w3ld = {0: None, 1: None}
            w3mm = {0: None, 1: None}
            evac_done = None
            emit_scales(0)
            emit_quant(0)
            emit_scales(1)
            emit_quant(1)
            for g in range(4):
                gpar = g % 2
                t0, t1 = 2 * g, 2 * g + 1
                xq_ready3 = sVE.n
                for dh in range(2):
                    for hb in range(16):
                        wpar = hb % 2
                        if w3mm[wpar] is not None:
                            GP.wait(sTE, w3mm[wpar])
                        GP.do(lambda gg, hb=hb, dh=dh, wpar=wpar:
                              gg.dma_start(
                                  out=w3pv[wpar][:, :, :],
                                  in_=w3t3[:, hb * 4:(hb + 1) * 4,
                                           dh * 1024:(dh + 1) * 1024]),
                              inc=sGD, k=16)
                        w3ld[wpar] = sGD.n
                        TE.wait(sGD, w3ld[wpar])
                        if hb == 0:
                            if dh == 0:
                                TE.wait(sVE, xq_ready3)
                            if evac_done is not None:
                                TE.wait(sVE, evac_done)
                        for hcl in range(4):
                            hc = hb * 4 + hcl
                            for ti, tt_ in enumerate((t0, t1)):
                                for dc in range(2):
                                    TE.do(lambda h, hc=hc, hcl=hcl, dc=dc,
                                          ti=ti, tt_=tt_, wpar=wpar:
                                          h.matmul(
                                              pBig[:, ti * 1024 + dc * 512:
                                                   ti * 1024 +
                                                   (dc + 1) * 512],
                                              xq3T[:, (tt_ % 4) * 8192 +
                                                   hc * 128:
                                                   (tt_ % 4) * 8192 +
                                                   (hc + 1) * 128],
                                              w3pf[wpar][:, hcl * 1024 +
                                                         dc * 512:
                                                         hcl * 1024 +
                                                         (dc + 1) * 512],
                                              start=(hc == 0),
                                              stop=(hc == HC - 1)),
                                          inc=sTE if (hcl == 3 and ti == 1
                                                      and dc == 1) else None)
                        w3mm[wpar] = sTE.n
                    dhmm = sTE.n
                    VE.wait(sTE, dhmm)
                    for ti, tt_ in enumerate((t0, t1)):
                        c = tt_ * 8
                        VE.do(lambda v, c=c, ti=ti, dh=dh: v.tensor_scalar(
                            out=yt2[:, ti * 2048 + dh * 1024:
                                    ti * 2048 + (dh + 1) * 1024],
                            in0=pBig[:, ti * 1024:(ti + 1) * 1024],
                            scalar1=statF[:, c + 6:c + 7], scalar2=None,
                            op0=ALU.mult), inc=sVE)
                    evac_done = sVE.n
                mm3_mark[gpar] = sTE.n
                if g + 1 < 4:
                    emit_scales(2 * g + 2)
                    emit_quant(2 * g + 2)
                    emit_scales(2 * g + 3)
                    emit_quant(2 * g + 3)
                for ti, tt_ in enumerate((t0, t1)):
                    SY.wait(sVE)
                    SY.do(lambda s, tt_=tt_: s.dma_start(
                        out=xptG[:], in_=xp_sp[tt_ * 128:(tt_ + 1) * 128, :]),
                        inc=sSY, k=16)
                    VE.wait(sSY)
                    VE.do(lambda v, ti=ti: v.tensor_tensor(
                        out=yt2[:, ti * 2048:(ti + 1) * 2048],
                        in0=yt2[:, ti * 2048:(ti + 1) * 2048],
                        in1=xptG[:], op=ALU.add), inc=sVE)
                    emit_ln(yt2[:, ti * 2048:(ti + 1) * 2048], outsb[:])
                    SY.wait(sVE)
                    SY.do(lambda s, tt_=tt_: s.dma_start(
                        out=out_d[tt_ * 128:(tt_ + 1) * 128, :],
                        in_=outsb[:]), inc=sST, k=16)
                    VE.wait(sST)
            SY.wait(sST, sST.n)

    return nc


_NC_CACHE = {}


def kernel(L0_attn_x, L0_attn_wq, L0_attn_bq, L0_attn_wk, L0_attn_bk,
           L0_attn_wv, L0_attn_bv, L0_attn_wo, L0_attn_bo,
           L0_ffn_w1, L0_ffn_g1, L0_ffn_w2, L0_ffn_g2, L0_ffn_w3, L0_ffn_g3):
    if "nc" not in _NC_CACHE:
        _NC_CACHE["nc"] = build_program()
    nc = _NC_CACHE["nc"]

    def f32(a):
        return np.ascontiguousarray(np.asarray(a, dtype=np.float32))

    x = f32(L0_attn_x).reshape(8192, 2048)
    wq, wk, wv, wo = f32(L0_attn_wq), f32(L0_attn_wk), f32(L0_attn_wv), f32(L0_attn_wo)
    w1, w2, w3 = f32(L0_ffn_w1), f32(L0_ffn_w2), f32(L0_ffn_w3)
    in_maps = []
    for c in range(8):
        in_maps.append({
            "x": np.ascontiguousarray(x[c * NT:(c + 1) * NT]),
            "wqc": np.ascontiguousarray(wq[:, c * 256:(c + 1) * 256]),
            "wkc": np.ascontiguousarray(wk[:, c * 256:(c + 1) * 256]),
            "wvc": np.ascontiguousarray(wv[:, c * 256:(c + 1) * 256]),
            "woc": np.ascontiguousarray(wo[:, c * 256:(c + 1) * 256]),
            "bq_r": f32(L0_attn_bq).reshape(16, 128),
            "bo_r": f32(L0_attn_bo).reshape(16, 128),
            "bo_row": f32(L0_attn_bo).reshape(1, D),
            "bk": f32(L0_attn_bk).reshape(1, D),
            "bv": f32(L0_attn_bv).reshape(1, D),
            "g1": f32(L0_ffn_g1).reshape(1, D),
            "g2": f32(L0_ffn_g2).reshape(1, D),
            "g3": f32(L0_ffn_g3).reshape(1, HD),
            "w1c": np.ascontiguousarray(w1[:, c * 256:(c + 1) * 256]),
            "w2c": np.ascontiguousarray(w2[:, c * 256:(c + 1) * 256]),
            "w3c": np.ascontiguousarray(w3[:, c * 1024:(c + 1) * 1024]),
            "w1s": np.ascontiguousarray(w1[c * (HD // 8):(c + 1) * (HD // 8)]),
            "w2s": np.ascontiguousarray(w2[c * (HD // 8):(c + 1) * (HD // 8)]),
            "w3s": np.ascontiguousarray(w3[c * (D // 8):(c + 1) * (D // 8)]),
        })
    res = run_bass_kernel_spmd(nc, in_maps, core_ids=list(range(8)))
    out = np.concatenate([np.asarray(r["out"]) for r in res.results], axis=0)
    return np.ascontiguousarray(out.reshape(1, 8192, 2048).astype(np.float32))


# version 13
# speedup vs baseline: 2.1013x; 1.0275x over previous
import os
import numpy as np
from contextlib import ExitStack

import concourse.bass as bass
import concourse.mybir as mybir
from concourse.bass_utils import run_bass_kernel_spmd

F32 = mybir.dt.float32
BF16 = mybir.dt.bfloat16
I8 = mybir.dt.int8
AF = mybir.ActivationFunctionType
ALU = mybir.AluOpType
AX = mybir.AxisListType.X

NT = 1024
TT = 8
D = 2048
KC = 16
NH = 16
HD = 8192
HC = 64
EPS_LN = 1e-5
EPS_RMS = 1e-8
EPS_AT = 1e-8
ATANH_HALF = 0.5493061443340548


class Sem:
    def __init__(self, handle):
        self.h = handle
        self.n = 0


class Eng:
    def __init__(self, section):
        self.section = section

    def do(self, fn, inc=None, k=1):
        if inc is None:
            self.section(fn)
            return None
        self.section(lambda h, fn=fn, s=inc.h, k=k: fn(h).then_inc(s, k))
        inc.n += k
        return inc.n

    def wait(self, sem, val=None):
        v = sem.n if val is None else val
        if v > 0:
            self.section(lambda h, s=sem.h, v=v: h.wait_ge(s, v))


def build_program():
    PHASE = int(os.environ.get("KPHASE", "99"))
    nc = bass.Bass("TRN2", target_bir_lowering=False,
                   detect_race_conditions=False)

    x_d = nc.dram_tensor("x", [NT, D], F32, kind="ExternalInput").ap()
    w_attc = {m: nc.dram_tensor(m + "c", [D, 256], F32, kind="ExternalInput").ap()
              for m in ("wq", "wk", "wv", "wo")}
    bqr_d = nc.dram_tensor("bq_r", [16, 128], F32, kind="ExternalInput").ap()
    bor_d = nc.dram_tensor("bo_r", [16, 128], F32, kind="ExternalInput").ap()
    bo_row = nc.dram_tensor("bo_row", [1, D], F32, kind="ExternalInput").ap()
    bk_d = nc.dram_tensor("bk", [1, D], F32, kind="ExternalInput").ap()
    bv_d = nc.dram_tensor("bv", [1, D], F32, kind="ExternalInput").ap()
    g1_d = nc.dram_tensor("g1", [1, D], F32, kind="ExternalInput").ap()
    g2_d = nc.dram_tensor("g2", [1, D], F32, kind="ExternalInput").ap()
    g3_d = nc.dram_tensor("g3", [1, HD], F32, kind="ExternalInput").ap()
    w1c_d = nc.dram_tensor("w1c", [HD, 256], F32, kind="ExternalInput").ap()
    w2c_d = nc.dram_tensor("w2c", [HD, 256], F32, kind="ExternalInput").ap()
    w3c_d = nc.dram_tensor("w3c", [D, 1024], F32, kind="ExternalInput").ap()
    w1s_d = nc.dram_tensor("w1s", [HD // 8, D], F32, kind="ExternalInput").ap()
    w2s_d = nc.dram_tensor("w2s", [HD // 8, D], F32, kind="ExternalInput").ap()
    w3s_d = nc.dram_tensor("w3s", [D // 8, HD], F32, kind="ExternalInput").ap()
    out_d = nc.dram_tensor("out", [NT, D], F32, kind="ExternalOutput").ap()

    cc_in = nc.dram_tensor("cc_in", [128, 2068], F32)
    cc_out = nc.dram_tensor("cc_out", [128, 2068], F32)
    xp_sp = nc.dram_tensor("xp_sp", [NT, D], F32).ap()
    y_sp = nc.dram_tensor("y_sp", [NT, D], F32).ap()
    h_sp = nc.dram_tensor("h_sp", [NT, HD], F32).ap()
    agA_in = nc.dram_tensor("agA_in", [256, 8192], BF16)
    agA_out = nc.dram_tensor("agA_out", [D, 8192], BF16)
    agB_in = nc.dram_tensor("agB_in", [256, 8192], BF16)
    agB_out = nc.dram_tensor("agB_out", [D, 8192], BF16)
    ag1_in = nc.dram_tensor("ag1_in", [D // 8, HD], I8)
    ag2_in = nc.dram_tensor("ag2_in", [D // 8, HD], I8)
    ag3_in = nc.dram_tensor("ag3_in", [HD // 8, D], I8)
    w1t_d = nc.dram_tensor("w1t", [D, HD], I8).ap()
    w2t_d = nc.dram_tensor("w2t", [D, HD], I8).ap()
    w3t_d = nc.dram_tensor("w3t", [HD, D], I8).ap()

    with ExitStack() as outer:
        eo = outer.enter_context
        sSY = Sem(eo(nc.semaphore("sSY")))
        sL2 = Sem(eo(nc.semaphore("sL2")))
        sGD = Sem(eo(nc.semaphore("sGD")))
        sTE = Sem(eo(nc.semaphore("sTE")))
        sVE = Sem(eo(nc.semaphore("sVE")))
        sAC = Sem(eo(nc.semaphore("sAC")))
        sGC = Sem(eo(nc.semaphore("sGC")))
        sCC = Sem(eo(nc.semaphore("sCC")))
        sST = Sem(eo(nc.semaphore("sST")))

        idf = eo(nc.sbuf_tensor("idf", [128, 128], F32))
        idb = eo(nc.sbuf_tensor("idb", [128, 128], BF16))
        onesf = eo(nc.sbuf_tensor("onesf", [128, 128], F32))
        onec = eo(nc.sbuf_tensor("onec", [128, 1], F32))
        colix = eo(nc.sbuf_tensor("colix", [128, 128], F32))
        rowix = eo(nc.sbuf_tensor("rowix", [128, 1], F32))
        bqc = eo(nc.sbuf_tensor("bqc", [128, 16], F32))
        boc = eo(nc.sbuf_tensor("boc", [128, 16], F32))
        wsumc = eo(nc.sbuf_tensor("wsumc", [128, 4], F32))
        tern = eo(nc.sbuf_tensor("tern", [128, 12], F32))
        statA = eo(nc.sbuf_tensor("statA", [128, 32], F32))
        statF = eo(nc.sbuf_tensor("statF", [128, TT * 8], F32))
        hstat = eo(nc.sbuf_tensor("hstat", [128, TT * 2], F32))
        scols = eo(nc.sbuf_tensor("scols", [128, 64], F32))
        statN = eo(nc.sbuf_tensor("statN", [128, 24], F32))
        lnsc = eo(nc.sbuf_tensor("lnsc", [128, D], F32))

        blk = eo(nc.Block())
        SY = Eng(blk.sync)
        SC = Eng(blk.scalar)
        VE = Eng(blk.vector)
        TE = Eng(blk.tensor)
        GP = Eng(blk.gpsimd)

        GP.do(lambda g: g.iota(colix[:], pattern=[[1, 128]], base=0,
                               channel_multiplier=0,
                               allow_small_or_imprecise_dtypes=True))
        GP.do(lambda g: g.iota(rowix[:], pattern=[[0, 1]], base=0,
                               channel_multiplier=1,
                               allow_small_or_imprecise_dtypes=True), inc=sGC)
        VE.wait(sGC)
        VE.do(lambda v: v.tensor_scalar(out=idf[:], in0=colix[:],
                                        scalar1=rowix[:, :], scalar2=None,
                                        op0=ALU.is_equal))
        VE.do(lambda v: v.tensor_copy(out=idb[:], in_=idf[:]))
        VE.do(lambda v: v.memset(onesf[:], 1.0))
        VE.do(lambda v: v.memset(onec[:], 1.0))
        VE.do(lambda v: v.memset(wsumc[:], 0.0), inc=sVE)
        consts_ready = sVE.n

        def emit_ln(ytile, xout):
            VE.wait(sAC)
            VE.do(lambda v: v.tensor_reduce(out=statA[:, 24:25], in_=ytile,
                                            axis=AX, op=ALU.add), inc=sVE)
            VE.wait(sVE)
            VE.do(lambda v: v.tensor_tensor(out=lnsc[:, 0:D], in0=ytile,
                                            in1=ytile, op=ALU.mult))
            VE.do(lambda v: v.tensor_reduce(out=statA[:, 25:26],
                                            in_=lnsc[:, 0:D], axis=AX,
                                            op=ALU.add), inc=sVE)
            VE.wait(sVE)
            VE.do(lambda v: v.tensor_scalar(out=statA[:, 26:27],
                                            in0=statA[:, 24:25],
                                            scalar1=1.0 / D, scalar2=None,
                                            op0=ALU.mult))
            VE.do(lambda v: v.tensor_tensor(out=statA[:, 27:28],
                                            in0=statA[:, 26:27],
                                            in1=statA[:, 26:27], op=ALU.mult))
            VE.do(lambda v: v.tensor_scalar(out=statA[:, 28:29],
                                            in0=statA[:, 25:26],
                                            scalar1=1.0 / D, scalar2=None,
                                            op0=ALU.mult))
            VE.do(lambda v: v.tensor_tensor(out=statA[:, 29:30],
                                            in0=statA[:, 28:29],
                                            in1=statA[:, 27:28],
                                            op=ALU.subtract))
            VE.do(lambda v: v.tensor_scalar(out=statA[:, 29:30],
                                            in0=statA[:, 29:30],
                                            scalar1=EPS_LN, scalar2=None,
                                            op0=ALU.add), inc=sVE)
            SC.wait(sVE)
            SC.do(lambda s: s.activation(out=statA[:, 30:31],
                                         in_=statA[:, 29:30], func=AF.Sqrt),
                  inc=sAC)
            VE.wait(sAC)
            VE.do(lambda v: v.reciprocal(out=statA[:, 31:32],
                                         in_=statA[:, 30:31]), inc=sVE)
            VE.wait(sVE)
            VE.do(lambda v: v.tensor_scalar(out=xout, in0=ytile,
                                            scalar1=statA[:, 26:27],
                                            scalar2=statA[:, 31:32],
                                            op0=ALU.subtract, op1=ALU.mult),
                  inc=sVE)

        def full_barrier():
            sems = (sSY, sL2, sGD, sTE, sVE, sAC, sGC, sST)
            for eng in (SY, SC, VE, TE, GP):
                for sm in sems:
                    eng.wait(sm)

        with ExitStack() as SA:
            ea = SA.enter_context
            xThi = ea(nc.sbuf_tensor("xThi", [128, KC * NT], BF16))
            xTlo = ea(nc.sbuf_tensor("xTlo", [128, KC * NT], BF16))
            kvacc = ea(nc.sbuf_tensor("kvacc", [128, 2068], F32))
            XSld = ExitStack()
            xld = XSld.enter_context(nc.sbuf_tensor("xld", [128, D], F32))
            VE.do(lambda v: v.memset(kvacc[:], 0.0), inc=sVE)
            pQ = ea(nc.psum_tensor("pQ", [128, 1024], F32))
            pK = ea(nc.psum_tensor("pK", [128, 512], F32))
            pV = ea(nc.psum_tensor("pV", [128, 512], F32))
            pKV = ea(nc.psum_tensor("pKV", [128, 512], F32))
            pT = ea(nc.psum_tensor("pT", [128, 512], F32))
            pBt = ea(nc.psum_tensor("pBt", [128, 1024], BF16))
            pT2 = ea(nc.psum_tensor("pT2", [128, 512], F32))

            xThi3 = xThi.ap().rearrange("p (c t) -> p c t", c=KC)
            xTlo3 = xTlo.ap().rearrange("p (c t) -> p c t", c=KC)
            pT4 = pT.ap().rearrange("p (j x) -> p j x", j=4)
            pBt8 = pBt.ap().rearrange("p (j x) -> p j x", j=8)

            def transpose16_f32(src_tile, dst3, dst_sel):
                for b in range(4):
                    for j in range(4):
                        c = b * 4 + j
                        TE.do(lambda h, c=c, j=j, s=src_tile: h.transpose(
                            pT[:, j * 128:(j + 1) * 128],
                            s[:, c * 128:(c + 1) * 128], idf[:]),
                            inc=sTE if j == 3 else None)
                    VE.wait(sTE)
                    d1, d2 = dst_sel(b)
                    VE.do(lambda v, d1=d1, d2=d2, dst3=dst3: v.tensor_copy(
                        out=dst3[:, d1, d2], in_=pT4[:, :, :]), inc=sVE)
                    TE.wait(sVE)

            with ExitStack() as PW:
                ew = PW.enter_context
                wcl = ew(nc.sbuf_tensor("wcl", [128, 4096], F32))
                whi = ew(nc.sbuf_tensor("whi", [128, 4096], BF16))
                wlo = ew(nc.sbuf_tensor("wlo", [128, 4096], BF16))
                wstg = ew(nc.sbuf_tensor("wstg", [128, 2048], BF16))
                wcl3 = wcl.ap().rearrange("p (cc j) -> p cc j", cc=16)
                pBt8w = pBt.ap().rearrange("p (j x) -> p j x", j=8)
                prep_cfg = [("wk", agA_in, 0), ("wv", agA_in, 4096),
                            ("wq", agB_in, 0), ("wo", agB_in, 4096)]
                st_mark = {"agA": 0, "agB": 0}
                for mi, (mn, agd, colb) in enumerate(prep_cfg):
                    src3 = w_attc[mn].rearrange("(cc p) j -> p cc j", p=128)
                    SY.wait(sTE)
                    SY.wait(sVE)
                    SY.do(lambda s, src3=src3: s.dma_start(
                        out=wcl3[:, :, :], in_=src3[:, :, :]), inc=sSY, k=16)
                    VE.wait(sSY)
                    VE.do(lambda v: v.tensor_copy(out=whi[:], in_=wcl[:]))
                    VE.wait(sVE)
                    VE.do(lambda v: v.tensor_tensor(
                        out=wlo[:], in0=wcl[:], in1=whi[:],
                        op=ALU.subtract), inc=sVE)
                    for half, hbuf in ((0, whi), (1, wlo)):
                        for kb in range(2):
                            TE.wait(sVE)
                            for bb in range(2):
                                for j in range(8):
                                    cc = bb * 8 + j
                                    TE.do(lambda h, cc=cc, j=j, kb=kb,
                                          hbuf=hbuf: h.transpose(
                                              pBt[:, j * 128:(j + 1) * 128],
                                              hbuf[:, cc * 256 + kb * 128:
                                                   cc * 256 + kb * 128 + 128],
                                              idb[:]),
                                          inc=sTE if j == 7 else None)
                                VE.wait(sTE)
                                VE.do(lambda v, bb=bb: v.tensor_copy(
                                    out=wstg.ap().rearrange(
                                        "p (cc j) -> p cc j", cc=16)
                                    [:, bb * 8:(bb + 1) * 8, :],
                                    in_=pBt8w[:, :, :]), inc=sVE)
                                TE.wait(sVE)
                            SY.wait(sVE)
                            SY.do(lambda s, agd=agd, colb=colb, half=half,
                                  kb=kb: s.dma_start(
                                      out=agd.ap()[kb * 128:(kb + 1) * 128,
                                                   colb + half * 2048:
                                                   colb + half * 2048 + 2048],
                                      in_=wstg[:]), inc=sSY, k=16)
                            VE.wait(sSY)
                agw_stores = sSY.n
                GP.wait(sSY, agw_stores)
                GP.do(lambda g: g.collective_compute(
                    "AllGather", ALU.bypass, replica_groups=[list(range(8))],
                    ins=[agA_in.ap().opt()], outs=[agA_out.ap().opt()]),
                    inc=sCC)
                agA_done = 1
                agB_done = 2

            for t in range(TT):
                SY.wait(sTE)
                SY.do(lambda s, t=t: s.dma_start(
                    out=xld[:], in_=x_d[t * 128:(t + 1) * 128, :]),
                    inc=sSY, k=16)
                TE.wait(sSY)
                if t == 0:
                    TE.wait(sVE, consts_ready)
                for b in range(4):
                    for j in range(4):
                        c = b * 4 + j
                        TE.do(lambda h, c=c, j=j: h.transpose(
                            pT[:, j * 128:(j + 1) * 128],
                            xld[:, c * 128:(c + 1) * 128], idf[:]),
                            inc=sTE if j == 3 else None)
                    VE.wait(sTE)
                    VE.do(lambda v, b=b, t=t: v.tensor_copy(
                        out=xThi3[:, b * 4:(b + 1) * 4,
                                  t * 128:(t + 1) * 128],
                        in_=pT4[:, :, :]), inc=sVE)
                    VE.wait(sVE)
                    VE.do(lambda v, b=b, t=t: v.tensor_tensor(
                        out=xTlo3[:, b * 4:(b + 1) * 4,
                                  t * 128:(t + 1) * 128],
                        in0=pT4[:, :, :],
                        in1=xThi3[:, b * 4:(b + 1) * 4,
                                  t * 128:(t + 1) * 128],
                        op=ALU.subtract), inc=sVE)
                    TE.wait(sVE)
            xT_ready = sVE.n

            SY.wait(sTE)
            SY.do(lambda s: s.dma_start(out=xld[0:16, 0:128], in_=bqr_d[:, :]),
                  inc=sSY, k=16)
            TE.wait(sSY)
            TE.do(lambda h: h.transpose(pT[:, 0:16], xld[0:16, 0:128],
                                        idf[0:16, 0:16]), inc=sTE)
            VE.wait(sTE)
            VE.do(lambda v: v.tensor_copy(out=bqc[:], in_=pT[:, 0:16]),
                  inc=sVE)
            SY.wait(sTE)
            SY.do(lambda s: s.dma_start(out=xld[0:16, 0:128], in_=bor_d[:, :]),
                  inc=sSY, k=16)
            TE.wait(sSY)
            TE.wait(sVE)
            TE.do(lambda h: h.transpose(pT[:, 128:144], xld[0:16, 0:128],
                                        idf[0:16, 0:16]), inc=sTE)
            VE.wait(sTE)
            VE.do(lambda v: v.tensor_copy(out=boc[:], in_=pT[:, 128:144]),
                  inc=sVE)
            SY.wait(sVE)

            if PHASE <= 0:
                return nc
            XSld.close()
            FW = 512
            with ExitStack() as S1:
                e1 = S1.enter_context
                wkhi = e1(nc.sbuf_tensor("wkhi", [128, KC * FW], BF16))
                wklo = e1(nc.sbuf_tensor("wklo", [128, KC * FW], BF16))
                wvhi = e1(nc.sbuf_tensor("wvhi", [128, KC * FW], BF16))
                wvlo = e1(nc.sbuf_tensor("wvlo", [128, KC * FW], BF16))
                bkrep = e1(nc.sbuf_tensor("bkrep", [128, D], F32))
                bvrep = e1(nc.sbuf_tensor("bvrep", [128, D], F32))
                ktile = e1(nc.sbuf_tensor("ktile", [128, 2 * FW], F32))
                vtile = e1(nc.sbuf_tensor("vtile", [128, 2 * FW], F32))
                etile = e1(nc.sbuf_tensor("etile", [128, 2 * FW], F32))
                etile2 = e1(nc.sbuf_tensor("etile2", [128, 2 * FW], F32))
                wsld = e1(nc.sbuf_tensor("wsld", [128, 2048], F32))

                SY.do(lambda s: s.dma_start(
                    out=bkrep[:], in_=bk_d[0:1, :].broadcast_to([128, D])),
                    inc=sSY, k=16)
                SY.do(lambda s: s.dma_start(
                    out=bvrep[:], in_=bv_d[0:1, :].broadcast_to([128, D])),
                    inc=sSY, k=16)
                bias_ready = sSY.n

                wsum_jobs = []
                for i in range(8):
                    wsum_jobs.append((0, w1s_d, i, 0))
                for i in range(8):
                    wsum_jobs.append((1, w2s_d, i, 0))
                for i in range(2):
                    for hf in range(4):
                        wsum_jobs.append((2, w3s_d, i, hf))

                def emit_wsum_job(job):
                    m, dram, i, hf = job
                    SC.wait(sVE)
                    SC.do(lambda s, dram=dram, i=i, hf=hf: s.dma_start(
                        out=wsld[:],
                        in_=dram[i * 128:(i + 1) * 128,
                                 hf * 2048:(hf + 1) * 2048]), inc=sL2, k=16)
                    VE.wait(sL2)
                    VE.do(lambda v: v.tensor_reduce(
                        out=statA[:, 0:1], in_=wsld[:], axis=AX,
                        op=ALU.add, apply_absolute_value=True), inc=sVE)
                    VE.wait(sVE)
                    VE.do(lambda v, m=m: v.tensor_tensor(
                        out=wsumc[:, m:m + 1], in0=wsumc[:, m:m + 1],
                        in1=statA[:, 0:1], op=ALU.add), inc=sVE)

                evacK = {0: None, 1: None}
                evacV = {0: None, 1: None}
                kvmm_half = {0: None, 1: None}
                headprev = {}
                panel_mark = {"khi": None, "klo": None, "vhi": None,
                              "vlo": None}
                ld_mark = {}
                agA3 = agA_out.ap().rearrange("(c p) f -> p c f", p=128)
                for q in range(4):
                    if q == 0:
                        SY.wait(sCC, agA_done)
                    for nm, buf, colb in (("khi", wkhi, 0), ("vhi", wvhi, 4096),
                                          ("klo", wklo, 2048),
                                          ("vlo", wvlo, 6144)):
                        if panel_mark[nm] is not None:
                            SY.wait(sTE, panel_mark[nm])
                        SY.do(lambda s, buf=buf, colb=colb, q=q: s.dma_start(
                            out=buf.ap().rearrange("p (c f) -> p c f", c=KC)
                            [:, :, :],
                            in_=agA3[:, :, colb + q * FW:
                                     colb + (q + 1) * FW]), inc=sSY, k=16)
                        ld_mark[nm] = sSY.n
                    if q == 0:
                        GP.wait(sSY)
                        GP.do(lambda g: g.collective_compute(
                            "AllGather", ALU.bypass,
                            replica_groups=[list(range(8))],
                            ins=[agB_in.ap().opt()],
                            outs=[agB_out.ap().opt()]), inc=sCC)
                        TE.wait(sSY, bias_ready)
                        TE.wait(sVE, xT_ready)

                    def emit_proj(t):
                        par = t % 2
                        bK = pK[:] if par == 0 else pQ[:, 0:FW]
                        bV = pV[:] if par == 0 else pQ[:, 512:512 + FW]
                        ksl = slice(par * FW, (par + 1) * FW)
                        if evacK[par] is not None:
                            TE.wait(sVE, evacK[par])
                        passes = [(xThi, wkhi, "khi"), (xTlo, wkhi, "khi"),
                                  (xThi, wklo, "klo")]
                        for pi, (xa, wb, nm) in enumerate(passes):
                            TE.wait(sSY, ld_mark["vlo"])
                            for k in range(KC):
                                TE.do(lambda h, k=k, t=t, bK=bK, xa=xa, wb=wb:
                                      h.matmul(
                                          bK,
                                          xa[:, k * NT + t * 128:
                                             k * NT + (t + 1) * 128],
                                          wb[:, k * FW:(k + 1) * FW],
                                          start=(pi == 0 and k == 0),
                                          stop=(pi == 2 and k == KC - 1)),
                                      inc=sTE if (pi == 2 and k == KC - 1)
                                      else None)
                        kmm = sTE.n
                        if evacV[par] is not None:
                            TE.wait(sVE, evacV[par])
                        vpasses = [(xThi, wvhi, "vhi"), (xTlo, wvhi, "vhi"),
                                   (xThi, wvlo, "vlo")]
                        for pi, (xa, wb, nm) in enumerate(vpasses):
                            TE.wait(sSY, ld_mark["vlo"])
                            for k in range(KC):
                                TE.do(lambda h, k=k, t=t, bV=bV, xa=xa, wb=wb:
                                      h.matmul(
                                          bV,
                                          xa[:, k * NT + t * 128:
                                             k * NT + (t + 1) * 128],
                                          wb[:, k * FW:(k + 1) * FW],
                                          start=(pi == 0 and k == 0),
                                          stop=(pi == 2 and k == KC - 1)),
                                      inc=sTE if (pi == 2 and k == KC - 1)
                                      else None)
                        vmm = sTE.n
                        VE.wait(sTE, kmm)
                        if kvmm_half[par] is not None:
                            VE.wait(sTE, kvmm_half[par])
                        VE.do(lambda v, q=q, bK=bK, ksl=ksl: v.tensor_tensor(
                            out=ktile[:, ksl], in0=bK,
                            in1=bkrep[:, q * FW:(q + 1) * FW], op=ALU.add))
                        VE.do(lambda v, ksl=ksl: v.tensor_scalar(
                            out=etile[:, ksl], in0=ktile[:, ksl], scalar1=0.0,
                            scalar2=None, op0=ALU.min), inc=sVE)
                        evacK[par] = sVE.n
                        SC.wait(sVE, evacK[par])
                        SC.do(lambda s, ksl=ksl: s.activation(
                            out=etile2[:, ksl], in_=etile[:, ksl],
                            func=AF.Exp), inc=sAC)
                        VE.wait(sAC)
                        VE.do(lambda v, ksl=ksl: v.scalar_tensor_tensor(
                            out=ktile[:, ksl], in0=ktile[:, ksl], scalar=0.0,
                            in1=etile2[:, ksl], op0=ALU.max, op1=ALU.add))
                        VE.wait(sTE, vmm)
                        VE.do(lambda v, q=q, bV=bV, ksl=ksl: v.tensor_tensor(
                            out=vtile[:, ksl], in0=bV,
                            in1=bvrep[:, q * FW:(q + 1) * FW], op=ALU.add),
                            inc=sVE)
                        evacV[par] = sVE.n

                    def emit_kv(t):
                        par = t % 2
                        for h2 in range(4):
                            gh = q * 4 + h2
                            bank = pKV if h2 % 2 == 0 else pT
                            off = par * FW + h2 * 128
                            if (h2 % 2) in headprev:
                                TE.wait(sVE, headprev[h2 % 2])
                            TE.wait(sVE, evacV[par])
                            TE.do(lambda h, off=off, bank=bank: h.matmul(
                                bank[:, 0:128],
                                ktile[:, off:off + 128],
                                vtile[:, off:off + 128],
                                start=True, stop=False))
                            TE.do(lambda h, off=off, bank=bank: h.matmul(
                                bank[:, 128:129],
                                ktile[:, off:off + 128],
                                onec[:], start=False, stop=True), inc=sTE)
                            VE.wait(sTE)
                            VE.do(lambda v, gh=gh, bank=bank: v.tensor_tensor(
                                out=kvacc[:, gh * 129:(gh + 1) * 129],
                                in0=kvacc[:, gh * 129:(gh + 1) * 129],
                                in1=bank[:, 0:129], op=ALU.add), inc=sVE)
                            headprev[h2 % 2] = sVE.n
                        kvmm_half[par] = sTE.n

                    for t in range(TT + 1):
                        if t < TT:
                            emit_proj(t)
                        if t >= 1:
                            emit_kv(t - 1)
                        if wsum_jobs:
                            emit_wsum_job(wsum_jobs.pop(0))
                    for nm in panel_mark:
                        panel_mark[nm] = sTE.n
                while wsum_jobs:
                    emit_wsum_job(wsum_jobs.pop(0))

                TE.wait(sVE)
                for m in range(3):
                    TE.do(lambda h, m=m: h.matmul(
                        pT[:, 256 + m:257 + m], onesf[:], wsumc[:, m:m + 1],
                        start=True, stop=True), inc=sTE if m == 2 else None)
                VE.wait(sTE)
                VE.do(lambda v: v.tensor_copy(out=kvacc[:, 2064:2067],
                                              in_=pT[:, 256:259]), inc=sVE)

                SY.wait(sVE)
                SY.do(lambda s: s.dma_start(out=cc_in.ap()[:, :],
                                            in_=kvacc[:]), inc=sSY, k=16)
                GP.wait(sSY)
                GP.do(lambda g: g.collective_compute(
                    "AllReduce", ALU.add, replica_groups=[list(range(8))],
                    ins=[cc_in.ap().opt()], outs=[cc_out.ap().opt()]),
                    inc=sCC)
                if PHASE <= 1:
                    return nc

            tern_jobs = []
            for m, (src, dst) in enumerate(((w1c_d, ag1_in.ap()),
                                            (w2c_d, ag2_in.ap()))):
                for kc in range(2):
                    for qq in range(4):
                        for hf in range(2):
                            tern_jobs.append((m, src, dst, kc, qq * 2 + hf))
            for hc in range(8):
                for hf in range(2):
                    tern_jobs.append((2, w3c_d, ag3_in.ap(), hc, hf))
            tstate = {"ld": None, "st": None, "first": True,
                      "stores": {0: 0, 1: 0, 2: 0}}
            ag_emitted = [False]

            def emit_tern_job(job):
                m, src, dst, a_, qh = job
                src3 = src.rearrange("(cc p) k -> p cc k", p=128)
                if tstate["ld"] is not None:
                    SC.wait(sTE, tstate["ld"])
                if m < 2:
                    kc = a_
                    SC.do(lambda s, src3=src3, kc=kc, qh=qh: s.dma_start(
                        out=tld3[:, :, :],
                        in_=src3[:, qh * 8:(qh + 1) * 8,
                                 kc * 128:(kc + 1) * 128]), inc=sL2, k=16)
                else:
                    hc = a_
                    SC.do(lambda s, src3=src3, hc=hc, qh=qh: s.dma_start(
                        out=tld3[:, :, :],
                        in_=src3[:, qh * 8:(qh + 1) * 8,
                                 hc * 128:(hc + 1) * 128]), inc=sL2, k=16)
                VE.wait(sL2)
                if tstate["st"] is not None:
                    VE.wait(sGD, tstate["st"])
                if tstate["first"]:
                    VE.wait(sVE, tern_ready)
                    tstate["first"] = False
                VE.do(lambda v, m=m: v.tensor_scalar(
                    out=tpos[:], in0=tld[:], scalar1=tern[:, 3 + m:4 + m],
                    scalar2=None, op0=ALU.is_lt))
                VE.do(lambda v, m=m: v.scalar_tensor_tensor(
                    out=tbf[:], in0=tld[:], scalar=tern[:, m:m + 1],
                    in1=tpos[:], op0=ALU.is_gt, op1=ALU.subtract), inc=sVE)
                TE.wait(sVE)
                for j in range(8):
                    TE.do(lambda h, j=j: h.transpose(
                        pBt[:, j * 128:(j + 1) * 128],
                        tbf[:, j * 128:(j + 1) * 128], idb[:]),
                        inc=sTE if j == 7 else None)
                VE.wait(sTE)
                VE.do(lambda v: v.tensor_copy(
                    out=tstg3[:, :, :], in_=pBt8[:, :, :]), inc=sVE)
                TE.wait(sVE)
                tstate["ld"] = sTE.n
                GP.wait(sVE)
                dst3 = dst.rearrange("(c p) o -> p c o", p=128)
                if m < 2:
                    kc = a_
                    GP.do(lambda g, dst3=dst3, kc=kc, qh=qh: g.dma_start(
                        out=dst3[:, kc:kc + 1, qh * 1024:(qh + 1) * 1024],
                        in_=tstg1[:, :, :]), inc=sGD, k=16)
                else:
                    hc = a_
                    GP.do(lambda g, dst3=dst3, hc=hc, qh=qh: g.dma_start(
                        out=dst3[:, hc:hc + 1, qh * 1024:(qh + 1) * 1024],
                        in_=tstg1[:, :, :]), inc=sGD, k=16)
                tstate["st"] = sGD.n

            full_barrier()
            SY.wait(sCC)
            SY.do(lambda s: s.dma_start(out=kvacc[:],
                                        in_=cc_out.ap()[:, :]),
                  inc=sSY, k=16)
            kv_reduced = sSY.n

            VE.wait(sSY, kv_reduced)
            NEL = float(HD * D)
            for m in range(3):
                c0 = 12 + m * 4
                VE.do(lambda v, m=m, c0=c0: v.tensor_scalar(
                    out=statA[:, c0:c0 + 1],
                    in0=kvacc[:, 2064 + m:2065 + m],
                    scalar1=1.0 / NEL, scalar2=None, op0=ALU.mult))
                VE.do(lambda v, m=m, c0=c0: v.tensor_scalar(
                    out=tern[:, m:m + 1], in0=statA[:, c0:c0 + 1],
                    scalar1=EPS_RMS, scalar2=ATANH_HALF,
                    op0=ALU.add, op1=ALU.mult))
                VE.do(lambda v, m=m: v.tensor_scalar(
                    out=tern[:, 3 + m:4 + m], in0=tern[:, m:m + 1],
                    scalar1=-1.0, scalar2=None, op0=ALU.mult))
                VE.do(lambda v, c0=c0: v.tensor_scalar(
                    out=statA[:, c0 + 1:c0 + 2], in0=statA[:, c0:c0 + 1],
                    scalar1=1.0, scalar2=None, op0=ALU.add))
                VE.do(lambda v, c0=c0: v.tensor_scalar(
                    out=statA[:, c0 + 2:c0 + 3], in0=statA[:, c0:c0 + 1],
                    scalar1=-1.0, scalar2=1.0, op0=ALU.mult, op1=ALU.add))
                VE.do(lambda v, c0=c0: v.reciprocal(
                    out=statA[:, c0 + 3:c0 + 4],
                    in_=statA[:, c0 + 2:c0 + 3]), inc=sVE)
                VE.wait(sVE)
                VE.do(lambda v, c0=c0: v.tensor_tensor(
                    out=statA[:, c0 + 1:c0 + 2],
                    in0=statA[:, c0 + 1:c0 + 2],
                    in1=statA[:, c0 + 3:c0 + 4], op=ALU.mult), inc=sVE)
                SC.wait(sVE)
                SC.do(lambda s, c0=c0: s.activation(
                    out=statA[:, c0 + 2:c0 + 3],
                    in_=statA[:, c0 + 1:c0 + 2], func=AF.Ln), inc=sAC)
                VE.wait(sAC)
                VE.do(lambda v, m=m, c0=c0: v.tensor_scalar(
                    out=tern[:, 6 + m:7 + m], in0=statA[:, c0 + 2:c0 + 3],
                    scalar1=0.5, scalar2=None, op0=ALU.mult), inc=sVE)
            tern_ready = sVE.n
            with ExitStack() as S2:
                e2 = S2.enter_context
                attnThi = e2(nc.sbuf_tensor("attnThi", [128, NH * NT], BF16))
                attnTlo = e2(nc.sbuf_tensor("attnTlo", [128, NH * NT], BF16))
                TERN = ExitStack()
                et = TERN.enter_context
                tld = et(nc.sbuf_tensor("tld", [128, 1024], F32))
                tpos = et(nc.sbuf_tensor("tpos", [128, 1024], F32))
                tbf = et(nc.sbuf_tensor("tbf", [128, 1024], BF16))
                tstg = et(nc.sbuf_tensor("tstg", [128, 1024], BF16))
                tld3 = tld.ap().rearrange("p (cc j) -> p cc j", cc=8)
                tstg3 = tstg.ap().rearrange("p (cc j) -> p cc j", cc=8)
                tstg1 = tstg.ap().rearrange("p (c o) -> p c o", c=1)
                with ExitStack() as S2a:
                    e2a = S2a.enter_context
                    qwh = e2a(nc.sbuf_tensor("qwh", [128, KC * 512], BF16))
                    qwl = e2a(nc.sbuf_tensor("qwl", [128, KC * 512], BF16))
                    qthbuf = [e2a(nc.sbuf_tensor(f"qth{i}", [128, NT], F32))
                              for i in range(2)]
                    qm = lnsc[:, 0:NT]
                    qe = lnsc[:, NT:2 * NT]
                    at8 = e2a(nc.sbuf_tensor("at8", [128, 1024], F32))
                    recc = e2a(nc.sbuf_tensor("recc", [128, 16], F32))

                    prevQ = {"v": None}
                    agB3 = agB_out.ap().rearrange("(c p) f -> p c f", p=128)
                    qw_ld = None
                    qw_mm = None
                    qb_evac = {"v": None}

                    def emit_qmm(h, qth):
                        nonlocal qw_ld, qw_mm
                        if h % 4 == 0:
                            if h == 0:
                                SY.wait(sCC, agB_done)
                            if qw_mm is not None:
                                SY.wait(sTE, qw_mm)
                            hg = h // 4
                            for buf, colb in ((qwh, 0), (qwl, 2048)):
                                SY.do(lambda s, buf=buf, colb=colb, hg=hg:
                                      s.dma_start(
                                          out=buf.ap().rearrange(
                                              "p (c j) -> p c j", c=KC)
                                          [:, :, :],
                                          in_=agB3[:, :,
                                                   colb + hg * 512:
                                                   colb + (hg + 1) * 512]),
                                      inc=sSY, k=16)
                            qw_ld = sSY.n
                        TE.wait(sSY, qw_ld)
                        if qb_evac["v"] is not None:
                            TE.wait(sVE, qb_evac["v"])
                        hof = (h % 4) * 128
                        passes = [(xThi, qwh), (xThi, qwl), (xTlo, qwh)]
                        for half in range(2):
                            for pi, (xa, wb) in enumerate(passes):
                                for k in range(KC):
                                    TE.do(lambda h_, k=k, half=half, xa=xa,
                                          wb=wb, pi=pi, hof=hof: h_.matmul(
                                              pQ[:, half * 512:
                                                 (half + 1) * 512],
                                              wb[:, k * 512 + hof:
                                                 k * 512 + hof + 128],
                                              xa[:, k * NT + half * 512:
                                                 k * NT + (half + 1) * 512],
                                              start=(pi == 0 and k == 0),
                                              stop=(pi == 2 and k == KC - 1)),
                                          inc=sTE if (pi == 2 and
                                                      k == KC - 1) else None)
                        qw_mm = sTE.n
                        VE.wait(sTE, qw_mm)
                        VE.do(lambda v, h=h: v.tensor_scalar(
                            out=qth[:], in0=pQ[:], scalar1=bqc[:, h:h + 1],
                            scalar2=None, op0=ALU.add), inc=sVE)
                        qb_evac["v"] = sVE.n
                        VE.do(lambda v: v.tensor_scalar(
                            out=qm, in0=qth[:], scalar1=0.0, scalar2=None,
                            op0=ALU.min), inc=sVE)
                        mv = sVE.n
                        SC.wait(sVE, mv)
                        SC.do(lambda s: s.activation(out=qe, in_=qm,
                                                     func=AF.Exp), inc=sAC)
                        VE.wait(sAC)
                        VE.do(lambda v: v.scalar_tensor_tensor(
                            out=qth[:], in0=qth[:], scalar=0.0, in1=qe,
                            op0=ALU.max, op1=ALU.add), inc=sVE)
                        return sVE.n

                    def emit_qcons(h, q_ready, qth2):
                        TE.wait(sVE, q_ready)
                        TE.wait(sVE)
                        if h == 0:
                            TE.wait(sSY, kv_reduced)
                        for t in range(TT):
                            TE.do(lambda h_, h=h, t=t: h_.matmul(
                                pKV[:, t:t + 1],
                                qth2[:, t * 128:(t + 1) * 128],
                                kvacc[:, h * 129 + 128:h * 129 + 129],
                                start=True, stop=True),
                                inc=sTE if t == TT - 1 else None)
                        VE.wait(sTE)
                        VE.do(lambda v: v.tensor_scalar(
                            out=recc[:, 0:8], in0=pKV[:, 0:8],
                            scalar1=EPS_AT, scalar2=None, op0=ALU.add),
                            inc=sVE)
                        VE.wait(sVE)
                        VE.do(lambda v: v.reciprocal(out=recc[:, 8:16],
                                                     in_=recc[:, 0:8]),
                              inc=sVE)
                        rec_ready = sVE.n
                        for t in range(TT):
                            bank = pK if t < 4 else pV
                            tof = (t % 4) * 128
                            TE.do(lambda h_, h=h, t=t, bank=bank, tof=tof:
                                  h_.matmul(
                                      bank[:, tof:tof + 128],
                                      qth2[:, t * 128:(t + 1) * 128],
                                      kvacc[:, h * 129:h * 129 + 128],
                                      start=True, stop=True),
                                  inc=sTE if t == TT - 1 else None)
                        VE.wait(sTE)
                        VE.wait(sVE, rec_ready)
                        for t in range(TT):
                            bank = pK if t < 4 else pV
                            tof = (t % 4) * 128
                            VE.do(lambda v, t=t, bank=bank, tof=tof:
                                  v.tensor_scalar(
                                      out=at8[:, t * 128:(t + 1) * 128],
                                      in0=bank[:, tof:tof + 128],
                                      scalar1=recc[:, 8 + t:9 + t],
                                      scalar2=None, op0=ALU.mult),
                                  inc=sVE if t == TT - 1 else None)
                        TE.wait(sVE)
                        for b in range(2):
                            for j in range(4):
                                t = b * 4 + j
                                TE.do(lambda h_, t=t, j=j: h_.transpose(
                                    pT2[:, j * 128:(j + 1) * 128],
                                    at8[:, t * 128:(t + 1) * 128], idf[:]),
                                    inc=sTE if j == 3 else None)
                            VE.wait(sTE)
                            VE.do(lambda v, h=h, b=b: v.tensor_copy(
                                out=attnThi[:, h * NT + b * 512:
                                            h * NT + (b + 1) * 512],
                                in_=pT2[:, 0:512]), inc=sVE)
                            VE.wait(sVE)
                            VE.do(lambda v, h=h, b=b: v.tensor_tensor(
                                out=attnTlo[:, h * NT + b * 512:
                                            h * NT + (b + 1) * 512],
                                in0=pT2[:, 0:512],
                                in1=attnThi[:, h * NT + b * 512:
                                            h * NT + (b + 1) * 512],
                                op=ALU.subtract), inc=sVE)
                            TE.wait(sVE)

                    qr = {}
                    for h in range(NH + 1):
                        if h < NH:
                            qr[h] = emit_qmm(h, qthbuf[h % 2])
                        if h >= 1:
                            emit_qcons(h - 1, qr[h - 1], qthbuf[(h - 1) % 2])
                        for _ in range(4):
                            if tern_jobs:
                                emit_tern_job(tern_jobs.pop(0))
                        if not tern_jobs and not ag_emitted[0]:
                            ag_emitted[0] = True
                            GP.wait(sGD)
                            for agi, ago, nm in ((ag1_in, w1t_d, "g1"),
                                                 (ag2_in, w2t_d, "g2"),
                                                 (ag3_in, w3t_d, "g3")):
                                GP.do(lambda g, agi=agi, ago=ago:
                                      g.collective_compute(
                                          "AllGather", ALU.bypass,
                                          replica_groups=[list(range(8))],
                                          ins=[agi.ap().opt()],
                                          outs=[ago.opt()]), inc=sCC)
                    attnT_ready = sVE.n
                assert not tern_jobs, "tern jobs must drain inside P3a"
                TERN.close()

                full_barrier()
                with ExitStack() as S3:
                    e3 = S3.enter_context
                    borep = e3(nc.sbuf_tensor("borep", [128, D], F32))
                    SY.do(lambda s: s.dma_start(
                        out=borep[:],
                        in_=bo_row[0:1, :].broadcast_to([128, D])),
                        inc=sSY, k=16)
                    bo_ready = sSY.n
                    agB3b = agB_out.ap().rearrange("(c p) f -> p c f",
                                                   p=128)
                    x3r = x_d.rearrange("(tt p) d -> p tt d", p=128)
                    with ExitStack() as N1:
                        en = N1.enter_context
                        woh = en(nc.sbuf_tensor("woh", [128, KC * 512],
                                                BF16))
                        wol = en(nc.sbuf_tensor("wol", [128, KC * 512],
                                                BF16))
                        ybuf = en(nc.sbuf_tensor("ybuf", [128, 1024], F32))
                        ysq = en(nc.sbuf_tensor("ysq", [128, 512], F32))
                        xrel = en(nc.sbuf_tensor("xrel", [128, 1024], F32))
                        xrel3 = xrel.ap().rearrange("p (o j) -> p o j", o=2)
                        owg = {"mm": None, "ld": None}
                        bank_mark = {0: None, 1: None}
                        ybuf_st = {0: None, 1: None}
                        xrel_free = {0: None, 1: None}
                        xrel_ld = {}
                        ylast = {}

                        def emit_xrel(fc, t):
                            par = t % 2
                            if xrel_free[par] is not None:
                                SY.wait(sVE, xrel_free[par])
                            SY.do(lambda s, fc=fc, t=t, par=par:
                                  s.dma_start(
                                      out=xrel3[:, par:par + 1, :],
                                      in_=x3r[:, t:t + 1,
                                              fc * 512:(fc + 1) * 512]),
                                  inc=sSY, k=16)
                            xrel_ld[(fc, t)] = sSY.n

                        def emit_oload(fc):
                            if owg["mm"] is not None:
                                SY.wait(sTE, owg["mm"])
                            for buf, colb in ((woh, 4096), (wol, 6144)):
                                SY.do(lambda s, buf=buf, colb=colb, fc=fc:
                                      s.dma_start(
                                          out=buf.ap().rearrange(
                                              "p (c j) -> p c j", c=KC)
                                          [:, :, :],
                                          in_=agB3b[:, :,
                                                    colb + fc * 512:
                                                    colb + (fc + 1) * 512]),
                                      inc=sSY, k=16)
                            emit_xrel(fc, 0)
                            owg["ld"] = sSY.n

                        def emit_omm(fc, t):
                            par = t % 2
                            pb = pQ[:, par * 512:(par + 1) * 512]
                            TE.wait(sSY, owg["ld"])
                            if fc == 0 and t == 0:
                                TE.wait(sVE, attnT_ready)
                            if bank_mark[par] is not None:
                                TE.wait(sVE, bank_mark[par])
                            opasses = [(attnThi, woh), (attnThi, wol),
                                       (attnTlo, woh)]
                            for pi, (ab, wb) in enumerate(opasses):
                                for ec in range(NH):
                                    TE.do(lambda h_, ec=ec, t=t, ab=ab,
                                          wb=wb, pb=pb, pi=pi: h_.matmul(
                                              pb,
                                              ab[:, ec * NT + t * 128:
                                                 ec * NT + (t + 1) * 128],
                                              wb[:, ec * 512:
                                                 (ec + 1) * 512],
                                              start=(pi == 0 and ec == 0),
                                              stop=(pi == 2 and
                                                    ec == NH - 1)),
                                          inc=sTE if (pi == 2 and
                                                      ec == NH - 1)
                                          else None)
                            owg["mm"] = sTE.n
                            return sTE.n

                        def emit_oevac(fc, t, omm):
                            par = t % 2
                            pb = pQ[:, par * 512:(par + 1) * 512]
                            yb = ybuf[:, par * 512:(par + 1) * 512]
                            VE.wait(sTE, omm)
                            if fc == 0 and t == 0:
                                VE.wait(sSY, bo_ready)
                            VE.wait(sSY)
                            VE.do(lambda v, pb=pb, yb=yb, fc=fc:
                                  v.tensor_tensor(
                                      out=yb, in0=pb,
                                      in1=borep[:, fc * 512:
                                                (fc + 1) * 512],
                                      op=ALU.add))
                            VE.do(lambda v, yb=yb, par=par:
                                  v.tensor_tensor(
                                      out=yb, in0=yb,
                                      in1=xrel[:, par * 512:
                                               (par + 1) * 512],
                                      op=ALU.add), inc=sVE)
                            bank_mark[par] = sVE.n
                            xrel_free[par] = sVE.n
                            y_ready = sVE.n
                            VE.do(lambda v, yb=yb, fc=fc, t=t:
                                  v.tensor_reduce(
                                      out=scols[:, t * 4 + fc:
                                                t * 4 + fc + 1],
                                      in_=yb, axis=AX, op=ALU.add))
                            VE.do(lambda v, yb=yb: v.tensor_tensor(
                                out=ysq[:], in0=yb, in1=yb, op=ALU.mult))
                            VE.do(lambda v, fc=fc, t=t: v.tensor_reduce(
                                out=scols[:, 32 + t * 4 + fc:
                                          33 + t * 4 + fc],
                                in_=ysq[:], axis=AX, op=ALU.add),
                                inc=sVE)
                            SY.wait(sVE, y_ready)
                            SY.do(lambda s, fc=fc, t=t, yb=yb:
                                  s.dma_start(
                                      out=y_sp[t * 128:(t + 1) * 128,
                                               fc * 512:(fc + 1) * 512],
                                      in_=yb), inc=sSY, k=16)
                            ybuf_st[par] = sSY.n
                            if fc == 3:
                                ylast[t] = sSY.n

                        for fc in range(4):
                            emit_oload(fc)
                            for t in range(TT):
                                om = emit_omm(fc, t)
                                emit_oevac(fc, t, om)
                                if t + 1 < TT:
                                    emit_xrel(fc, t + 1)
                        stats_done = sVE.n
            while tern_jobs:
                emit_tern_job(tern_jobs.pop(0))
            if PHASE <= 2:
                return nc
            w12_stores = max(tstate["stores"][0], tstate["stores"][1])
            w3_stores = tstate["stores"][2]

        full_barrier()
        with ExitStack() as SF:
            ef = SF.enter_context
            pU = [ef(nc.psum_tensor(f"pU{i}", [128, 512], F32))
                  for i in range(6)]
            pBq = ef(nc.psum_tensor("pBq", [128, 1024], BF16))
            pBq8 = pBq.ap().rearrange("p (j x) -> p j x", j=8)

            g1rep = ef(nc.sbuf_tensor("g1rep", [128, D], F32))
            g2rep = ef(nc.sbuf_tensor("g2rep", [128, D], F32))
            g3sl = ef(nc.sbuf_tensor("g3sl", [128, 1024], F32))
            xq1T = ef(nc.sbuf_tensor("xq1T", [128, KC * NT], BF16))
            xq2T = ef(nc.sbuf_tensor("xq2T", [128, KC * NT], BF16))
            xptF = ef(nc.sbuf_tensor("xptF", [128, D], F32))
            tq = ef(nc.sbuf_tensor("tq", [128, D], F32))
            pqb = ef(nc.sbuf_tensor("pqb", [128, D], BF16))
            nqb = ef(nc.sbuf_tensor("nqb", [128, D], BF16))
            xqc = ef(nc.sbuf_tensor("xqc", [128, D], BF16))
            w1pA = ef(nc.sbuf_tensor("w1pA", [128, KC * 512], BF16))
            w1pB = ef(nc.sbuf_tensor("w1pB", [128, KC * 512], BF16))
            w2pA = ef(nc.sbuf_tensor("w2pA", [128, KC * 512], BF16))
            w2pB = ef(nc.sbuf_tensor("w2pB", [128, KC * 512], BF16))
            silu_sb = ef(nc.sbuf_tensor("silu_sb", [128, 3 * 512], F32))
            h_sb = ef(nc.sbuf_tensor("h_sb", [128, 2 * 512], F32))
            hh_scr = ef(nc.sbuf_tensor("hh_scr", [128, 512], F32))
            yldF = ef(nc.sbuf_tensor("yldF", [128, D], F32))
            hcols = ef(nc.sbuf_tensor("hcols", [128, 256], F32))

            SY.do(lambda s: s.dma_start(
                out=g1rep[:], in_=g1_d[0:1, :].broadcast_to([128, D])),
                inc=sSY, k=16)
            SY.do(lambda s: s.dma_start(
                out=g2rep[:], in_=g2_d[0:1, :].broadcast_to([128, D])),
                inc=sSY, k=16)
            for t in range(TT):
                VE.do(lambda v, t=t: v.tensor_reduce(
                    out=statN[:, t:t + 1], in_=scols[:, t * 4:(t + 1) * 4],
                    axis=AX, op=ALU.add))
                VE.do(lambda v, t=t: v.tensor_reduce(
                    out=statN[:, 8 + t:9 + t],
                    in_=scols[:, 32 + t * 4:32 + (t + 1) * 4],
                    axis=AX, op=ALU.add),
                    inc=sVE if t == TT - 1 else None)
            VE.wait(sVE)
            VE.do(lambda v: v.tensor_scalar(
                out=statN[:, 0:8], in0=statN[:, 0:8], scalar1=1.0 / D,
                scalar2=None, op0=ALU.mult))
            VE.do(lambda v: v.tensor_scalar(
                out=statN[:, 8:16], in0=statN[:, 8:16], scalar1=1.0 / D,
                scalar2=None, op0=ALU.mult))
            VE.do(lambda v: v.tensor_tensor(
                out=statN[:, 16:24], in0=statN[:, 0:8], in1=statN[:, 0:8],
                op=ALU.mult))
            VE.do(lambda v: v.tensor_tensor(
                out=statN[:, 8:16], in0=statN[:, 8:16], in1=statN[:, 16:24],
                op=ALU.subtract))
            VE.do(lambda v: v.tensor_scalar(
                out=statN[:, 8:16], in0=statN[:, 8:16], scalar1=EPS_LN,
                scalar2=None, op0=ALU.add), inc=sVE)
            SC.wait(sVE)
            SC.do(lambda s: s.activation(out=statN[:, 16:24],
                                         in_=statN[:, 8:16], func=AF.Sqrt),
                  inc=sAC)
            VE.wait(sAC)
            VE.do(lambda v: v.reciprocal(out=statN[:, 8:16],
                                         in_=statN[:, 16:24]), inc=sVE)
            VE.wait(sVE)
            xq_tile = {}
            nrm_mark = {"v": None}
            for t in range(TT):
                if nrm_mark["v"] is not None:
                    SY.wait(sVE, nrm_mark["v"])
                SY.do(lambda s, t=t: s.dma_start(
                    out=yldF[:],
                    in_=y_sp[t * 128:(t + 1) * 128, :]), inc=sSY, k=16)
                VE.wait(sSY)
                VE.do(lambda v, t=t: v.tensor_scalar(
                    out=xptF[:], in0=yldF[:],
                    scalar1=statN[:, t:t + 1],
                    scalar2=statN[:, 8 + t:9 + t],
                    op0=ALU.subtract, op1=ALU.mult), inc=sVE)
                nrm_mark["v"] = sVE.n
                SY.wait(sVE, nrm_mark["v"])
                SY.do(lambda s, t=t: s.dma_start(
                    out=xp_sp[t * 128:(t + 1) * 128, :], in_=xptF[:]),
                    inc=sSY, k=16)
                VE.wait(sAC)
                c = t * 8
                VE.do(lambda v: v.tensor_tensor(out=lnsc[:], in0=xptF[:],
                                                in1=xptF[:], op=ALU.mult))
                VE.do(lambda v: v.tensor_reduce(out=statA[:, 24:25],
                                                in_=lnsc[:], axis=AX,
                                                op=ALU.add), inc=sVE)
                VE.wait(sVE)
                VE.do(lambda v: v.tensor_scalar(
                    out=statA[:, 25:26], in0=statA[:, 24:25],
                    scalar1=1.0 / D, scalar2=EPS_RMS, op0=ALU.mult,
                    op1=ALU.add), inc=sVE)
                SC.wait(sVE)
                SC.do(lambda s: s.activation(out=statA[:, 26:27],
                                             in_=statA[:, 25:26],
                                             func=AF.Sqrt), inc=sAC)
                VE.wait(sAC)
                VE.do(lambda v, c=c: v.reciprocal(out=statF[:, c:c + 1],
                                                  in_=statA[:, 26:27]), inc=sVE)
                VE.wait(sVE)
                for gi, (grep, xqT, amcol) in enumerate(
                        ((g1rep, xq1T, 1), (g2rep, xq2T, 2))):
                    VE.do(lambda v, grep=grep: v.tensor_tensor(
                        out=tq[:], in0=xptF[:], in1=grep[:], op=ALU.mult))
                    VE.do(lambda v: v.tensor_reduce(
                        out=statA[:, 27:28], in_=tq[:], axis=AX, op=ALU.max,
                        apply_absolute_value=True), inc=sVE)
                    VE.wait(sVE)
                    VE.do(lambda v, c=c: v.tensor_tensor(
                        out=statA[:, 28:29], in0=statA[:, 27:28],
                        in1=statF[:, c:c + 1], op=ALU.mult))
                    VE.do(lambda v: v.tensor_scalar(
                        out=statA[:, 28:29], in0=statA[:, 28:29],
                        scalar1=1e-4, scalar2=None, op0=ALU.max))
                    VE.do(lambda v: v.reciprocal(out=statA[:, 29:30],
                                                 in_=statA[:, 28:29]), inc=sVE)
                    VE.wait(sVE)
                    VE.do(lambda v: v.tensor_scalar(
                        out=statA[:, 30:31], in0=statA[:, 29:30],
                        scalar1=127.0, scalar2=None, op0=ALU.mult))
                    VE.do(lambda v, c=c: v.tensor_tensor(
                        out=statA[:, 31:32], in0=statF[:, c:c + 1],
                        in1=statA[:, 30:31], op=ALU.mult))
                    VE.do(lambda v: v.tensor_scalar(
                        out=statA[:, 27:28], in0=statA[:, 28:29],
                        scalar1=1.0 / 127.0, scalar2=None, op0=ALU.mult))
                    VE.do(lambda v, c=c, gi=gi, amcol=amcol: v.tensor_tensor(
                        out=statF[:, c + amcol:c + amcol + 1],
                        in0=statA[:, 27:28], in1=tern[:, 6 + gi:7 + gi],
                        op=ALU.mult))
                    VE.do(lambda v: v.tensor_scalar(
                        out=tq[:], in0=tq[:], scalar1=statA[:, 31:32],
                        scalar2=None, op0=ALU.mult))
                    VE.do(lambda v: v.tensor_scalar(
                        out=pqb[:], in0=tq[:], scalar1=0.0, scalar2=128.0,
                        op0=ALU.max, op1=ALU.add))
                    VE.do(lambda v: v.tensor_scalar(
                        out=nqb[:], in0=tq[:], scalar1=0.0, scalar2=128.0,
                        op0=ALU.min, op1=ALU.subtract))
                    VE.do(lambda v: v.tensor_tensor(
                        out=xqc[:], in0=pqb[:], in1=nqb[:], op=ALU.add),
                        inc=sVE)
                    TE.wait(sVE)
                    for b in range(2):
                        for j in range(8):
                            cc = b * 8 + j
                            TE.do(lambda h, cc=cc, j=j: h.transpose(
                                pBq[:, j * 128:(j + 1) * 128],
                                xqc[:, cc * 128:(cc + 1) * 128], idb[:]),
                                inc=sTE if j == 7 else None)
                        VE.wait(sTE)
                        xqT3 = xqT.ap().rearrange("p (cx tx) -> p cx tx",
                                                  cx=KC)
                        VE.do(lambda v, b=b, t=t, xqT3=xqT3: v.tensor_copy(
                            out=xqT3[:, b * 8:(b + 1) * 8,
                                     t * 128:(t + 1) * 128],
                            in_=pBq8[:, :, :]), inc=sVE)
                        TE.wait(sVE)
                xq_tile[t] = sVE.n
            xq_ready = sVE.n

            w1t3 = w1t_d.rearrange("(c p) o -> p c o", p=128)
            w2t3 = w2t_d.rearrange("(c p) o -> p c o", p=128)
            w1pv = [w1pA.ap().rearrange("p (c o) -> p c o", c=KC),
                    w1pB.ap().rearrange("p (c o) -> p c o", c=KC)]
            w2pv = [w2pA.ap().rearrange("p (c o) -> p c o", c=KC),
                    w2pB.ap().rearrange("p (c o) -> p c o", c=KC)]
            w1pf = [w1pA, w1pB]
            w2pf = [w2pA, w2pB]
            GP.wait(sCC)
            panel_mark = {0: None, 1: None}
            panel_ld = {0: None, 1: None}
            hprev0 = {0: None, 1: None, 2: None}
            hprev1 = {0: None, 1: None, 2: None}
            for oc in range(KC):
                opar = oc % 2
                if panel_mark[opar] is not None:
                    GP.wait(sTE, panel_mark[opar])
                GP.do(lambda g, oc=oc, opar=opar: g.dma_start(
                    out=w1pv[opar][:, :, :],
                    in_=w1t3[:, :, oc * 512:(oc + 1) * 512]), inc=sGD, k=16)
                GP.do(lambda g, oc=oc, opar=opar: g.dma_start(
                    out=w2pv[opar][:, :, :],
                    in_=w2t3[:, :, oc * 512:(oc + 1) * 512]), inc=sGD, k=16)
                panel_ld[opar] = sGD.n
                SY.wait(sVE)
                SY.do(lambda s, oc=oc, opar=opar: s.dma_start(
                    out=g3sl[:, opar * 512:(opar + 1) * 512],
                    in_=g3_d[0:1, oc * 512:(oc + 1) * 512]
                    .broadcast_to([128, 512])), inc=sSY, k=16)
                g3_ld = sSY.n
                TE.wait(sGD, panel_ld[opar])
                for t in range(TT):
                    if oc == 0:
                        TE.wait(sVE, xq_tile[t])
                    tpar = t % 3
                    bU1 = pU[2 * tpar]
                    bU2 = pU[2 * tpar + 1]
                    if hprev0[tpar] is not None:
                        TE.wait(sAC, hprev0[tpar])
                    for k in range(KC):
                        TE.do(lambda h, k=k, t=t, opar=opar, bU1=bU1: h.matmul(
                            bU1[:],
                            xq1T[:, k * NT + t * 128:k * NT + (t + 1) * 128],
                            w1pf[opar][:, k * 512:(k + 1) * 512],
                            start=(k == 0), stop=(k == KC - 1)),
                            inc=sTE if k == KC - 1 else None)
                    u1mm = sTE.n
                    if hprev1[tpar] is not None:
                        TE.wait(sVE, hprev1[tpar])
                    for k in range(KC):
                        TE.do(lambda h, k=k, t=t, opar=opar, bU2=bU2: h.matmul(
                            bU2[:],
                            xq2T[:, k * NT + t * 128:k * NT + (t + 1) * 128],
                            w2pf[opar][:, k * 512:(k + 1) * 512],
                            start=(k == 0), stop=(k == KC - 1)),
                            inc=sTE if k == KC - 1 else None)
                    u2mm = sTE.n
                    SC.wait(sTE, u1mm)
                    if hprev1[tpar] is not None:
                        SC.wait(sVE, hprev1[tpar])
                    c = t * 8
                    SC.do(lambda s, c=c, tpar=tpar, bU1=bU1: s.activation(
                        out=silu_sb[:, tpar * 512:(tpar + 1) * 512],
                        in_=bU1[:], func=AF.Silu,
                        scale=statF[:, c + 1:c + 2]), inc=sAC)
                    hprev0[tpar] = sAC.n
                    par = (oc * TT + t) % 2
                    VE.wait(sAC, hprev0[tpar])
                    VE.wait(sTE, u2mm)
                    VE.wait(sSY)
                    VE.do(lambda v, c=c, par=par, tpar=tpar, bU2=bU2:
                          v.scalar_tensor_tensor(
                              out=h_sb[:, par * 512:(par + 1) * 512],
                              in0=bU2[:], scalar=statF[:, c + 2:c + 3],
                              in1=silu_sb[:, tpar * 512:(tpar + 1) * 512],
                              op0=ALU.mult, op1=ALU.mult), inc=sVE)
                    hprev1[tpar] = sVE.n
                    VE.do(lambda v, par=par: v.tensor_tensor(
                        out=hh_scr[:], in0=h_sb[:, par * 512:(par + 1) * 512],
                        in1=h_sb[:, par * 512:(par + 1) * 512], op=ALU.mult))
                    VE.do(lambda v, t=t, oc=oc: v.tensor_reduce(
                        out=hcols[:, t * 16 + oc:t * 16 + oc + 1],
                        in_=hh_scr[:], axis=AX, op=ALU.add))
                    VE.do(lambda v, par=par, opar=opar: v.tensor_tensor(
                        out=hh_scr[:], in0=h_sb[:, par * 512:(par + 1) * 512],
                        in1=g3sl[:, opar * 512:(opar + 1) * 512],
                        op=ALU.mult))
                    VE.do(lambda v, t=t, oc=oc: v.tensor_reduce(
                        out=hcols[:, 128 + t * 16 + oc:129 + t * 16 + oc],
                        in_=hh_scr[:], axis=AX, op=ALU.max,
                        apply_absolute_value=True), inc=sVE)
                    SY.wait(sVE)
                    SY.do(lambda s, t=t, oc=oc, par=par: s.dma_start(
                        out=h_sp[t * 128:(t + 1) * 128,
                                 oc * 512:(oc + 1) * 512],
                        in_=h_sb[:, par * 512:(par + 1) * 512]),
                        inc=sSY, k=16)
                panel_mark[opar] = sTE.n
            VE.wait(sVE)
            for t in range(TT):
                VE.do(lambda v, t=t: v.tensor_reduce(
                    out=hstat[:, 2 * t:2 * t + 1],
                    in_=hcols[:, t * 16:(t + 1) * 16], axis=AX, op=ALU.add))
                VE.do(lambda v, t=t: v.tensor_reduce(
                    out=hstat[:, 2 * t + 1:2 * t + 2],
                    in_=hcols[:, 128 + t * 16:128 + (t + 1) * 16], axis=AX,
                    op=ALU.max), inc=sVE if t == TT - 1 else None)
            hstat_ready = sVE.n
            h_all_stored = sSY.n
            if PHASE <= 3:
                return nc

        full_barrier()
        with ExitStack() as SG:
            eg = SG.enter_context
            pBig = eg(nc.psum_tensor("pBig", [128, 2048], F32))
            pBr = eg(nc.psum_tensor("pBr", [128, 1024], BF16))
            pBr8 = pBr.ap().rearrange("p (j x) -> p j x", j=8)
            g3repF = eg(nc.sbuf_tensor("g3repF", [128, HD], F32))
            hld = eg(nc.sbuf_tensor("hld", [128, 2 * 2048], F32))
            y3 = eg(nc.sbuf_tensor("y3", [128, 2048], F32))
            p3b = eg(nc.sbuf_tensor("p3b", [128, 2048], BF16))
            n3b = eg(nc.sbuf_tensor("n3b", [128, 2048], BF16))
            xq3c = eg(nc.sbuf_tensor("xq3c", [128, 2048], BF16))
            xq3T = eg(nc.sbuf_tensor("xq3T", [128, 4 * HC * 128], BF16))
            w3pA = eg(nc.sbuf_tensor("w3pA", [128, 4 * 1024], BF16))
            w3pB = eg(nc.sbuf_tensor("w3pB", [128, 4 * 1024], BF16))
            xptG = eg(nc.sbuf_tensor("xptG", [128, D], F32))
            yt2 = eg(nc.sbuf_tensor("yt2", [128, 2 * D], F32))
            outsb = eg(nc.sbuf_tensor("outsb", [128, D], F32))
            w3pv = [w3pA.ap().rearrange("p (c d) -> p c d", c=4),
                    w3pB.ap().rearrange("p (c d) -> p c d", c=4)]
            w3pf = [w3pA, w3pB]
            w3t3 = w3t_d.rearrange("(c p) d -> p c d", p=128)

            SY.do(lambda s: s.dma_start(
                out=g3repF[:], in_=g3_d[0:1, :].broadcast_to([128, HD])),
                inc=sSY, k=16)
            g3f_ready = sSY.n
            SY.wait(sSY, h_all_stored)
            VE.wait(sVE, hstat_ready)

            def emit_scales(t):
                c = t * 8
                VE.wait(sAC)
                VE.do(lambda v, t=t: v.tensor_scalar(
                    out=statA[:, 24:25], in0=hstat[:, 2 * t:2 * t + 1],
                    scalar1=1.0 / HD, scalar2=EPS_RMS, op0=ALU.mult,
                    op1=ALU.add), inc=sVE)
                SC.wait(sVE)
                SC.do(lambda s: s.activation(out=statA[:, 25:26],
                                             in_=statA[:, 24:25],
                                             func=AF.Sqrt), inc=sAC)
                VE.wait(sAC)
                VE.do(lambda v: v.reciprocal(out=statA[:, 26:27],
                                             in_=statA[:, 25:26]), inc=sVE)
                VE.wait(sVE)
                VE.do(lambda v, t=t: v.tensor_tensor(
                    out=statA[:, 27:28], in0=hstat[:, 2 * t + 1:2 * t + 2],
                    in1=statA[:, 26:27], op=ALU.mult))
                VE.do(lambda v: v.tensor_scalar(
                    out=statA[:, 27:28], in0=statA[:, 27:28], scalar1=1e-4,
                    scalar2=None, op0=ALU.max))
                VE.do(lambda v: v.reciprocal(out=statA[:, 28:29],
                                             in_=statA[:, 27:28]), inc=sVE)
                VE.wait(sVE)
                VE.do(lambda v: v.tensor_scalar(
                    out=statA[:, 28:29], in0=statA[:, 28:29], scalar1=127.0,
                    scalar2=None, op0=ALU.mult))
                VE.do(lambda v, c=c: v.tensor_tensor(
                    out=statF[:, c + 5:c + 6], in0=statA[:, 26:27],
                    in1=statA[:, 28:29], op=ALU.mult))
                VE.do(lambda v: v.tensor_scalar(
                    out=statA[:, 27:28], in0=statA[:, 27:28],
                    scalar1=1.0 / 127.0, scalar2=None, op0=ALU.mult))
                VE.do(lambda v, c=c: v.tensor_tensor(
                    out=statF[:, c + 6:c + 7], in0=statA[:, 27:28],
                    in1=tern[:, 8:9], op=ALU.mult), inc=sVE)

            mm3_mark = {0: None, 1: None}
            xq3T3v = xq3T.ap().rearrange("p (s c j) -> p s c j", s=4, j=128)

            def emit_quant(t):
                c = t * 8
                slot = t % 4
                for ch in range(4):
                    hpar = ch % 2
                    SY.wait(sVE)
                    SY.do(lambda s, t=t, ch=ch, hpar=hpar: s.dma_start(
                        out=hld[:, hpar * 2048:(hpar + 1) * 2048],
                        in_=h_sp[t * 128:(t + 1) * 128,
                                 ch * 2048:(ch + 1) * 2048]), inc=sSY, k=16)
                    VE.wait(sSY)
                    if t == 0 and ch == 0:
                        VE.wait(sSY, g3f_ready)
                    VE.do(lambda v, ch=ch, hpar=hpar: v.tensor_tensor(
                        out=y3[:], in0=hld[:, hpar * 2048:(hpar + 1) * 2048],
                        in1=g3repF[:, ch * 2048:(ch + 1) * 2048],
                        op=ALU.mult))
                    VE.do(lambda v, c=c: v.tensor_scalar(
                        out=y3[:], in0=y3[:], scalar1=statF[:, c + 5:c + 6],
                        scalar2=None, op0=ALU.mult))
                    VE.do(lambda v: v.tensor_scalar(
                        out=p3b[:], in0=y3[:], scalar1=0.0, scalar2=128.0,
                        op0=ALU.max, op1=ALU.add))
                    VE.do(lambda v: v.tensor_scalar(
                        out=n3b[:], in0=y3[:], scalar1=0.0, scalar2=128.0,
                        op0=ALU.min, op1=ALU.subtract))
                    VE.do(lambda v: v.tensor_tensor(
                        out=xq3c[:], in0=p3b[:], in1=n3b[:], op=ALU.add),
                        inc=sVE)
                    TE.wait(sVE)
                    for b in range(2):
                        for j in range(8):
                            cc = b * 8 + j
                            TE.do(lambda h, cc=cc, j=j: h.transpose(
                                pBr[:, j * 128:(j + 1) * 128],
                                xq3c[:, cc * 128:(cc + 1) * 128], idb[:]),
                                inc=sTE if j == 7 else None)
                        VE.wait(sTE)
                        if mm3_mark[(t // 2) % 2] is not None:
                            VE.wait(sTE, mm3_mark[(t // 2) % 2])
                        VE.do(lambda v, ch=ch, b=b, slot=slot: v.tensor_copy(
                            out=xq3T3v[:, slot,
                                       ch * 16 + b * 8:ch * 16 + (b + 1) * 8,
                                       :], in_=pBr8[:, :, :]), inc=sVE)
                        TE.wait(sVE)

            w3ld = {0: None, 1: None}
            w3mm = {0: None, 1: None}
            evac_done = None
            emit_scales(0)
            emit_quant(0)
            emit_scales(1)
            emit_quant(1)
            for g in range(4):
                gpar = g % 2
                t0, t1 = 2 * g, 2 * g + 1
                xq_ready3 = sVE.n
                for dh in range(2):
                    for hb in range(16):
                        wpar = hb % 2
                        if w3mm[wpar] is not None:
                            GP.wait(sTE, w3mm[wpar])
                        GP.do(lambda gg, hb=hb, dh=dh, wpar=wpar:
                              gg.dma_start(
                                  out=w3pv[wpar][:, :, :],
                                  in_=w3t3[:, hb * 4:(hb + 1) * 4,
                                           dh * 1024:(dh + 1) * 1024]),
                              inc=sGD, k=16)
                        w3ld[wpar] = sGD.n
                        TE.wait(sGD, w3ld[wpar])
                        if hb == 0:
                            if dh == 0:
                                TE.wait(sVE, xq_ready3)
                            if evac_done is not None:
                                TE.wait(sVE, evac_done)
                        for hcl in range(4):
                            hc = hb * 4 + hcl
                            for ti, tt_ in enumerate((t0, t1)):
                                for dc in range(2):
                                    TE.do(lambda h, hc=hc, hcl=hcl, dc=dc,
                                          ti=ti, tt_=tt_, wpar=wpar:
                                          h.matmul(
                                              pBig[:, ti * 1024 + dc * 512:
                                                   ti * 1024 +
                                                   (dc + 1) * 512],
                                              xq3T[:, (tt_ % 4) * 8192 +
                                                   hc * 128:
                                                   (tt_ % 4) * 8192 +
                                                   (hc + 1) * 128],
                                              w3pf[wpar][:, hcl * 1024 +
                                                         dc * 512:
                                                         hcl * 1024 +
                                                         (dc + 1) * 512],
                                              start=(hc == 0),
                                              stop=(hc == HC - 1)),
                                          inc=sTE if (hcl == 3 and ti == 1
                                                      and dc == 1) else None)
                        w3mm[wpar] = sTE.n
                    dhmm = sTE.n
                    VE.wait(sTE, dhmm)
                    for ti, tt_ in enumerate((t0, t1)):
                        c = tt_ * 8
                        VE.do(lambda v, c=c, ti=ti, dh=dh: v.tensor_scalar(
                            out=yt2[:, ti * 2048 + dh * 1024:
                                    ti * 2048 + (dh + 1) * 1024],
                            in0=pBig[:, ti * 1024:(ti + 1) * 1024],
                            scalar1=statF[:, c + 6:c + 7], scalar2=None,
                            op0=ALU.mult), inc=sVE)
                    evac_done = sVE.n
                mm3_mark[gpar] = sTE.n
                if g + 1 < 4:
                    emit_scales(2 * g + 2)
                    emit_quant(2 * g + 2)
                    emit_scales(2 * g + 3)
                    emit_quant(2 * g + 3)
                for ti, tt_ in enumerate((t0, t1)):
                    SY.wait(sVE)
                    SY.do(lambda s, tt_=tt_: s.dma_start(
                        out=xptG[:], in_=xp_sp[tt_ * 128:(tt_ + 1) * 128, :]),
                        inc=sSY, k=16)
                    VE.wait(sSY)
                    VE.do(lambda v, ti=ti: v.tensor_tensor(
                        out=yt2[:, ti * 2048:(ti + 1) * 2048],
                        in0=yt2[:, ti * 2048:(ti + 1) * 2048],
                        in1=xptG[:], op=ALU.add), inc=sVE)
                    emit_ln(yt2[:, ti * 2048:(ti + 1) * 2048], outsb[:])
                    SY.wait(sVE)
                    SY.do(lambda s, tt_=tt_: s.dma_start(
                        out=out_d[tt_ * 128:(tt_ + 1) * 128, :],
                        in_=outsb[:]), inc=sST, k=16)
                    VE.wait(sST)
            SY.wait(sST, sST.n)

    return nc


_NC_CACHE = {}


def kernel(L0_attn_x, L0_attn_wq, L0_attn_bq, L0_attn_wk, L0_attn_bk,
           L0_attn_wv, L0_attn_bv, L0_attn_wo, L0_attn_bo,
           L0_ffn_w1, L0_ffn_g1, L0_ffn_w2, L0_ffn_g2, L0_ffn_w3, L0_ffn_g3):
    if "nc" not in _NC_CACHE:
        _NC_CACHE["nc"] = build_program()
    nc = _NC_CACHE["nc"]

    def f32(a):
        return np.ascontiguousarray(np.asarray(a, dtype=np.float32))

    x = f32(L0_attn_x).reshape(8192, 2048)
    wq, wk, wv, wo = f32(L0_attn_wq), f32(L0_attn_wk), f32(L0_attn_wv), f32(L0_attn_wo)
    w1, w2, w3 = f32(L0_ffn_w1), f32(L0_ffn_w2), f32(L0_ffn_w3)
    in_maps = []
    for c in range(8):
        in_maps.append({
            "x": np.ascontiguousarray(x[c * NT:(c + 1) * NT]),
            "wqc": np.ascontiguousarray(wq[:, c * 256:(c + 1) * 256]),
            "wkc": np.ascontiguousarray(wk[:, c * 256:(c + 1) * 256]),
            "wvc": np.ascontiguousarray(wv[:, c * 256:(c + 1) * 256]),
            "woc": np.ascontiguousarray(wo[:, c * 256:(c + 1) * 256]),
            "bq_r": f32(L0_attn_bq).reshape(16, 128),
            "bo_r": f32(L0_attn_bo).reshape(16, 128),
            "bo_row": f32(L0_attn_bo).reshape(1, D),
            "bk": f32(L0_attn_bk).reshape(1, D),
            "bv": f32(L0_attn_bv).reshape(1, D),
            "g1": f32(L0_ffn_g1).reshape(1, D),
            "g2": f32(L0_ffn_g2).reshape(1, D),
            "g3": f32(L0_ffn_g3).reshape(1, HD),
            "w1c": np.ascontiguousarray(w1[:, c * 256:(c + 1) * 256]),
            "w2c": np.ascontiguousarray(w2[:, c * 256:(c + 1) * 256]),
            "w3c": np.ascontiguousarray(w3[:, c * 1024:(c + 1) * 1024]),
            "w1s": np.ascontiguousarray(w1[c * (HD // 8):(c + 1) * (HD // 8)]),
            "w2s": np.ascontiguousarray(w2[c * (HD // 8):(c + 1) * (HD // 8)]),
            "w3s": np.ascontiguousarray(w3[c * (D // 8):(c + 1) * (D // 8)]),
        })
    res = run_bass_kernel_spmd(nc, in_maps, core_ids=list(range(8)))
    out = np.concatenate([np.asarray(r["out"]) for r in res.results], axis=0)
    return np.ascontiguousarray(out.reshape(1, 8192, 2048).astype(np.float32))


# version 14 (final)
# speedup vs baseline: 2.1889x; 1.0417x over previous
import os
import numpy as np
from contextlib import ExitStack

import concourse.bass as bass
import concourse.mybir as mybir
from concourse.bass_utils import run_bass_kernel_spmd

F32 = mybir.dt.float32
BF16 = mybir.dt.bfloat16
I8 = mybir.dt.int8
AF = mybir.ActivationFunctionType
ALU = mybir.AluOpType
AX = mybir.AxisListType.X

NT = 1024
TT = 8
D = 2048
KC = 16
NH = 16
HD = 8192
HC = 64
EPS_LN = 1e-5
EPS_RMS = 1e-8
EPS_AT = 1e-8
ATANH_HALF = 0.5493061443340548


class Sem:
    def __init__(self, handle):
        self.h = handle
        self.n = 0


class Eng:
    def __init__(self, section):
        self.section = section

    def do(self, fn, inc=None, k=1):
        if inc is None:
            self.section(fn)
            return None
        self.section(lambda h, fn=fn, s=inc.h, k=k: fn(h).then_inc(s, k))
        inc.n += k
        return inc.n

    def wait(self, sem, val=None):
        v = sem.n if val is None else val
        if v > 0:
            self.section(lambda h, s=sem.h, v=v: h.wait_ge(s, v))


def build_program():
    PHASE = int(os.environ.get("KPHASE", "99"))
    nc = bass.Bass("TRN2", target_bir_lowering=False,
                   detect_race_conditions=False)

    x_d = nc.dram_tensor("x", [NT, D], F32, kind="ExternalInput").ap()
    w_attc = {m: nc.dram_tensor(m + "c", [D, 256], F32, kind="ExternalInput").ap()
              for m in ("wq", "wk", "wv", "wo")}
    bqr_d = nc.dram_tensor("bq_r", [16, 128], F32, kind="ExternalInput").ap()
    bor_d = nc.dram_tensor("bo_r", [16, 128], F32, kind="ExternalInput").ap()
    bo_row = nc.dram_tensor("bo_row", [1, D], F32, kind="ExternalInput").ap()
    bk_d = nc.dram_tensor("bk", [1, D], F32, kind="ExternalInput").ap()
    bv_d = nc.dram_tensor("bv", [1, D], F32, kind="ExternalInput").ap()
    g1_d = nc.dram_tensor("g1", [1, D], F32, kind="ExternalInput").ap()
    g2_d = nc.dram_tensor("g2", [1, D], F32, kind="ExternalInput").ap()
    g3_d = nc.dram_tensor("g3", [1, HD], F32, kind="ExternalInput").ap()
    w1c_d = nc.dram_tensor("w1c", [HD, 256], F32, kind="ExternalInput").ap()
    w2c_d = nc.dram_tensor("w2c", [HD, 256], F32, kind="ExternalInput").ap()
    w3c_d = nc.dram_tensor("w3c", [D, 1024], F32, kind="ExternalInput").ap()
    w1s_d = nc.dram_tensor("w1s", [HD // 8, D], F32, kind="ExternalInput").ap()
    w2s_d = nc.dram_tensor("w2s", [HD // 8, D], F32, kind="ExternalInput").ap()
    w3s_d = nc.dram_tensor("w3s", [D // 8, HD], F32, kind="ExternalInput").ap()
    out_d = nc.dram_tensor("out", [NT, D], F32, kind="ExternalOutput").ap()

    cc_in = nc.dram_tensor("cc_in", [128, 2068], F32)
    cc_out = nc.dram_tensor("cc_out", [128, 2068], F32)
    xp_sp = nc.dram_tensor("xp_sp", [NT, D], F32).ap()
    y_sp = nc.dram_tensor("y_sp", [NT, D], F32).ap()
    h_sp = nc.dram_tensor("h_sp", [NT, HD], F32).ap()
    agA_in = nc.dram_tensor("agA_in", [256, 8192], BF16)
    agA_out = nc.dram_tensor("agA_out", [D, 8192], BF16)
    agB_in = nc.dram_tensor("agB_in", [256, 8192], BF16)
    agB_out = nc.dram_tensor("agB_out", [D, 8192], BF16)
    ag1_in = nc.dram_tensor("ag1_in", [D // 8, HD], I8)
    ag2_in = nc.dram_tensor("ag2_in", [D // 8, HD], I8)
    ag3_in = nc.dram_tensor("ag3_in", [HD // 8, D], I8)
    w1t_d = nc.dram_tensor("w1t", [D, HD], I8).ap()
    w2t_d = nc.dram_tensor("w2t", [D, HD], I8).ap()
    w3t_d = nc.dram_tensor("w3t", [HD, D], I8).ap()

    with ExitStack() as outer:
        eo = outer.enter_context
        sSY = Sem(eo(nc.semaphore("sSY")))
        sL2 = Sem(eo(nc.semaphore("sL2")))
        sGD = Sem(eo(nc.semaphore("sGD")))
        sTE = Sem(eo(nc.semaphore("sTE")))
        sVE = Sem(eo(nc.semaphore("sVE")))
        sAC = Sem(eo(nc.semaphore("sAC")))
        sGC = Sem(eo(nc.semaphore("sGC")))
        sCC = Sem(eo(nc.semaphore("sCC")))
        sST = Sem(eo(nc.semaphore("sST")))

        idf = eo(nc.sbuf_tensor("idf", [128, 128], F32))
        idb = eo(nc.sbuf_tensor("idb", [128, 128], BF16))
        onesf = eo(nc.sbuf_tensor("onesf", [128, 128], F32))
        onec = eo(nc.sbuf_tensor("onec", [128, 1], F32))
        colix = eo(nc.sbuf_tensor("colix", [128, 128], F32))
        rowix = eo(nc.sbuf_tensor("rowix", [128, 1], F32))
        bqc = eo(nc.sbuf_tensor("bqc", [128, 16], F32))
        boc = eo(nc.sbuf_tensor("boc", [128, 16], F32))
        wsumc = eo(nc.sbuf_tensor("wsumc", [128, 4], F32))
        tern = eo(nc.sbuf_tensor("tern", [128, 12], F32))
        statA = eo(nc.sbuf_tensor("statA", [128, 32], F32))
        statF = eo(nc.sbuf_tensor("statF", [128, TT * 8], F32))
        hstat = eo(nc.sbuf_tensor("hstat", [128, TT * 2], F32))
        scols = eo(nc.sbuf_tensor("scols", [128, 64], F32))
        statN = eo(nc.sbuf_tensor("statN", [128, 24], F32))
        lnsc = eo(nc.sbuf_tensor("lnsc", [128, D], F32))

        blk = eo(nc.Block())
        SY = Eng(blk.sync)
        SC = Eng(blk.scalar)
        VE = Eng(blk.vector)
        TE = Eng(blk.tensor)
        GP = Eng(blk.gpsimd)

        GP.do(lambda g: g.iota(colix[:], pattern=[[1, 128]], base=0,
                               channel_multiplier=0,
                               allow_small_or_imprecise_dtypes=True))
        GP.do(lambda g: g.iota(rowix[:], pattern=[[0, 1]], base=0,
                               channel_multiplier=1,
                               allow_small_or_imprecise_dtypes=True), inc=sGC)
        VE.wait(sGC)
        VE.do(lambda v: v.tensor_scalar(out=idf[:], in0=colix[:],
                                        scalar1=rowix[:, :], scalar2=None,
                                        op0=ALU.is_equal))
        VE.do(lambda v: v.tensor_copy(out=idb[:], in_=idf[:]))
        VE.do(lambda v: v.memset(onesf[:], 1.0))
        VE.do(lambda v: v.memset(onec[:], 1.0))
        VE.do(lambda v: v.memset(wsumc[:], 0.0), inc=sVE)
        consts_ready = sVE.n

        def emit_ln(ytile, xout):
            VE.wait(sAC)
            VE.do(lambda v: v.tensor_reduce(out=statA[:, 24:25], in_=ytile,
                                            axis=AX, op=ALU.add), inc=sVE)
            VE.wait(sVE)
            VE.do(lambda v: v.tensor_tensor(out=lnsc[:, 0:D], in0=ytile,
                                            in1=ytile, op=ALU.mult))
            VE.do(lambda v: v.tensor_reduce(out=statA[:, 25:26],
                                            in_=lnsc[:, 0:D], axis=AX,
                                            op=ALU.add), inc=sVE)
            VE.wait(sVE)
            VE.do(lambda v: v.tensor_scalar(out=statA[:, 26:27],
                                            in0=statA[:, 24:25],
                                            scalar1=1.0 / D, scalar2=None,
                                            op0=ALU.mult))
            VE.do(lambda v: v.tensor_tensor(out=statA[:, 27:28],
                                            in0=statA[:, 26:27],
                                            in1=statA[:, 26:27], op=ALU.mult))
            VE.do(lambda v: v.tensor_scalar(out=statA[:, 28:29],
                                            in0=statA[:, 25:26],
                                            scalar1=1.0 / D, scalar2=None,
                                            op0=ALU.mult))
            VE.do(lambda v: v.tensor_tensor(out=statA[:, 29:30],
                                            in0=statA[:, 28:29],
                                            in1=statA[:, 27:28],
                                            op=ALU.subtract))
            VE.do(lambda v: v.tensor_scalar(out=statA[:, 29:30],
                                            in0=statA[:, 29:30],
                                            scalar1=EPS_LN, scalar2=None,
                                            op0=ALU.add), inc=sVE)
            SC.wait(sVE)
            SC.do(lambda s: s.activation(out=statA[:, 30:31],
                                         in_=statA[:, 29:30], func=AF.Sqrt),
                  inc=sAC)
            VE.wait(sAC)
            VE.do(lambda v: v.reciprocal(out=statA[:, 31:32],
                                         in_=statA[:, 30:31]), inc=sVE)
            VE.wait(sVE)
            VE.do(lambda v: v.tensor_scalar(out=xout, in0=ytile,
                                            scalar1=statA[:, 26:27],
                                            scalar2=statA[:, 31:32],
                                            op0=ALU.subtract, op1=ALU.mult),
                  inc=sVE)

        def full_barrier():
            sems = (sSY, sL2, sGD, sTE, sVE, sAC, sGC, sST)
            for eng in (SY, SC, VE, TE, GP):
                for sm in sems:
                    eng.wait(sm)

        with ExitStack() as SA:
            ea = SA.enter_context
            xThi = ea(nc.sbuf_tensor("xThi", [128, KC * NT], BF16))
            xTlo = ea(nc.sbuf_tensor("xTlo", [128, KC * NT], BF16))
            kvacc = ea(nc.sbuf_tensor("kvacc", [128, 2068], F32))
            XSld = ExitStack()
            xld = XSld.enter_context(nc.sbuf_tensor("xld", [128, D], F32))
            VE.do(lambda v: v.memset(kvacc[:], 0.0), inc=sVE)
            pQ = ea(nc.psum_tensor("pQ", [128, 1024], F32))
            pK = ea(nc.psum_tensor("pK", [128, 512], F32))
            pV = ea(nc.psum_tensor("pV", [128, 512], F32))
            pKV = ea(nc.psum_tensor("pKV", [128, 512], F32))
            pT = ea(nc.psum_tensor("pT", [128, 512], F32))
            pBt = ea(nc.psum_tensor("pBt", [128, 1024], BF16))
            pT2 = ea(nc.psum_tensor("pT2", [128, 512], F32))

            xThi3 = xThi.ap().rearrange("p (c t) -> p c t", c=KC)
            xTlo3 = xTlo.ap().rearrange("p (c t) -> p c t", c=KC)
            pT4 = pT.ap().rearrange("p (j x) -> p j x", j=4)
            pBt8 = pBt.ap().rearrange("p (j x) -> p j x", j=8)

            def transpose16_f32(src_tile, dst3, dst_sel):
                for b in range(4):
                    for j in range(4):
                        c = b * 4 + j
                        TE.do(lambda h, c=c, j=j, s=src_tile: h.transpose(
                            pT[:, j * 128:(j + 1) * 128],
                            s[:, c * 128:(c + 1) * 128], idf[:]),
                            inc=sTE if j == 3 else None)
                    VE.wait(sTE)
                    d1, d2 = dst_sel(b)
                    VE.do(lambda v, d1=d1, d2=d2, dst3=dst3: v.tensor_copy(
                        out=dst3[:, d1, d2], in_=pT4[:, :, :]), inc=sVE)
                    TE.wait(sVE)

            with ExitStack() as PW:
                ew = PW.enter_context
                wcl = ew(nc.sbuf_tensor("wcl", [128, 4096], F32))
                whi = ew(nc.sbuf_tensor("whi", [128, 4096], BF16))
                wlo = ew(nc.sbuf_tensor("wlo", [128, 4096], BF16))
                wstg = ew(nc.sbuf_tensor("wstg", [128, 2048], BF16))
                wcl3 = wcl.ap().rearrange("p (cc j) -> p cc j", cc=16)
                pBt8w = pBt.ap().rearrange("p (j x) -> p j x", j=8)
                prep_cfg = [("wk", agA_in, 0), ("wv", agA_in, 4096),
                            ("wq", agB_in, 0), ("wo", agB_in, 4096)]
                st_mark = {"agA": 0, "agB": 0}
                for mi, (mn, agd, colb) in enumerate(prep_cfg):
                    src3 = w_attc[mn].rearrange("(cc p) j -> p cc j", p=128)
                    SY.wait(sTE)
                    SY.wait(sVE)
                    SY.do(lambda s, src3=src3: s.dma_start(
                        out=wcl3[:, :, :], in_=src3[:, :, :]), inc=sSY, k=16)
                    VE.wait(sSY)
                    VE.do(lambda v: v.tensor_copy(out=whi[:], in_=wcl[:]))
                    VE.wait(sVE)
                    VE.do(lambda v: v.tensor_tensor(
                        out=wlo[:], in0=wcl[:], in1=whi[:],
                        op=ALU.subtract), inc=sVE)
                    for half, hbuf in ((0, whi), (1, wlo)):
                        for kb in range(2):
                            TE.wait(sVE)
                            for bb in range(2):
                                for j in range(8):
                                    cc = bb * 8 + j
                                    TE.do(lambda h, cc=cc, j=j, kb=kb,
                                          hbuf=hbuf: h.transpose(
                                              pBt[:, j * 128:(j + 1) * 128],
                                              hbuf[:, cc * 256 + kb * 128:
                                                   cc * 256 + kb * 128 + 128],
                                              idb[:]),
                                          inc=sTE if j == 7 else None)
                                VE.wait(sTE)
                                VE.do(lambda v, bb=bb: v.tensor_copy(
                                    out=wstg.ap().rearrange(
                                        "p (cc j) -> p cc j", cc=16)
                                    [:, bb * 8:(bb + 1) * 8, :],
                                    in_=pBt8w[:, :, :]), inc=sVE)
                                TE.wait(sVE)
                            SY.wait(sVE)
                            SY.do(lambda s, agd=agd, colb=colb, half=half,
                                  kb=kb: s.dma_start(
                                      out=agd.ap()[kb * 128:(kb + 1) * 128,
                                                   colb + half * 2048:
                                                   colb + half * 2048 + 2048],
                                      in_=wstg[:]), inc=sSY, k=16)
                            VE.wait(sSY)
                agw_stores = sSY.n
                GP.wait(sSY, agw_stores)
                GP.do(lambda g: g.collective_compute(
                    "AllGather", ALU.bypass, replica_groups=[list(range(8))],
                    ins=[agA_in.ap().opt()], outs=[agA_out.ap().opt()]),
                    inc=sCC)
                agA_done = 1
                agB_done = 2

            for t in range(TT):
                SY.wait(sTE)
                SY.do(lambda s, t=t: s.dma_start(
                    out=xld[:], in_=x_d[t * 128:(t + 1) * 128, :]),
                    inc=sSY, k=16)
                TE.wait(sSY)
                if t == 0:
                    TE.wait(sVE, consts_ready)
                for b in range(4):
                    for j in range(4):
                        c = b * 4 + j
                        TE.do(lambda h, c=c, j=j: h.transpose(
                            pT[:, j * 128:(j + 1) * 128],
                            xld[:, c * 128:(c + 1) * 128], idf[:]),
                            inc=sTE if j == 3 else None)
                    VE.wait(sTE)
                    VE.do(lambda v, b=b, t=t: v.tensor_copy(
                        out=xThi3[:, b * 4:(b + 1) * 4,
                                  t * 128:(t + 1) * 128],
                        in_=pT4[:, :, :]), inc=sVE)
                    VE.wait(sVE)
                    VE.do(lambda v, b=b, t=t: v.tensor_tensor(
                        out=xTlo3[:, b * 4:(b + 1) * 4,
                                  t * 128:(t + 1) * 128],
                        in0=pT4[:, :, :],
                        in1=xThi3[:, b * 4:(b + 1) * 4,
                                  t * 128:(t + 1) * 128],
                        op=ALU.subtract), inc=sVE)
                    TE.wait(sVE)
            xT_ready = sVE.n

            SY.wait(sTE)
            SY.do(lambda s: s.dma_start(out=xld[0:16, 0:128], in_=bqr_d[:, :]),
                  inc=sSY, k=16)
            TE.wait(sSY)
            TE.do(lambda h: h.transpose(pT[:, 0:16], xld[0:16, 0:128],
                                        idf[0:16, 0:16]), inc=sTE)
            VE.wait(sTE)
            VE.do(lambda v: v.tensor_copy(out=bqc[:], in_=pT[:, 0:16]),
                  inc=sVE)
            SY.wait(sTE)
            SY.do(lambda s: s.dma_start(out=xld[0:16, 0:128], in_=bor_d[:, :]),
                  inc=sSY, k=16)
            TE.wait(sSY)
            TE.wait(sVE)
            TE.do(lambda h: h.transpose(pT[:, 128:144], xld[0:16, 0:128],
                                        idf[0:16, 0:16]), inc=sTE)
            VE.wait(sTE)
            VE.do(lambda v: v.tensor_copy(out=boc[:], in_=pT[:, 128:144]),
                  inc=sVE)
            SY.wait(sVE)

            if PHASE <= 0:
                return nc
            XSld.close()
            FW = 512
            with ExitStack() as S1:
                e1 = S1.enter_context
                wkhi = e1(nc.sbuf_tensor("wkhi", [128, KC * FW], BF16))
                wklo = e1(nc.sbuf_tensor("wklo", [128, KC * FW], BF16))
                wvhi = e1(nc.sbuf_tensor("wvhi", [128, KC * FW], BF16))
                wvlo = e1(nc.sbuf_tensor("wvlo", [128, KC * FW], BF16))
                bkrep = e1(nc.sbuf_tensor("bkrep", [128, D], F32))
                bvrep = e1(nc.sbuf_tensor("bvrep", [128, D], F32))
                ktile = e1(nc.sbuf_tensor("ktile", [128, 2 * FW], F32))
                vtile = e1(nc.sbuf_tensor("vtile", [128, 2 * FW], F32))
                etile = e1(nc.sbuf_tensor("etile", [128, 2 * FW], F32))
                etile2 = e1(nc.sbuf_tensor("etile2", [128, 2 * FW], F32))
                wsld = e1(nc.sbuf_tensor("wsld", [128, 2048], F32))

                SY.do(lambda s: s.dma_start(
                    out=bkrep[:], in_=bk_d[0:1, :].broadcast_to([128, D])),
                    inc=sSY, k=16)
                SY.do(lambda s: s.dma_start(
                    out=bvrep[:], in_=bv_d[0:1, :].broadcast_to([128, D])),
                    inc=sSY, k=16)
                bias_ready = sSY.n

                wsum_jobs = []
                for i in range(8):
                    wsum_jobs.append((0, w1s_d, i, 0))
                for i in range(8):
                    wsum_jobs.append((1, w2s_d, i, 0))
                for i in range(2):
                    for hf in range(4):
                        wsum_jobs.append((2, w3s_d, i, hf))

                def emit_wsum_job(job):
                    m, dram, i, hf = job
                    SC.wait(sVE)
                    SC.do(lambda s, dram=dram, i=i, hf=hf: s.dma_start(
                        out=wsld[:],
                        in_=dram[i * 128:(i + 1) * 128,
                                 hf * 2048:(hf + 1) * 2048]), inc=sL2, k=16)
                    VE.wait(sL2)
                    VE.do(lambda v: v.tensor_reduce(
                        out=statA[:, 0:1], in_=wsld[:], axis=AX,
                        op=ALU.add, apply_absolute_value=True), inc=sVE)
                    VE.wait(sVE)
                    VE.do(lambda v, m=m: v.tensor_tensor(
                        out=wsumc[:, m:m + 1], in0=wsumc[:, m:m + 1],
                        in1=statA[:, 0:1], op=ALU.add), inc=sVE)

                evacK = {0: None, 1: None}
                evacV = {0: None, 1: None}
                kvmm_half = {0: None, 1: None}
                headprev = {}
                panel_mark = {"khi": None, "klo": None, "vhi": None,
                              "vlo": None}
                ld_mark = {}
                agA3 = agA_out.ap().rearrange("(c p) f -> p c f", p=128)
                for q in range(4):
                    if q == 0:
                        SY.wait(sCC, agA_done)
                    for nm, buf, colb in (("khi", wkhi, 0), ("vhi", wvhi, 4096),
                                          ("klo", wklo, 2048),
                                          ("vlo", wvlo, 6144)):
                        if panel_mark[nm] is not None:
                            SY.wait(sTE, panel_mark[nm])
                        SY.do(lambda s, buf=buf, colb=colb, q=q: s.dma_start(
                            out=buf.ap().rearrange("p (c f) -> p c f", c=KC)
                            [:, :, :],
                            in_=agA3[:, :, colb + q * FW:
                                     colb + (q + 1) * FW]), inc=sSY, k=16)
                        ld_mark[nm] = sSY.n
                    if q == 0:
                        GP.wait(sSY)
                        GP.do(lambda g: g.collective_compute(
                            "AllGather", ALU.bypass,
                            replica_groups=[list(range(8))],
                            ins=[agB_in.ap().opt()],
                            outs=[agB_out.ap().opt()]), inc=sCC)
                        TE.wait(sSY, bias_ready)
                        TE.wait(sVE, xT_ready)

                    def emit_proj(t):
                        par = t % 2
                        bK = pK[:] if par == 0 else pQ[:, 0:FW]
                        bV = pV[:] if par == 0 else pQ[:, 512:512 + FW]
                        ksl = slice(par * FW, (par + 1) * FW)
                        if evacK[par] is not None:
                            TE.wait(sVE, evacK[par])
                        passes = [(xThi, wkhi, "khi"), (xTlo, wkhi, "khi"),
                                  (xThi, wklo, "klo")]
                        for pi, (xa, wb, nm) in enumerate(passes):
                            TE.wait(sSY, ld_mark["vlo"])
                            for k in range(KC):
                                TE.do(lambda h, k=k, t=t, bK=bK, xa=xa, wb=wb:
                                      h.matmul(
                                          bK,
                                          xa[:, k * NT + t * 128:
                                             k * NT + (t + 1) * 128],
                                          wb[:, k * FW:(k + 1) * FW],
                                          start=(pi == 0 and k == 0),
                                          stop=(pi == 2 and k == KC - 1)),
                                      inc=sTE if (pi == 2 and k == KC - 1)
                                      else None)
                        kmm = sTE.n
                        if evacV[par] is not None:
                            TE.wait(sVE, evacV[par])
                        vpasses = [(xThi, wvhi, "vhi"), (xTlo, wvhi, "vhi"),
                                   (xThi, wvlo, "vlo")]
                        for pi, (xa, wb, nm) in enumerate(vpasses):
                            TE.wait(sSY, ld_mark["vlo"])
                            for k in range(KC):
                                TE.do(lambda h, k=k, t=t, bV=bV, xa=xa, wb=wb:
                                      h.matmul(
                                          bV,
                                          xa[:, k * NT + t * 128:
                                             k * NT + (t + 1) * 128],
                                          wb[:, k * FW:(k + 1) * FW],
                                          start=(pi == 0 and k == 0),
                                          stop=(pi == 2 and k == KC - 1)),
                                      inc=sTE if (pi == 2 and k == KC - 1)
                                      else None)
                        vmm = sTE.n
                        VE.wait(sTE, kmm)
                        if kvmm_half[par] is not None:
                            VE.wait(sTE, kvmm_half[par])
                        VE.do(lambda v, q=q, bK=bK, ksl=ksl: v.tensor_tensor(
                            out=ktile[:, ksl], in0=bK,
                            in1=bkrep[:, q * FW:(q + 1) * FW], op=ALU.add))
                        VE.do(lambda v, ksl=ksl: v.tensor_scalar(
                            out=etile[:, ksl], in0=ktile[:, ksl], scalar1=0.0,
                            scalar2=None, op0=ALU.min), inc=sVE)
                        evacK[par] = sVE.n
                        SC.wait(sVE, evacK[par])
                        SC.do(lambda s, ksl=ksl: s.activation(
                            out=etile2[:, ksl], in_=etile[:, ksl],
                            func=AF.Exp), inc=sAC)
                        VE.wait(sAC)
                        VE.do(lambda v, ksl=ksl: v.scalar_tensor_tensor(
                            out=ktile[:, ksl], in0=ktile[:, ksl], scalar=0.0,
                            in1=etile2[:, ksl], op0=ALU.max, op1=ALU.add))
                        VE.wait(sTE, vmm)
                        VE.do(lambda v, q=q, bV=bV, ksl=ksl: v.tensor_tensor(
                            out=vtile[:, ksl], in0=bV,
                            in1=bvrep[:, q * FW:(q + 1) * FW], op=ALU.add),
                            inc=sVE)
                        evacV[par] = sVE.n

                    def emit_kv(t):
                        par = t % 2
                        for h2 in range(4):
                            gh = q * 4 + h2
                            bank = pKV if h2 % 2 == 0 else pT
                            off = par * FW + h2 * 128
                            if (h2 % 2) in headprev:
                                TE.wait(sVE, headprev[h2 % 2])
                            TE.wait(sVE, evacV[par])
                            TE.do(lambda h, off=off, bank=bank: h.matmul(
                                bank[:, 0:128],
                                ktile[:, off:off + 128],
                                vtile[:, off:off + 128],
                                start=True, stop=False))
                            TE.do(lambda h, off=off, bank=bank: h.matmul(
                                bank[:, 128:129],
                                ktile[:, off:off + 128],
                                onec[:], start=False, stop=True), inc=sTE)
                            VE.wait(sTE)
                            VE.do(lambda v, gh=gh, bank=bank: v.tensor_tensor(
                                out=kvacc[:, gh * 129:(gh + 1) * 129],
                                in0=kvacc[:, gh * 129:(gh + 1) * 129],
                                in1=bank[:, 0:129], op=ALU.add), inc=sVE)
                            headprev[h2 % 2] = sVE.n
                        kvmm_half[par] = sTE.n

                    for t in range(TT + 1):
                        if t < TT:
                            emit_proj(t)
                        if t >= 1:
                            emit_kv(t - 1)
                        if wsum_jobs:
                            emit_wsum_job(wsum_jobs.pop(0))
                    for nm in panel_mark:
                        panel_mark[nm] = sTE.n
                while wsum_jobs:
                    emit_wsum_job(wsum_jobs.pop(0))

                TE.wait(sVE)
                for m in range(3):
                    TE.do(lambda h, m=m: h.matmul(
                        pT[:, 256 + m:257 + m], onesf[:], wsumc[:, m:m + 1],
                        start=True, stop=True), inc=sTE if m == 2 else None)
                VE.wait(sTE)
                VE.do(lambda v: v.tensor_copy(out=kvacc[:, 2064:2067],
                                              in_=pT[:, 256:259]), inc=sVE)

                SY.wait(sVE)
                SY.do(lambda s: s.dma_start(out=cc_in.ap()[:, :],
                                            in_=kvacc[:]), inc=sSY, k=16)
                GP.wait(sSY)
                GP.do(lambda g: g.collective_compute(
                    "AllReduce", ALU.add, replica_groups=[list(range(8))],
                    ins=[cc_in.ap().opt()], outs=[cc_out.ap().opt()]),
                    inc=sCC)
                if PHASE <= 1:
                    return nc

            tern_jobs = []
            for m, (src, dst) in enumerate(((w1c_d, ag1_in.ap()),
                                            (w2c_d, ag2_in.ap()))):
                for kc in range(2):
                    for qq in range(4):
                        for hf in range(2):
                            tern_jobs.append((m, src, dst, kc, qq * 2 + hf))
            for hc in range(8):
                for hf in range(2):
                    tern_jobs.append((2, w3c_d, ag3_in.ap(), hc, hf))
            tstate = {"ld": None, "st": None, "first": True,
                      "stores": {0: 0, 1: 0, 2: 0}}
            ag_emitted = [False]

            def emit_tern_job(job):
                m, src, dst, a_, qh = job
                src3 = src.rearrange("(cc p) k -> p cc k", p=128)
                if tstate["ld"] is not None:
                    SC.wait(sTE, tstate["ld"])
                if m < 2:
                    kc = a_
                    SC.do(lambda s, src3=src3, kc=kc, qh=qh: s.dma_start(
                        out=tld3[:, :, :],
                        in_=src3[:, qh * 8:(qh + 1) * 8,
                                 kc * 128:(kc + 1) * 128]), inc=sL2, k=16)
                else:
                    hc = a_
                    SC.do(lambda s, src3=src3, hc=hc, qh=qh: s.dma_start(
                        out=tld3[:, :, :],
                        in_=src3[:, qh * 8:(qh + 1) * 8,
                                 hc * 128:(hc + 1) * 128]), inc=sL2, k=16)
                VE.wait(sL2)
                if tstate["st"] is not None:
                    VE.wait(sGD, tstate["st"])
                if tstate["first"]:
                    VE.wait(sVE, tern_ready)
                    tstate["first"] = False
                VE.do(lambda v, m=m: v.tensor_scalar(
                    out=tpos[:], in0=tld[:], scalar1=tern[:, 3 + m:4 + m],
                    scalar2=None, op0=ALU.is_lt))
                VE.do(lambda v, m=m: v.scalar_tensor_tensor(
                    out=tbf[:], in0=tld[:], scalar=tern[:, m:m + 1],
                    in1=tpos[:], op0=ALU.is_gt, op1=ALU.subtract), inc=sVE)
                TE.wait(sVE)
                for j in range(8):
                    TE.do(lambda h, j=j: h.transpose(
                        pBt[:, j * 128:(j + 1) * 128],
                        tbf[:, j * 128:(j + 1) * 128], idb[:]),
                        inc=sTE if j == 7 else None)
                VE.wait(sTE)
                VE.do(lambda v: v.tensor_copy(
                    out=tstg3[:, :, :], in_=pBt8[:, :, :]), inc=sVE)
                TE.wait(sVE)
                tstate["ld"] = sTE.n
                GP.wait(sVE)
                dst3 = dst.rearrange("(c p) o -> p c o", p=128)
                if m < 2:
                    kc = a_
                    GP.do(lambda g, dst3=dst3, kc=kc, qh=qh: g.dma_start(
                        out=dst3[:, kc:kc + 1, qh * 1024:(qh + 1) * 1024],
                        in_=tstg1[:, :, :]), inc=sGD, k=16)
                else:
                    hc = a_
                    GP.do(lambda g, dst3=dst3, hc=hc, qh=qh: g.dma_start(
                        out=dst3[:, hc:hc + 1, qh * 1024:(qh + 1) * 1024],
                        in_=tstg1[:, :, :]), inc=sGD, k=16)
                tstate["st"] = sGD.n

            full_barrier()
            SC.wait(sCC)
            SC.do(lambda s: s.dma_start(out=kvacc[:],
                                        in_=cc_out.ap()[:, :]),
                  inc=sL2, k=16)
            kv_reduced = sL2.n
            SC.wait(sL2, kv_reduced)

            VE.wait(sL2, kv_reduced)
            NEL = float(HD * D)
            for m in range(3):
                c0 = 12 + m * 4
                VE.do(lambda v, m=m, c0=c0: v.tensor_scalar(
                    out=statA[:, c0:c0 + 1],
                    in0=kvacc[:, 2064 + m:2065 + m],
                    scalar1=1.0 / NEL, scalar2=None, op0=ALU.mult))
                VE.do(lambda v, m=m, c0=c0: v.tensor_scalar(
                    out=tern[:, m:m + 1], in0=statA[:, c0:c0 + 1],
                    scalar1=EPS_RMS, scalar2=ATANH_HALF,
                    op0=ALU.add, op1=ALU.mult))
                VE.do(lambda v, m=m: v.tensor_scalar(
                    out=tern[:, 3 + m:4 + m], in0=tern[:, m:m + 1],
                    scalar1=-1.0, scalar2=None, op0=ALU.mult))
                VE.do(lambda v, c0=c0: v.tensor_scalar(
                    out=statA[:, c0 + 1:c0 + 2], in0=statA[:, c0:c0 + 1],
                    scalar1=1.0, scalar2=None, op0=ALU.add))
                VE.do(lambda v, c0=c0: v.tensor_scalar(
                    out=statA[:, c0 + 2:c0 + 3], in0=statA[:, c0:c0 + 1],
                    scalar1=-1.0, scalar2=1.0, op0=ALU.mult, op1=ALU.add))
                VE.do(lambda v, c0=c0: v.reciprocal(
                    out=statA[:, c0 + 3:c0 + 4],
                    in_=statA[:, c0 + 2:c0 + 3]), inc=sVE)
                VE.wait(sVE)
                VE.do(lambda v, c0=c0: v.tensor_tensor(
                    out=statA[:, c0 + 1:c0 + 2],
                    in0=statA[:, c0 + 1:c0 + 2],
                    in1=statA[:, c0 + 3:c0 + 4], op=ALU.mult), inc=sVE)
                SC.wait(sVE)
                SC.do(lambda s, c0=c0: s.activation(
                    out=statA[:, c0 + 2:c0 + 3],
                    in_=statA[:, c0 + 1:c0 + 2], func=AF.Ln), inc=sAC)
                VE.wait(sAC)
                VE.do(lambda v, m=m, c0=c0: v.tensor_scalar(
                    out=tern[:, 6 + m:7 + m], in0=statA[:, c0 + 2:c0 + 3],
                    scalar1=0.5, scalar2=None, op0=ALU.mult), inc=sVE)
            tern_ready = sVE.n
            with ExitStack() as S2:
                e2 = S2.enter_context
                attnThi = e2(nc.sbuf_tensor("attnThi", [128, NH * NT], BF16))
                attnTlo = e2(nc.sbuf_tensor("attnTlo", [128, NH * NT], BF16))
                TERN = ExitStack()
                et = TERN.enter_context
                tld = et(nc.sbuf_tensor("tld", [128, 1024], F32))
                tpos = et(nc.sbuf_tensor("tpos", [128, 1024], F32))
                tbf = et(nc.sbuf_tensor("tbf", [128, 1024], BF16))
                tstg = et(nc.sbuf_tensor("tstg", [128, 1024], BF16))
                tld3 = tld.ap().rearrange("p (cc j) -> p cc j", cc=8)
                tstg3 = tstg.ap().rearrange("p (cc j) -> p cc j", cc=8)
                tstg1 = tstg.ap().rearrange("p (c o) -> p c o", c=1)
                with ExitStack() as S2a:
                    e2a = S2a.enter_context
                    qwh = e2a(nc.sbuf_tensor("qwh", [128, KC * 512], BF16))
                    qwl = e2a(nc.sbuf_tensor("qwl", [128, KC * 512], BF16))
                    qthbuf = [e2a(nc.sbuf_tensor(f"qth{i}", [128, NT], F32))
                              for i in range(2)]
                    qm = lnsc[:, 0:NT]
                    qe = lnsc[:, NT:2 * NT]
                    at8 = e2a(nc.sbuf_tensor("at8", [128, 1024], F32))
                    recc = e2a(nc.sbuf_tensor("recc", [128, 16], F32))

                    prevQ = {"v": None}
                    agB3 = agB_out.ap().rearrange("(c p) f -> p c f", p=128)
                    qw_ld = None
                    qw_mm = None
                    qb_evac = {"v": None}

                    def emit_qmm(h, qth):
                        nonlocal qw_ld, qw_mm
                        if h % 4 == 0:
                            if h == 0:
                                SY.wait(sCC, agB_done)
                            if qw_mm is not None:
                                SY.wait(sTE, qw_mm)
                            hg = h // 4
                            for buf, colb in ((qwh, 0), (qwl, 2048)):
                                SY.do(lambda s, buf=buf, colb=colb, hg=hg:
                                      s.dma_start(
                                          out=buf.ap().rearrange(
                                              "p (c j) -> p c j", c=KC)
                                          [:, :, :],
                                          in_=agB3[:, :,
                                                   colb + hg * 512:
                                                   colb + (hg + 1) * 512]),
                                      inc=sSY, k=16)
                            qw_ld = sSY.n
                        TE.wait(sSY, qw_ld)
                        if qb_evac["v"] is not None:
                            TE.wait(sVE, qb_evac["v"])
                        hof = (h % 4) * 128
                        passes = [(xThi, qwh), (xThi, qwl), (xTlo, qwh)]
                        for half in range(2):
                            for pi, (xa, wb) in enumerate(passes):
                                for k in range(KC):
                                    TE.do(lambda h_, k=k, half=half, xa=xa,
                                          wb=wb, pi=pi, hof=hof: h_.matmul(
                                              pQ[:, half * 512:
                                                 (half + 1) * 512],
                                              wb[:, k * 512 + hof:
                                                 k * 512 + hof + 128],
                                              xa[:, k * NT + half * 512:
                                                 k * NT + (half + 1) * 512],
                                              start=(pi == 0 and k == 0),
                                              stop=(pi == 2 and k == KC - 1)),
                                          inc=sTE if (pi == 2 and
                                                      k == KC - 1) else None)
                        qw_mm = sTE.n
                        VE.wait(sTE, qw_mm)
                        VE.do(lambda v, h=h: v.tensor_scalar(
                            out=qth[:], in0=pQ[:], scalar1=bqc[:, h:h + 1],
                            scalar2=None, op0=ALU.add), inc=sVE)
                        qb_evac["v"] = sVE.n
                        VE.do(lambda v: v.tensor_scalar(
                            out=qm, in0=qth[:], scalar1=0.0, scalar2=None,
                            op0=ALU.min), inc=sVE)
                        mv = sVE.n
                        SC.wait(sVE, mv)
                        SC.do(lambda s: s.activation(out=qe, in_=qm,
                                                     func=AF.Exp), inc=sAC)
                        VE.wait(sAC)
                        VE.do(lambda v: v.scalar_tensor_tensor(
                            out=qth[:], in0=qth[:], scalar=0.0, in1=qe,
                            op0=ALU.max, op1=ALU.add), inc=sVE)
                        return sVE.n

                    def emit_qcons(h, q_ready, qth2):
                        TE.wait(sVE, q_ready)
                        TE.wait(sVE)
                        if h == 0:
                            TE.wait(sL2, kv_reduced)
                        for t in range(TT):
                            TE.do(lambda h_, h=h, t=t: h_.matmul(
                                pKV[:, t:t + 1],
                                qth2[:, t * 128:(t + 1) * 128],
                                kvacc[:, h * 129 + 128:h * 129 + 129],
                                start=True, stop=True),
                                inc=sTE if t == TT - 1 else None)
                        VE.wait(sTE)
                        VE.do(lambda v: v.tensor_scalar(
                            out=recc[:, 0:8], in0=pKV[:, 0:8],
                            scalar1=EPS_AT, scalar2=None, op0=ALU.add),
                            inc=sVE)
                        VE.wait(sVE)
                        VE.do(lambda v: v.reciprocal(out=recc[:, 8:16],
                                                     in_=recc[:, 0:8]),
                              inc=sVE)
                        rec_ready = sVE.n
                        for t in range(TT):
                            bank = pK if t < 4 else pV
                            tof = (t % 4) * 128
                            TE.do(lambda h_, h=h, t=t, bank=bank, tof=tof:
                                  h_.matmul(
                                      bank[:, tof:tof + 128],
                                      qth2[:, t * 128:(t + 1) * 128],
                                      kvacc[:, h * 129:h * 129 + 128],
                                      start=True, stop=True),
                                  inc=sTE if t == TT - 1 else None)
                        VE.wait(sTE)
                        VE.wait(sVE, rec_ready)
                        for t in range(TT):
                            bank = pK if t < 4 else pV
                            tof = (t % 4) * 128
                            VE.do(lambda v, t=t, bank=bank, tof=tof:
                                  v.tensor_scalar(
                                      out=at8[:, t * 128:(t + 1) * 128],
                                      in0=bank[:, tof:tof + 128],
                                      scalar1=recc[:, 8 + t:9 + t],
                                      scalar2=None, op0=ALU.mult),
                                  inc=sVE if t == TT - 1 else None)
                        TE.wait(sVE)
                        for b in range(2):
                            for j in range(4):
                                t = b * 4 + j
                                TE.do(lambda h_, t=t, j=j: h_.transpose(
                                    pT2[:, j * 128:(j + 1) * 128],
                                    at8[:, t * 128:(t + 1) * 128], idf[:]),
                                    inc=sTE if j == 3 else None)
                            VE.wait(sTE)
                            VE.do(lambda v, h=h, b=b: v.tensor_copy(
                                out=attnThi[:, h * NT + b * 512:
                                            h * NT + (b + 1) * 512],
                                in_=pT2[:, 0:512]), inc=sVE)
                            VE.wait(sVE)
                            VE.do(lambda v, h=h, b=b: v.tensor_tensor(
                                out=attnTlo[:, h * NT + b * 512:
                                            h * NT + (b + 1) * 512],
                                in0=pT2[:, 0:512],
                                in1=attnThi[:, h * NT + b * 512:
                                            h * NT + (b + 1) * 512],
                                op=ALU.subtract), inc=sVE)
                            TE.wait(sVE)

                    qr = {}
                    for h in range(NH + 1):
                        if h < NH:
                            qr[h] = emit_qmm(h, qthbuf[h % 2])
                        if h >= 1:
                            emit_qcons(h - 1, qr[h - 1], qthbuf[(h - 1) % 2])
                        for _ in range(4):
                            if tern_jobs:
                                emit_tern_job(tern_jobs.pop(0))
                        if not tern_jobs and not ag_emitted[0]:
                            ag_emitted[0] = True
                            GP.wait(sGD)
                            for agi, ago, nm in ((ag1_in, w1t_d, "g1"),
                                                 (ag2_in, w2t_d, "g2"),
                                                 (ag3_in, w3t_d, "g3")):
                                GP.do(lambda g, agi=agi, ago=ago:
                                      g.collective_compute(
                                          "AllGather", ALU.bypass,
                                          replica_groups=[list(range(8))],
                                          ins=[agi.ap().opt()],
                                          outs=[ago.opt()]), inc=sCC)
                    attnT_ready = sVE.n
                assert not tern_jobs, "tern jobs must drain inside P3a"
                TERN.close()

                full_barrier()
                with ExitStack() as S3:
                    e3 = S3.enter_context
                    borep = e3(nc.sbuf_tensor("borep", [128, D], F32))
                    SY.do(lambda s: s.dma_start(
                        out=borep[:],
                        in_=bo_row[0:1, :].broadcast_to([128, D])),
                        inc=sSY, k=16)
                    bo_ready = sSY.n
                    agB3b = agB_out.ap().rearrange("(c p) f -> p c f",
                                                   p=128)
                    x3r = x_d.rearrange("(tt p) d -> p tt d", p=128)
                    with ExitStack() as N1:
                        en = N1.enter_context
                        woh = en(nc.sbuf_tensor("woh", [128, KC * 512],
                                                BF16))
                        wol = en(nc.sbuf_tensor("wol", [128, KC * 512],
                                                BF16))
                        ybuf = en(nc.sbuf_tensor("ybuf", [128, 1024], F32))
                        ysq = en(nc.sbuf_tensor("ysq", [128, 512], F32))
                        xrel = en(nc.sbuf_tensor("xrel", [128, 1024], F32))
                        xrel3 = xrel.ap().rearrange("p (o j) -> p o j", o=2)
                        owg = {"mm": None, "ld": None}
                        bank_mark = {0: None, 1: None}
                        ybuf_st = {0: None, 1: None}
                        xrel_free = {0: None, 1: None}
                        xrel_ld = {}
                        ylast = {}

                        def emit_xrel(fc, t):
                            par = t % 2
                            if xrel_free[par] is not None:
                                SY.wait(sVE, xrel_free[par])
                            SY.do(lambda s, fc=fc, t=t, par=par:
                                  s.dma_start(
                                      out=xrel3[:, par:par + 1, :],
                                      in_=x3r[:, t:t + 1,
                                              fc * 512:(fc + 1) * 512]),
                                  inc=sSY, k=16)
                            xrel_ld[(fc, t)] = sSY.n

                        def emit_oload(fc):
                            if owg["mm"] is not None:
                                SY.wait(sTE, owg["mm"])
                            for buf, colb in ((woh, 4096), (wol, 6144)):
                                SY.do(lambda s, buf=buf, colb=colb, fc=fc:
                                      s.dma_start(
                                          out=buf.ap().rearrange(
                                              "p (c j) -> p c j", c=KC)
                                          [:, :, :],
                                          in_=agB3b[:, :,
                                                    colb + fc * 512:
                                                    colb + (fc + 1) * 512]),
                                      inc=sSY, k=16)
                            emit_xrel(fc, 0)
                            owg["ld"] = sSY.n

                        def emit_omm(fc, t):
                            par = t % 2
                            pb = pQ[:, par * 512:(par + 1) * 512]
                            TE.wait(sSY, owg["ld"])
                            if fc == 0 and t == 0:
                                TE.wait(sVE, attnT_ready)
                            if bank_mark[par] is not None:
                                TE.wait(sVE, bank_mark[par])
                            opasses = [(attnThi, woh), (attnThi, wol),
                                       (attnTlo, woh)]
                            for pi, (ab, wb) in enumerate(opasses):
                                for ec in range(NH):
                                    TE.do(lambda h_, ec=ec, t=t, ab=ab,
                                          wb=wb, pb=pb, pi=pi: h_.matmul(
                                              pb,
                                              ab[:, ec * NT + t * 128:
                                                 ec * NT + (t + 1) * 128],
                                              wb[:, ec * 512:
                                                 (ec + 1) * 512],
                                              start=(pi == 0 and ec == 0),
                                              stop=(pi == 2 and
                                                    ec == NH - 1)),
                                          inc=sTE if (pi == 2 and
                                                      ec == NH - 1)
                                          else None)
                            owg["mm"] = sTE.n
                            return sTE.n

                        def emit_oevac(fc, t, omm):
                            par = t % 2
                            pb = pQ[:, par * 512:(par + 1) * 512]
                            yb = ybuf[:, par * 512:(par + 1) * 512]
                            VE.wait(sTE, omm)
                            if fc == 0 and t == 0:
                                VE.wait(sSY, bo_ready)
                            VE.wait(sSY)
                            VE.do(lambda v, pb=pb, yb=yb, fc=fc:
                                  v.tensor_tensor(
                                      out=yb, in0=pb,
                                      in1=borep[:, fc * 512:
                                                (fc + 1) * 512],
                                      op=ALU.add))
                            VE.do(lambda v, yb=yb, par=par:
                                  v.tensor_tensor(
                                      out=yb, in0=yb,
                                      in1=xrel[:, par * 512:
                                               (par + 1) * 512],
                                      op=ALU.add), inc=sVE)
                            bank_mark[par] = sVE.n
                            xrel_free[par] = sVE.n
                            y_ready = sVE.n
                            VE.do(lambda v, yb=yb, fc=fc, t=t:
                                  v.tensor_reduce(
                                      out=scols[:, t * 4 + fc:
                                                t * 4 + fc + 1],
                                      in_=yb, axis=AX, op=ALU.add))
                            VE.do(lambda v, yb=yb: v.tensor_tensor(
                                out=ysq[:], in0=yb, in1=yb, op=ALU.mult))
                            VE.do(lambda v, fc=fc, t=t: v.tensor_reduce(
                                out=scols[:, 32 + t * 4 + fc:
                                          33 + t * 4 + fc],
                                in_=ysq[:], axis=AX, op=ALU.add),
                                inc=sVE)
                            SY.wait(sVE, y_ready)
                            SY.do(lambda s, fc=fc, t=t, yb=yb:
                                  s.dma_start(
                                      out=y_sp[t * 128:(t + 1) * 128,
                                               fc * 512:(fc + 1) * 512],
                                      in_=yb), inc=sSY, k=16)
                            ybuf_st[par] = sSY.n
                            if fc == 3:
                                ylast[t] = sSY.n

                        for fc in range(4):
                            emit_oload(fc)
                            for t in range(TT):
                                om = emit_omm(fc, t)
                                emit_oevac(fc, t, om)
                                if t + 1 < TT:
                                    emit_xrel(fc, t + 1)
                        stats_done = sVE.n
            while tern_jobs:
                emit_tern_job(tern_jobs.pop(0))
            if PHASE <= 2:
                return nc
            w12_stores = max(tstate["stores"][0], tstate["stores"][1])
            w3_stores = tstate["stores"][2]

        full_barrier()
        with ExitStack() as SF:
            ef = SF.enter_context
            pU = [ef(nc.psum_tensor(f"pU{i}", [128, 512], F32))
                  for i in range(6)]
            pBq = ef(nc.psum_tensor("pBq", [128, 1024], BF16))
            pBq8 = pBq.ap().rearrange("p (j x) -> p j x", j=8)

            g1rep = ef(nc.sbuf_tensor("g1rep", [128, D], F32))
            g2rep = ef(nc.sbuf_tensor("g2rep", [128, D], F32))
            g3sl = ef(nc.sbuf_tensor("g3sl", [128, 1024], F32))
            xq1T = ef(nc.sbuf_tensor("xq1T", [128, KC * NT], BF16))
            xq2T = ef(nc.sbuf_tensor("xq2T", [128, KC * NT], BF16))
            xptF = ef(nc.sbuf_tensor("xptF", [128, D], F32))
            tq = ef(nc.sbuf_tensor("tq", [128, D], F32))
            pqb = ef(nc.sbuf_tensor("pqb", [128, D], BF16))
            nqb = ef(nc.sbuf_tensor("nqb", [128, D], BF16))
            xqc = ef(nc.sbuf_tensor("xqc", [128, D], BF16))
            w1pA = ef(nc.sbuf_tensor("w1pA", [128, KC * 512], BF16))
            w1pB = ef(nc.sbuf_tensor("w1pB", [128, KC * 512], BF16))
            w2pA = ef(nc.sbuf_tensor("w2pA", [128, KC * 512], BF16))
            w2pB = ef(nc.sbuf_tensor("w2pB", [128, KC * 512], BF16))
            silu_sb = ef(nc.sbuf_tensor("silu_sb", [128, 3 * 512], F32))
            h_sb = ef(nc.sbuf_tensor("h_sb", [128, 2 * 512], F32))
            hh_scr = ef(nc.sbuf_tensor("hh_scr", [128, 512], F32))
            yldF = ef(nc.sbuf_tensor("yldF", [128, D], F32))
            hcols = ef(nc.sbuf_tensor("hcols", [128, 256], F32))

            SY.do(lambda s: s.dma_start(
                out=g1rep[:], in_=g1_d[0:1, :].broadcast_to([128, D])),
                inc=sSY, k=16)
            SY.do(lambda s: s.dma_start(
                out=g2rep[:], in_=g2_d[0:1, :].broadcast_to([128, D])),
                inc=sSY, k=16)
            for t in range(TT):
                VE.do(lambda v, t=t: v.tensor_reduce(
                    out=statN[:, t:t + 1], in_=scols[:, t * 4:(t + 1) * 4],
                    axis=AX, op=ALU.add))
                VE.do(lambda v, t=t: v.tensor_reduce(
                    out=statN[:, 8 + t:9 + t],
                    in_=scols[:, 32 + t * 4:32 + (t + 1) * 4],
                    axis=AX, op=ALU.add),
                    inc=sVE if t == TT - 1 else None)
            VE.wait(sVE)
            VE.do(lambda v: v.tensor_scalar(
                out=statN[:, 0:8], in0=statN[:, 0:8], scalar1=1.0 / D,
                scalar2=None, op0=ALU.mult))
            VE.do(lambda v: v.tensor_scalar(
                out=statN[:, 8:16], in0=statN[:, 8:16], scalar1=1.0 / D,
                scalar2=None, op0=ALU.mult))
            VE.do(lambda v: v.tensor_tensor(
                out=statN[:, 16:24], in0=statN[:, 0:8], in1=statN[:, 0:8],
                op=ALU.mult))
            VE.do(lambda v: v.tensor_tensor(
                out=statN[:, 8:16], in0=statN[:, 8:16], in1=statN[:, 16:24],
                op=ALU.subtract))
            VE.do(lambda v: v.tensor_scalar(
                out=statN[:, 8:16], in0=statN[:, 8:16], scalar1=EPS_LN,
                scalar2=None, op0=ALU.add), inc=sVE)
            SC.wait(sVE)
            SC.do(lambda s: s.activation(out=statN[:, 16:24],
                                         in_=statN[:, 8:16], func=AF.Sqrt),
                  inc=sAC)
            VE.wait(sAC)
            VE.do(lambda v: v.reciprocal(out=statN[:, 8:16],
                                         in_=statN[:, 16:24]), inc=sVE)
            VE.wait(sVE)
            xq_tile = {}
            nrm_mark = {"v": None}
            for t in range(TT):
                if nrm_mark["v"] is not None:
                    SY.wait(sVE, nrm_mark["v"])
                SY.do(lambda s, t=t: s.dma_start(
                    out=yldF[:],
                    in_=y_sp[t * 128:(t + 1) * 128, :]), inc=sSY, k=16)
                VE.wait(sSY)
                VE.do(lambda v, t=t: v.tensor_scalar(
                    out=xptF[:], in0=yldF[:],
                    scalar1=statN[:, t:t + 1],
                    scalar2=statN[:, 8 + t:9 + t],
                    op0=ALU.subtract, op1=ALU.mult), inc=sVE)
                nrm_mark["v"] = sVE.n
                SY.wait(sVE, nrm_mark["v"])
                SY.do(lambda s, t=t: s.dma_start(
                    out=xp_sp[t * 128:(t + 1) * 128, :], in_=xptF[:]),
                    inc=sSY, k=16)
                VE.wait(sAC)
                c = t * 8
                VE.do(lambda v: v.tensor_tensor(out=lnsc[:], in0=xptF[:],
                                                in1=xptF[:], op=ALU.mult))
                VE.do(lambda v: v.tensor_reduce(out=statA[:, 24:25],
                                                in_=lnsc[:], axis=AX,
                                                op=ALU.add), inc=sVE)
                VE.wait(sVE)
                VE.do(lambda v: v.tensor_scalar(
                    out=statA[:, 25:26], in0=statA[:, 24:25],
                    scalar1=1.0 / D, scalar2=EPS_RMS, op0=ALU.mult,
                    op1=ALU.add), inc=sVE)
                SC.wait(sVE)
                SC.do(lambda s: s.activation(out=statA[:, 26:27],
                                             in_=statA[:, 25:26],
                                             func=AF.Sqrt), inc=sAC)
                VE.wait(sAC)
                VE.do(lambda v, c=c: v.reciprocal(out=statF[:, c:c + 1],
                                                  in_=statA[:, 26:27]), inc=sVE)
                VE.wait(sVE)
                for gi, (grep, xqT, amcol) in enumerate(
                        ((g1rep, xq1T, 1), (g2rep, xq2T, 2))):
                    VE.do(lambda v, grep=grep: v.tensor_tensor(
                        out=tq[:], in0=xptF[:], in1=grep[:], op=ALU.mult))
                    VE.do(lambda v: v.tensor_reduce(
                        out=statA[:, 27:28], in_=tq[:], axis=AX, op=ALU.max,
                        apply_absolute_value=True), inc=sVE)
                    VE.wait(sVE)
                    VE.do(lambda v, c=c: v.tensor_tensor(
                        out=statA[:, 28:29], in0=statA[:, 27:28],
                        in1=statF[:, c:c + 1], op=ALU.mult))
                    VE.do(lambda v: v.tensor_scalar(
                        out=statA[:, 28:29], in0=statA[:, 28:29],
                        scalar1=1e-4, scalar2=None, op0=ALU.max))
                    VE.do(lambda v: v.reciprocal(out=statA[:, 29:30],
                                                 in_=statA[:, 28:29]), inc=sVE)
                    VE.wait(sVE)
                    VE.do(lambda v: v.tensor_scalar(
                        out=statA[:, 30:31], in0=statA[:, 29:30],
                        scalar1=127.0, scalar2=None, op0=ALU.mult))
                    VE.do(lambda v, c=c: v.tensor_tensor(
                        out=statA[:, 31:32], in0=statF[:, c:c + 1],
                        in1=statA[:, 30:31], op=ALU.mult))
                    VE.do(lambda v: v.tensor_scalar(
                        out=statA[:, 27:28], in0=statA[:, 28:29],
                        scalar1=1.0 / 127.0, scalar2=None, op0=ALU.mult))
                    VE.do(lambda v, c=c, gi=gi, amcol=amcol: v.tensor_tensor(
                        out=statF[:, c + amcol:c + amcol + 1],
                        in0=statA[:, 27:28], in1=tern[:, 6 + gi:7 + gi],
                        op=ALU.mult))
                    VE.do(lambda v: v.tensor_scalar(
                        out=tq[:], in0=tq[:], scalar1=statA[:, 31:32],
                        scalar2=None, op0=ALU.mult))
                    VE.do(lambda v: v.tensor_scalar(
                        out=pqb[:], in0=tq[:], scalar1=0.0, scalar2=128.0,
                        op0=ALU.max, op1=ALU.add))
                    VE.do(lambda v: v.tensor_scalar(
                        out=nqb[:], in0=tq[:], scalar1=0.0, scalar2=128.0,
                        op0=ALU.min, op1=ALU.subtract))
                    VE.do(lambda v: v.tensor_tensor(
                        out=xqc[:], in0=pqb[:], in1=nqb[:], op=ALU.add),
                        inc=sVE)
                    TE.wait(sVE)
                    for b in range(2):
                        for j in range(8):
                            cc = b * 8 + j
                            TE.do(lambda h, cc=cc, j=j: h.transpose(
                                pBq[:, j * 128:(j + 1) * 128],
                                xqc[:, cc * 128:(cc + 1) * 128], idb[:]),
                                inc=sTE if j == 7 else None)
                        VE.wait(sTE)
                        xqT3 = xqT.ap().rearrange("p (cx tx) -> p cx tx",
                                                  cx=KC)
                        VE.do(lambda v, b=b, t=t, xqT3=xqT3: v.tensor_copy(
                            out=xqT3[:, b * 8:(b + 1) * 8,
                                     t * 128:(t + 1) * 128],
                            in_=pBq8[:, :, :]), inc=sVE)
                        TE.wait(sVE)
                xq_tile[t] = sVE.n
            xq_ready = sVE.n

            w1t3 = w1t_d.rearrange("(c p) o -> p c o", p=128)
            w2t3 = w2t_d.rearrange("(c p) o -> p c o", p=128)
            w1pv = [w1pA.ap().rearrange("p (c o) -> p c o", c=KC),
                    w1pB.ap().rearrange("p (c o) -> p c o", c=KC)]
            w2pv = [w2pA.ap().rearrange("p (c o) -> p c o", c=KC),
                    w2pB.ap().rearrange("p (c o) -> p c o", c=KC)]
            w1pf = [w1pA, w1pB]
            w2pf = [w2pA, w2pB]
            GP.wait(sCC)
            panel_mark = {0: None, 1: None}
            panel_ld = {0: None, 1: None}
            hprev0 = {0: None, 1: None, 2: None}
            hprev1 = {0: None, 1: None, 2: None}
            for oc in range(KC):
                opar = oc % 2
                if panel_mark[opar] is not None:
                    GP.wait(sTE, panel_mark[opar])
                GP.do(lambda g, oc=oc, opar=opar: g.dma_start(
                    out=w1pv[opar][:, :, :],
                    in_=w1t3[:, :, oc * 512:(oc + 1) * 512]), inc=sGD, k=16)
                GP.do(lambda g, oc=oc, opar=opar: g.dma_start(
                    out=w2pv[opar][:, :, :],
                    in_=w2t3[:, :, oc * 512:(oc + 1) * 512]), inc=sGD, k=16)
                panel_ld[opar] = sGD.n
                SY.wait(sVE)
                SY.do(lambda s, oc=oc, opar=opar: s.dma_start(
                    out=g3sl[:, opar * 512:(opar + 1) * 512],
                    in_=g3_d[0:1, oc * 512:(oc + 1) * 512]
                    .broadcast_to([128, 512])), inc=sSY, k=16)
                g3_ld = sSY.n
                TE.wait(sGD, panel_ld[opar])
                for t in range(TT):
                    if oc == 0:
                        TE.wait(sVE, xq_tile[t])
                    tpar = t % 3
                    bU1 = pU[2 * tpar]
                    bU2 = pU[2 * tpar + 1]
                    if hprev0[tpar] is not None:
                        TE.wait(sAC, hprev0[tpar])
                    for k in range(KC):
                        TE.do(lambda h, k=k, t=t, opar=opar, bU1=bU1: h.matmul(
                            bU1[:],
                            xq1T[:, k * NT + t * 128:k * NT + (t + 1) * 128],
                            w1pf[opar][:, k * 512:(k + 1) * 512],
                            start=(k == 0), stop=(k == KC - 1)),
                            inc=sTE if k == KC - 1 else None)
                    u1mm = sTE.n
                    if hprev1[tpar] is not None:
                        TE.wait(sVE, hprev1[tpar])
                    for k in range(KC):
                        TE.do(lambda h, k=k, t=t, opar=opar, bU2=bU2: h.matmul(
                            bU2[:],
                            xq2T[:, k * NT + t * 128:k * NT + (t + 1) * 128],
                            w2pf[opar][:, k * 512:(k + 1) * 512],
                            start=(k == 0), stop=(k == KC - 1)),
                            inc=sTE if k == KC - 1 else None)
                    u2mm = sTE.n
                    SC.wait(sTE, u1mm)
                    if hprev1[tpar] is not None:
                        SC.wait(sVE, hprev1[tpar])
                    c = t * 8
                    SC.do(lambda s, c=c, tpar=tpar, bU1=bU1: s.activation(
                        out=silu_sb[:, tpar * 512:(tpar + 1) * 512],
                        in_=bU1[:], func=AF.Silu,
                        scale=statF[:, c + 1:c + 2]), inc=sAC)
                    hprev0[tpar] = sAC.n
                    par = (oc * TT + t) % 2
                    VE.wait(sAC, hprev0[tpar])
                    VE.wait(sTE, u2mm)
                    VE.wait(sSY)
                    VE.do(lambda v, c=c, par=par, tpar=tpar, bU2=bU2:
                          v.scalar_tensor_tensor(
                              out=h_sb[:, par * 512:(par + 1) * 512],
                              in0=bU2[:], scalar=statF[:, c + 2:c + 3],
                              in1=silu_sb[:, tpar * 512:(tpar + 1) * 512],
                              op0=ALU.mult, op1=ALU.mult), inc=sVE)
                    hprev1[tpar] = sVE.n
                    VE.do(lambda v, par=par: v.tensor_tensor(
                        out=hh_scr[:], in0=h_sb[:, par * 512:(par + 1) * 512],
                        in1=h_sb[:, par * 512:(par + 1) * 512], op=ALU.mult))
                    VE.do(lambda v, t=t, oc=oc: v.tensor_reduce(
                        out=hcols[:, t * 16 + oc:t * 16 + oc + 1],
                        in_=hh_scr[:], axis=AX, op=ALU.add))
                    VE.do(lambda v, par=par, opar=opar: v.tensor_tensor(
                        out=hh_scr[:], in0=h_sb[:, par * 512:(par + 1) * 512],
                        in1=g3sl[:, opar * 512:(opar + 1) * 512],
                        op=ALU.mult))
                    VE.do(lambda v, t=t, oc=oc: v.tensor_reduce(
                        out=hcols[:, 128 + t * 16 + oc:129 + t * 16 + oc],
                        in_=hh_scr[:], axis=AX, op=ALU.max,
                        apply_absolute_value=True), inc=sVE)
                    SY.wait(sVE)
                    SY.do(lambda s, t=t, oc=oc, par=par: s.dma_start(
                        out=h_sp[t * 128:(t + 1) * 128,
                                 oc * 512:(oc + 1) * 512],
                        in_=h_sb[:, par * 512:(par + 1) * 512]),
                        inc=sSY, k=16)
                panel_mark[opar] = sTE.n
            VE.wait(sVE)
            for t in range(TT):
                VE.do(lambda v, t=t: v.tensor_reduce(
                    out=hstat[:, 2 * t:2 * t + 1],
                    in_=hcols[:, t * 16:(t + 1) * 16], axis=AX, op=ALU.add))
                VE.do(lambda v, t=t: v.tensor_reduce(
                    out=hstat[:, 2 * t + 1:2 * t + 2],
                    in_=hcols[:, 128 + t * 16:128 + (t + 1) * 16], axis=AX,
                    op=ALU.max), inc=sVE if t == TT - 1 else None)
            hstat_ready = sVE.n
            h_all_stored = sSY.n
            if PHASE <= 3:
                return nc

        full_barrier()
        with ExitStack() as SG:
            eg = SG.enter_context
            pBig = eg(nc.psum_tensor("pBig", [128, 2048], F32))
            pBr = eg(nc.psum_tensor("pBr", [128, 1024], BF16))
            pBr8 = pBr.ap().rearrange("p (j x) -> p j x", j=8)
            g3repF = eg(nc.sbuf_tensor("g3repF", [128, HD], F32))
            hld = eg(nc.sbuf_tensor("hld", [128, 2 * 2048], F32))
            y3 = eg(nc.sbuf_tensor("y3", [128, 2048], F32))
            p3b = eg(nc.sbuf_tensor("p3b", [128, 2048], BF16))
            n3b = eg(nc.sbuf_tensor("n3b", [128, 2048], BF16))
            xq3c = eg(nc.sbuf_tensor("xq3c", [128, 2048], BF16))
            xq3T = eg(nc.sbuf_tensor("xq3T", [128, 4 * HC * 128], BF16))
            w3pA = eg(nc.sbuf_tensor("w3pA", [128, 4 * 1024], BF16))
            w3pB = eg(nc.sbuf_tensor("w3pB", [128, 4 * 1024], BF16))
            xptG = eg(nc.sbuf_tensor("xptG", [128, D], F32))
            yt2 = eg(nc.sbuf_tensor("yt2", [128, 2 * D], F32))
            outsb = eg(nc.sbuf_tensor("outsb", [128, D], F32))
            w3pv = [w3pA.ap().rearrange("p (c d) -> p c d", c=4),
                    w3pB.ap().rearrange("p (c d) -> p c d", c=4)]
            w3pf = [w3pA, w3pB]
            w3t3 = w3t_d.rearrange("(c p) d -> p c d", p=128)

            SY.do(lambda s: s.dma_start(
                out=g3repF[:], in_=g3_d[0:1, :].broadcast_to([128, HD])),
                inc=sSY, k=16)
            g3f_ready = sSY.n
            SY.wait(sSY, h_all_stored)
            VE.wait(sVE, hstat_ready)

            def emit_scales(t):
                c = t * 8
                VE.wait(sAC)
                VE.do(lambda v, t=t: v.tensor_scalar(
                    out=statA[:, 24:25], in0=hstat[:, 2 * t:2 * t + 1],
                    scalar1=1.0 / HD, scalar2=EPS_RMS, op0=ALU.mult,
                    op1=ALU.add), inc=sVE)
                SC.wait(sVE)
                SC.do(lambda s: s.activation(out=statA[:, 25:26],
                                             in_=statA[:, 24:25],
                                             func=AF.Sqrt), inc=sAC)
                VE.wait(sAC)
                VE.do(lambda v: v.reciprocal(out=statA[:, 26:27],
                                             in_=statA[:, 25:26]), inc=sVE)
                VE.wait(sVE)
                VE.do(lambda v, t=t: v.tensor_tensor(
                    out=statA[:, 27:28], in0=hstat[:, 2 * t + 1:2 * t + 2],
                    in1=statA[:, 26:27], op=ALU.mult))
                VE.do(lambda v: v.tensor_scalar(
                    out=statA[:, 27:28], in0=statA[:, 27:28], scalar1=1e-4,
                    scalar2=None, op0=ALU.max))
                VE.do(lambda v: v.reciprocal(out=statA[:, 28:29],
                                             in_=statA[:, 27:28]), inc=sVE)
                VE.wait(sVE)
                VE.do(lambda v: v.tensor_scalar(
                    out=statA[:, 28:29], in0=statA[:, 28:29], scalar1=127.0,
                    scalar2=None, op0=ALU.mult))
                VE.do(lambda v, c=c: v.tensor_tensor(
                    out=statF[:, c + 5:c + 6], in0=statA[:, 26:27],
                    in1=statA[:, 28:29], op=ALU.mult))
                VE.do(lambda v: v.tensor_scalar(
                    out=statA[:, 27:28], in0=statA[:, 27:28],
                    scalar1=1.0 / 127.0, scalar2=None, op0=ALU.mult))
                VE.do(lambda v, c=c: v.tensor_tensor(
                    out=statF[:, c + 6:c + 7], in0=statA[:, 27:28],
                    in1=tern[:, 8:9], op=ALU.mult), inc=sVE)

            mm3_mark = {0: None, 1: None}
            xq3T3v = xq3T.ap().rearrange("p (s c j) -> p s c j", s=4, j=128)

            def emit_quant(t):
                c = t * 8
                slot = t % 4
                for ch in range(4):
                    hpar = ch % 2
                    SY.wait(sVE)
                    SY.do(lambda s, t=t, ch=ch, hpar=hpar: s.dma_start(
                        out=hld[:, hpar * 2048:(hpar + 1) * 2048],
                        in_=h_sp[t * 128:(t + 1) * 128,
                                 ch * 2048:(ch + 1) * 2048]), inc=sSY, k=16)
                    VE.wait(sSY)
                    if t == 0 and ch == 0:
                        VE.wait(sSY, g3f_ready)
                    VE.do(lambda v, ch=ch, hpar=hpar, c=c:
                          v.scalar_tensor_tensor(
                              out=y3[:],
                              in0=hld[:, hpar * 2048:(hpar + 1) * 2048],
                              scalar=statF[:, c + 5:c + 6],
                              in1=g3repF[:, ch * 2048:(ch + 1) * 2048],
                              op0=ALU.mult, op1=ALU.mult))
                    VE.do(lambda v: v.tensor_scalar(
                        out=p3b[:], in0=y3[:], scalar1=0.0, scalar2=128.0,
                        op0=ALU.max, op1=ALU.add))
                    VE.do(lambda v: v.tensor_scalar(
                        out=n3b[:], in0=y3[:], scalar1=0.0, scalar2=128.0,
                        op0=ALU.min, op1=ALU.subtract))
                    VE.do(lambda v: v.tensor_tensor(
                        out=xq3c[:], in0=p3b[:], in1=n3b[:], op=ALU.add),
                        inc=sVE)
                    TE.wait(sVE)
                    for b in range(2):
                        for j in range(8):
                            cc = b * 8 + j
                            TE.do(lambda h, cc=cc, j=j: h.transpose(
                                pBr[:, j * 128:(j + 1) * 128],
                                xq3c[:, cc * 128:(cc + 1) * 128], idb[:]),
                                inc=sTE if j == 7 else None)
                        VE.wait(sTE)
                        if mm3_mark[(t // 2) % 2] is not None:
                            VE.wait(sTE, mm3_mark[(t // 2) % 2])
                        VE.do(lambda v, ch=ch, b=b, slot=slot: v.tensor_copy(
                            out=xq3T3v[:, slot,
                                       ch * 16 + b * 8:ch * 16 + (b + 1) * 8,
                                       :], in_=pBr8[:, :, :]), inc=sVE)
                        TE.wait(sVE)

            w3ld = {0: None, 1: None}
            w3mm = {0: None, 1: None}
            evac_done = None
            emit_scales(0)
            emit_quant(0)
            emit_scales(1)
            emit_quant(1)
            qmark = {0: sVE.n}
            for g in range(4):
                gpar = g % 2
                t0, t1 = 2 * g, 2 * g + 1
                xq_ready3 = qmark[g]
                for dh in range(2):
                    if g + 1 < 4 and dh == 1:
                        emit_scales(2 * g + 3)
                        emit_quant(2 * g + 3)
                    for hb in range(16):
                        wpar = hb % 2
                        if w3mm[wpar] is not None:
                            GP.wait(sTE, w3mm[wpar])
                        GP.do(lambda gg, hb=hb, dh=dh, wpar=wpar:
                              gg.dma_start(
                                  out=w3pv[wpar][:, :, :],
                                  in_=w3t3[:, hb * 4:(hb + 1) * 4,
                                           dh * 1024:(dh + 1) * 1024]),
                              inc=sGD, k=16)
                        w3ld[wpar] = sGD.n
                        TE.wait(sGD, w3ld[wpar])
                        if hb == 0:
                            if dh == 0:
                                TE.wait(sVE, xq_ready3)
                            if evac_done is not None:
                                TE.wait(sVE, evac_done)
                        for hcl in range(4):
                            hc = hb * 4 + hcl
                            for ti, tt_ in enumerate((t0, t1)):
                                for dc in range(2):
                                    TE.do(lambda h, hc=hc, hcl=hcl, dc=dc,
                                          ti=ti, tt_=tt_, wpar=wpar:
                                          h.matmul(
                                              pBig[:, ti * 1024 + dc * 512:
                                                   ti * 1024 +
                                                   (dc + 1) * 512],
                                              xq3T[:, (tt_ % 4) * 8192 +
                                                   hc * 128:
                                                   (tt_ % 4) * 8192 +
                                                   (hc + 1) * 128],
                                              w3pf[wpar][:, hcl * 1024 +
                                                         dc * 512:
                                                         hcl * 1024 +
                                                         (dc + 1) * 512],
                                              start=(hc == 0),
                                              stop=(hc == HC - 1)),
                                          inc=sTE if (hcl == 3 and ti == 1
                                                      and dc == 1) else None)
                        w3mm[wpar] = sTE.n
                    dhmm = sTE.n
                    VE.wait(sTE, dhmm)
                    for ti, tt_ in enumerate((t0, t1)):
                        c = tt_ * 8
                        VE.do(lambda v, c=c, ti=ti, dh=dh: v.tensor_scalar(
                            out=yt2[:, ti * 2048 + dh * 1024:
                                    ti * 2048 + (dh + 1) * 1024],
                            in0=pBig[:, ti * 1024:(ti + 1) * 1024],
                            scalar1=statF[:, c + 6:c + 7], scalar2=None,
                            op0=ALU.mult), inc=sVE)
                    evac_done = sVE.n
                mm3_mark[gpar] = sTE.n
                if g + 1 < 4:
                    emit_scales(2 * g + 2)
                    emit_quant(2 * g + 2)
                    qmark[g + 1] = sVE.n
                for ti, tt_ in enumerate((t0, t1)):
                    SY.wait(sVE)
                    SY.do(lambda s, tt_=tt_: s.dma_start(
                        out=xptG[:], in_=xp_sp[tt_ * 128:(tt_ + 1) * 128, :]),
                        inc=sSY, k=16)
                    VE.wait(sSY)
                    VE.do(lambda v, ti=ti: v.tensor_tensor(
                        out=yt2[:, ti * 2048:(ti + 1) * 2048],
                        in0=yt2[:, ti * 2048:(ti + 1) * 2048],
                        in1=xptG[:], op=ALU.add), inc=sVE)
                    emit_ln(yt2[:, ti * 2048:(ti + 1) * 2048], outsb[:])
                    SY.wait(sVE)
                    SY.do(lambda s, tt_=tt_: s.dma_start(
                        out=out_d[tt_ * 128:(tt_ + 1) * 128, :],
                        in_=outsb[:]), inc=sST, k=16)
                    VE.wait(sST)
            SY.wait(sST, sST.n)

    return nc


_NC_CACHE = {}


def kernel(L0_attn_x, L0_attn_wq, L0_attn_bq, L0_attn_wk, L0_attn_bk,
           L0_attn_wv, L0_attn_bv, L0_attn_wo, L0_attn_bo,
           L0_ffn_w1, L0_ffn_g1, L0_ffn_w2, L0_ffn_g2, L0_ffn_w3, L0_ffn_g3):
    if "nc" not in _NC_CACHE:
        _NC_CACHE["nc"] = build_program()
    nc = _NC_CACHE["nc"]

    def f32(a):
        return np.ascontiguousarray(np.asarray(a, dtype=np.float32))

    x = f32(L0_attn_x).reshape(8192, 2048)
    wq, wk, wv, wo = f32(L0_attn_wq), f32(L0_attn_wk), f32(L0_attn_wv), f32(L0_attn_wo)
    w1, w2, w3 = f32(L0_ffn_w1), f32(L0_ffn_w2), f32(L0_ffn_w3)
    in_maps = []
    for c in range(8):
        in_maps.append({
            "x": np.ascontiguousarray(x[c * NT:(c + 1) * NT]),
            "wqc": np.ascontiguousarray(wq[:, c * 256:(c + 1) * 256]),
            "wkc": np.ascontiguousarray(wk[:, c * 256:(c + 1) * 256]),
            "wvc": np.ascontiguousarray(wv[:, c * 256:(c + 1) * 256]),
            "woc": np.ascontiguousarray(wo[:, c * 256:(c + 1) * 256]),
            "bq_r": f32(L0_attn_bq).reshape(16, 128),
            "bo_r": f32(L0_attn_bo).reshape(16, 128),
            "bo_row": f32(L0_attn_bo).reshape(1, D),
            "bk": f32(L0_attn_bk).reshape(1, D),
            "bv": f32(L0_attn_bv).reshape(1, D),
            "g1": f32(L0_ffn_g1).reshape(1, D),
            "g2": f32(L0_ffn_g2).reshape(1, D),
            "g3": f32(L0_ffn_g3).reshape(1, HD),
            "w1c": np.ascontiguousarray(w1[:, c * 256:(c + 1) * 256]),
            "w2c": np.ascontiguousarray(w2[:, c * 256:(c + 1) * 256]),
            "w3c": np.ascontiguousarray(w3[:, c * 1024:(c + 1) * 1024]),
            "w1s": np.ascontiguousarray(w1[c * (HD // 8):(c + 1) * (HD // 8)]),
            "w2s": np.ascontiguousarray(w2[c * (HD // 8):(c + 1) * (HD // 8)]),
            "w3s": np.ascontiguousarray(w3[c * (D // 8):(c + 1) * (D // 8)]),
        })
    res = run_bass_kernel_spmd(nc, in_maps, core_ids=list(range(8)))
    out = np.concatenate([np.asarray(r["out"]) for r in res.results], axis=0)
    return np.ascontiguousarray(out.reshape(1, 8192, 2048).astype(np.float32))
